# version 09
# baseline (speedup 1.0000x reference)
import os
import numpy as np
from contextlib import ExitStack

import concourse.bass as bass
import concourse.mybir as mb
from concourse.bass_utils import run_bass_kernel_spmd
from concourse.alu_op_type import AluOpType as A

F32 = mb.dt.float32
BF16 = mb.dt.bfloat16
AXX = mb.AxisListType.X
AF = mb.ActivationFunctionType

P = 128
D = 2048
T = 2048
NCORES = 8
BPC = 2
NT = BPC * T // P
TPS = T // P
KC = D // P
NH, NG, DH = 16, 4, 128
WIN, SINK = 512, 4
NE = 16
BS = 16 * T
TOPK = BS // NE
SCALE = DH ** -0.5
EPS = 1e-5
DQKV = D + 1024
NEG = -1.0e30
RING = 5
NBIS = 40

ENGS = ("T", "V", "S", "G", "Y")


class Sched:

    def __init__(self, nc, es):
        self.nc = nc
        self.es = es
        self.ops = {e: [] for e in ENGS}
        self.sems = {}
        self.cnt = {}
        for e in ENGS:
            self.mksem("tk" + e)
        self.bufs = {}
        self.sat = {e: {} for e in ENGS}
        self.pending = {e: [] for e in ENGS}

    def mksem(self, name):
        self.sems[name] = self.es.enter_context(self.nc.semaphore(name))
        self.cnt[name] = 0
        return name

    def emit(self, eng, fn, reads=(), writes=(), sem=None, amt=1, tick=True,
             extra_waits=()):
        need = {}
        for s, c in extra_waits:
            if need.get(s, 0) < c:
                need[s] = c

        def add(mark):
            if mark is not None and need.get(mark[0], 0) < mark[1]:
                need[mark[0]] = mark[1]

        for b in reads:
            st = self.bufs.get(b)
            if st:
                add(st.get("w"))
        for b in writes:
            st = self.bufs.get(b)
            if st:
                add(st.get("w"))
                for m in st.get("r", ()):
                    add(m)
        sat = self.sat[eng]
        waits = [(s, c) for s, c in need.items() if sat.get(s, 0) < c]
        for s, c in waits:
            sat[s] = c
        if sem is None and tick:
            sem = "tk" + eng
        if sem is not None:
            self.cnt[sem] += amt
            mark = (sem, self.cnt[sem])
        else:
            mark = None
        self.ops[eng].append((waits, fn, sem, amt))
        for b in writes:
            self.bufs[b] = {"w": mark, "r": []}
        if mark is None:
            self.pending[eng].extend(reads)
            self.pending[eng].extend(writes)
        else:
            for b in list(reads) + self.pending[eng]:
                st = self.bufs.setdefault(b, {"w": None, "r": []})
                st["r"].append(mark)
                if st.get("w") is None and b in self.pending[eng]:
                    pass
            for b in self.pending[eng]:
                st = self.bufs.setdefault(b, {"w": None, "r": []})
            self.pending[eng] = []
        return mark

    def wait_all(self, eng, marks):
        sat = self.sat[eng]
        ws = []
        for s, c in marks:
            if sat.get(s, 0) < c:
                ws.append((s, c))
                sat[s] = c
        if ws:
            self.ops[eng].append((ws, None, None, 0))

    def realize(self, block):
        def runner(ename):
            def run(eng):
                for waits, fn, sem, amt in self.ops[ename]:
                    for s, c in waits:
                        eng.wait_ge(self.sems[s], c)
                    if fn is not None:
                        inst = fn(eng)
                        if sem is not None:
                            inst.then_inc(self.sems[sem], amt)
            return run

        block.tensor(runner("T"))
        block.vector(runner("V"))
        block.scalar(runner("S"))
        block.gpsimd(runner("G"))
        block.sync(runner("Y"))


def build_program():
    nc = bass.Bass("TRN2", target_bir_lowering=False)
    es = ExitStack()

    def din(name, shape, dt):
        return nc.dram_tensor(name, shape, dt, kind="ExternalInput").ap()

    dx = din("x", [BPC * T, D], F32)
    dWq = din("Wq", [D, D], F32)
    dbq = din("bq", [D], F32)
    dWk = din("Wk", [D, 512], F32)
    dbk = din("bk", [512], F32)
    dWv = din("Wv", [D, 512], F32)
    dbv = din("bv", [512], F32)
    dWo = din("Wo", [D, D], F32)
    dbo = din("bo", [D], F32)
    dWr = din("Wr", [D, NE], F32)
    dbr = din("br", [NE], F32)
    dWe = din("We", [D, D], F32)
    dbe = din("be", [D], F32)
    dIDf = din("IDf", [P, P], F32)
    dIDb = din("IDb", [P, P], BF16)
    dMLO = din("MLO", [P, P], BF16)
    dMLOS = din("MLOS", [P, P], BF16)
    dMHI = din("MHI", [P, P], BF16)
    dMSINK = din("MSINK", [P, P], BF16)
    dROPC = din("ROPC", [T, DH], F32)
    dROPS = din("ROPS", [T, DH], F32)
    dBmat = din("Bmat", [P, P], F32)
    dCmat = din("Cmat", [P, 8], F32)
    dOneh = din("Oneh", [8, 1], F32)
    dOnes = din("OnesRow", [1, P], BF16)
    dOnesF = din("OnesRowF", [1, P], F32)
    dOut = nc.dram_tensor("out", [BPC * T, D], F32, kind="ExternalOutput").ap()

    if int(os.environ.get("KNOSCR", "0")):
        class _FakeAp:
            def __getitem__(self, *a):
                return None
            def rearrange(self, *a, **k):
                return None
            def ap(self):
                return None
            def opt(self):
                return None
        _f = _FakeAp()
        sWqkvH = sWqkvL = sWoH = sWoL = sWeH = sWeL = _f
        sBrows = sAtT = sFfn = sPP = sSin = sSout = _f
        hSin = hSout = _f
    else:
        sWqkvH = nc.dram_tensor("sWqkvH", [D, DQKV], BF16).ap()
        sWqkvL = nc.dram_tensor("sWqkvL", [D, DQKV], BF16).ap()
        sWoH = nc.dram_tensor("sWoH", [D, D], BF16).ap()
        sWoL = nc.dram_tensor("sWoL", [D, D], BF16).ap()
        sWeH = nc.dram_tensor("sWeH", [D, D], BF16).ap()
        sWeL = nc.dram_tensor("sWeL", [D, D], BF16).ap()
        sBrows = nc.dram_tensor("sBrows", [6, DQKV], BF16).ap()
        sAtT = nc.dram_tensor("sAtT", [2, KC, 4, P, P], BF16).ap()
        KDBG = int(os.environ.get("KDBG", "0"))
        kind = dict(kind="ExternalOutput") if KDBG else {}
        sFfn = nc.dram_tensor("sFfn", [NT * P, D], F32, **kind).ap()
        sPP = nc.dram_tensor("sPP", [NT * P, D], F32, **kind).ap()
        hSin = nc.dram_tensor("sSin", [NE, BPC * T], F32, **kind)
        hSout = nc.dram_tensor("sSout", [NCORES, NE, BPC * T], F32)
        sSin = hSin.ap()
        sSout = hSout.ap()

    if int(os.environ.get("KBARE", "0")):
        with nc.sbuf_tensor("bx", [64, 64], F32) as bx, \
             nc.semaphore("bd") as bd, nc.semaphore("bg") as bg, nc.Block() as block:
            @block.sync
            def _(sync):
                sync.dma_start(out=bx[:], in_=dx[0:64, 0:64]).then_inc(bd, 16)
            @block.gpsimd
            def _(g):
                g.wait_ge(bd, 16)
                g.dma_start(out=dOut[0:64, 0:64], in_=bx[:]).then_inc(bg, 16)
                g.wait_ge(bg, 16)
        return nc
    KSB = int(os.environ.get("KSB", "0"))
    def sb(name, shape, dt):
        if KSB:
            shape = [1, 4]
        return es.enter_context(nc.sbuf_tensor("b_" + name, shape, dt))

    IDf = sb("IDf", [P, P], F32)
    IDb = sb("IDb", [P, P], BF16)
    MLO = sb("MLO", [P, P], BF16)
    MLOS = sb("MLOS", [P, P], BF16)
    MHI = sb("MHI", [P, P], BF16)
    MSINK = sb("MSINK", [P, P], BF16)
    Bmat = sb("Bmat", [P, P], F32)
    Cmat = sb("Cmat", [P, 8], F32)
    Oneh = sb("Oneh", [8, 1], F32)
    OnesR = sb("OnesR", [1, P], BF16)
    OnesF = sb("OnesF", [1, P], F32)
    RopC = sb("RopC", [P, 4 * DH], F32)
    RopS = sb("RopS", [P, 4 * DH], F32)
    WrH = sb("WrH", [P, KC * NE], BF16)
    WrL = sb("WrL", [P, KC * NE], BF16)
    BrowR = sb("BrowR", [1, 2 * NE], BF16)
    Brow = sb("Brow", [1, 2 * DQKV], BF16)
    WB = [sb(f"WB{i}", [P, 2 * DQKV], BF16) for i in range(2)]
    WO = [sb(f"WOs{i}", [P, 1024], BF16) for i in range(2)]
    WE = [sb(f"WEs{i}", [P, 1024], BF16) for i in range(2)]
    AT = [sb(f"ATs{i}", [P, 1024], BF16) for i in range(2)]
    XT = [sb(f"XTs{i}", [P, D], F32) for i in range(2)]
    SCR = sb("SCR", [P, 2 * D], F32)
    XnH = sb("XnH", [P, D], BF16)
    XnL = sb("XnL", [P, D], BF16)
    XnTH = sb("XnTH", [P, 4 * D], BF16)
    XnTL = sb("XnTL", [P, 4 * D], BF16)
    QB = sb("QB", [P, 512], F32)
    QROT = sb("QROT", [P, 512], F32)
    QT = sb("QT", [P, NH * P], F32)
    KT = sb("KT", [P, NG * RING * P], F32)
    KTS = sb("KTS", [P, NG * P], F32)
    VA = sb("VA", [P, NG * RING * 132], F32)
    VAS = sb("VAS", [P, NG * 132], F32)
    EB = sb("EB", [P, 4 * 6 * P], F32)
    ATH = sb("ATH", [P, D], BF16)
    ATL = sb("ATL", [P, D], BF16)
    ATT = sb("ATT", [P, D], BF16)
    STAT = sb("STAT", [P, 40], F32)
    QN2 = sb("QN2", [P, NH], F32)
    KN2 = sb("KN2", [P, NG], F32)
    CB = sb("CB", [P, NH], F32)
    RCP = sb("RCP", [P, 2], F32)
    STILE = sb("STILE", [P, NE], F32)
    SBLK = sb("SBLK", [P, 4096], F32)
    LO = sb("LO", [P, 1], F32)
    HI = sb("HI", [P, 1], F32)
    MID = sb("MID", [P, 1], F32)
    WCOL = sb("WCOL", [P, NT], F32)

    KPS = int(os.environ.get("KPS", "0"))
    if KPS:
        ps6 = es.enter_context(nc.psum_tensor("ps6", [P, 512], F32))
        psA = ps6
        psC = es.enter_context(nc.psum_tensor("psC", [P, 512], BF16))
    else:
        ps6 = es.enter_context(nc.psum_tensor("ps6", [P, 3072], F32))
        psA = es.enter_context(nc.psum_tensor("psA", [P, 512], F32))
        psC = es.enter_context(nc.psum_tensor("psC", [P, 1024], BF16))

    sch = Sched(nc, es)
    for s in ["dC", "dW0", "dW1", "dWl0", "dWl1", "dX0", "dX1", "dR", "dB",
              "dA0", "dA1", "dAl0", "dAl1", "dO0", "dO1", "dOl0", "dOl1",
              "dE0", "dE1", "dEl0", "dEl1",
              "gS", "gA", "gF", "gP", "gO", "gW", "cc"]:
        sch.mksem(s)

    def dma(eng, sem, out, in_, reads, writes):
        pw = [(sem, sch.cnt[sem])] if sch.cnt[sem] > 0 else []
        sch.emit(eng, lambda e, o=out, i=in_: e.dma_start(out=o, in_=i),
                 reads=reads, writes=writes, sem=sem, amt=16, tick=False,
                 extra_waits=pw)

    def mm(out, lhsT, rhs, start, stop, reads, writes, tick=None):
        sch.emit("T", lambda e, o=out, l=lhsT, r=rhs, st=start, sp=stop:
                 e.matmul(o, l, r, start=st, stop=sp),
                 reads=reads, writes=writes, tick=(stop if tick is None else tick))

    def transp(out, in_, ident, start, stop, reads, writes):
        sch.emit("T", lambda e, o=out, i=in_, d=ident, st=start, sp=stop:
                 e.matmul(o, i, d, start=st, stop=sp, is_transpose=True),
                 reads=reads, writes=writes, tick=stop)

    def V(fn, reads=(), writes=()):
        sch.emit("V", fn, reads=reads, writes=writes)

    def S_(fn, reads=(), writes=()):
        sch.emit("S", fn, reads=reads, writes=writes)

    def apf(t, off, dims):
        base = t[:]
        return bass.AP(base.tensor, off, [list(base.ap[0])] + [list(d) for d in dims])

    KNC = int(os.environ.get("KNC", "99"))
    for buf, src, nm in [(IDf, dIDf, "IDf"), (IDb, dIDb, "IDb"), (MLO, dMLO, "MLO"),
                         (MLOS, dMLOS, "MLOS"), (MHI, dMHI, "MHI"), (MSINK, dMSINK, "MSINK"),
                         (Bmat, dBmat, "Bmat"), (Cmat, dCmat, "Cmat"), (Oneh, dOneh, "Oneh"),
                         (OnesR, dOnes, "OnesR"), (OnesF, dOnesF, "OnesF")][:KNC]:
        dma("Y", "dC", buf[:], src, [], [nm])

    KST = int(os.environ.get("KST", "9"))
    def split_weight(src_ap, dst_h, dst_l, ncols, dname):
        if KST < 1:
            return
        for k in range(KC):
            xb = XT[k % 2]
            xn_ = f"xt{k % 2}"
            dma("Y", f"dX{k % 2}", xb[:, 0:ncols], src_ap[k * P:(k + 1) * P, :], [xn_], [xn_])
            V(lambda e, xb=xb, n=ncols: e.tensor_copy(XnH[:, 0:n], xb[:, 0:n]),
              reads=[xn_], writes=["xnh"])
            V(lambda e, xb=xb, n=ncols: e.tensor_tensor(
                out=XnL[:, 0:n], in0=xb[:, 0:n], in1=XnH[:, 0:n], op=A.subtract),
              reads=[xn_, "xnh"], writes=["xnl"])
            dma("G", "gW", dst_h[k * P:(k + 1) * P, :], XnH[:, 0:ncols], ["xnh"], [dname])
            dma("G", "gW", dst_l[k * P:(k + 1) * P, :], XnL[:, 0:ncols], ["xnl"], [dname])

    split_weight(dWq, sWqkvH[:, 0:D], sWqkvL[:, 0:D], D, "sWqkv")
    split_weight(dWk, sWqkvH[:, D:D + 512], sWqkvL[:, D:D + 512], 512, "sWqkv")
    split_weight(dWv, sWqkvH[:, D + 512:DQKV], sWqkvL[:, D + 512:DQKV], 512, "sWqkv")
    split_weight(dWo, sWoH, sWoL, D, "sWo")
    split_weight(dWe, sWeH, sWeL, D, "sWe")

    for k in (range(KC) if KST >= 2 else []):
        dma("Y", "dR", SCR[:, 0:NE], dWr[k * P:(k + 1) * P, :], ["scr"], ["scr"])
        V(lambda e, k=k: e.tensor_copy(WrH[:, k * NE:(k + 1) * NE], SCR[:, 0:NE]),
          reads=["scr"], writes=["WrH"])
        V(lambda e, k=k: e.tensor_tensor(
            out=WrL[:, k * NE:(k + 1) * NE], in0=SCR[:, 0:NE],
            in1=WrH[:, k * NE:(k + 1) * NE], op=A.subtract),
          reads=["scr", "WrH"], writes=["WrL"])

    def split_bias(parts, row_h, row_l):
        if KST < 3:
            return
        for ap1, ncols, off in parts:
            dma("Y", "dB", SCR[0:1, off:off + ncols],
                ap1.rearrange("(a n) -> a n", a=1), ["scr"], ["scr"])
        V(lambda e: e.tensor_copy(Brow[0:1, 0:DQKV], SCR[0:1, 0:DQKV]),
          reads=["scr"], writes=["browh"])
        V(lambda e: e.tensor_tensor(
            out=Brow[0:1, DQKV:2 * DQKV], in0=SCR[0:1, 0:DQKV],
            in1=Brow[0:1, 0:DQKV], op=A.subtract),
          reads=["scr", "browh"], writes=["browl"])
        dma("G", "gW", row_h, Brow[0:1, 0:DQKV], ["browh"], ["sBrows"])
        dma("G", "gW", row_l, Brow[0:1, DQKV:2 * DQKV], ["browl"], ["sBrows"])

    split_bias([(dbq, D, 0), (dbk, 512, D), (dbv, 512, D + 512)], sBrows[0:1, :], sBrows[1:2, :])
    split_bias([(dbo, D, 0)], sBrows[2:3, :], sBrows[3:4, :])
    split_bias([(dbe, D, 0)], sBrows[4:5, :], sBrows[5:6, :])
    if KST >= 3:
        dma("Y", "dB", SCR[0:1, 0:NE], dbr.rearrange("(a n) -> a n", a=1), ["scr"], ["scr"])
        V(lambda e: e.tensor_copy(BrowR[0:1, 0:NE], SCR[0:1, 0:NE]),
          reads=["scr"], writes=["browr"])
        V(lambda e: e.tensor_tensor(
            out=BrowR[0:1, NE:2 * NE], in0=SCR[0:1, 0:NE],
            in1=BrowR[0:1, 0:NE], op=A.subtract),
          reads=["scr", "browr"], writes=["browr"])

    if KST < 4:
        with nc.Block() as block:
            sch.realize(block)
        es.close()
        return nc
    V(lambda e: e.memset(VA[:], 0.0), reads=[], writes=["va0", "va1", "va2", "va3", "va4", "vaone"])
    V(lambda e: e.memset(
        VA[:].rearrange("p (gs c) -> p gs c", c=132)[:, :, 131:132], 1.0),
      reads=["vaone"], writes=["vaone"])
    V(lambda e: e.memset(VAS[:], 0.0), reads=[], writes=["VAS"])
    V(lambda e: e.memset(
        VAS[:].rearrange("p (g c) -> p g c", c=132)[:, :, 131:132], 1.0),
      reads=["VAS"], writes=["VAS"])
    V(lambda e: e.memset(KTS[:], 0.0), reads=[], writes=["KTS"])

    KP = int(os.environ.get("KP", "9"))
    NSEQ = 0 if KP < 2 else (1 if KP == 2 else BPC)
    for seq in range(NSEQ):
        tok0 = seq * T
        for sbi in range(TPS // 4):
            sb4 = [sbi * 4 + j for j in range(4)]
            dma("Y", "dR", RopC[:].rearrange("p (j c) -> p j c", j=4),
                dROPC.rearrange("(j p) c -> p j c", p=P)[:, sbi * 4:(sbi + 1) * 4, :],
                ["ropC"], ["ropC"])
            dma("Y", "dR", RopS[:].rearrange("p (j c) -> p j c", j=4),
                dROPS.rearrange("(j p) c -> p j c", p=P)[:, sbi * 4:(sbi + 1) * 4, :],
                ["ropS"], ["ropS"])

            dma("Y", "dB", Brow[0:1, 0:DQKV], sBrows[0:1, :], ["browh", "sBrows"], ["browh"])
            dma("Y", "dB", Brow[0:1, DQKV:2 * DQKV], sBrows[1:2, :],
                ["browl", "sBrows"], ["browl"])
            for jj, tp in enumerate(sb4):
                tt = seq * TPS + tp
                xslot = tt % 2
                xn_ = f"xt{xslot}"
                dma("Y", f"dX{xslot}", XT[xslot][:],
                    dx[tok0 + tp * P: tok0 + (tp + 1) * P, :], [xn_], [xn_])
                st = STAT
                V(lambda e, xb=XT[xslot]: e.reduce_sum(out=st[:, 0:1], in_=xb[:], axis=AXX),
                  reads=[xn_], writes=["st"])
                S_(lambda e, xb=XT[xslot]: e.activation(
                    out=SCR[:, 0:D], in_=xb[:], func=AF.Square, scale=1.0,
                    accum_out=st[:, 1:2]),
                   reads=[xn_], writes=["scr", "st"])
                V(lambda e: e.tensor_scalar(out=st[:, 2:3], in0=st[:, 0:1],
                                            scalar1=1.0 / D, scalar2=None, op0=A.mult),
                  reads=["st"], writes=["st"])
                V(lambda e: e.tensor_tensor(out=st[:, 3:4], in0=st[:, 2:3],
                                            in1=st[:, 2:3], op=A.mult),
                  reads=["st"], writes=["st"])
                V(lambda e: e.tensor_scalar(out=st[:, 4:5], in0=st[:, 1:2],
                                            scalar1=1.0 / D, scalar2=None, op0=A.mult),
                  reads=["st"], writes=["st"])
                V(lambda e: e.tensor_tensor(out=st[:, 5:6], in0=st[:, 4:5],
                                            in1=st[:, 3:4], op=A.subtract),
                  reads=["st"], writes=["st"])
                V(lambda e: e.tensor_scalar(out=st[:, 6:7], in0=st[:, 5:6],
                                            scalar1=EPS, scalar2=None, op0=A.add),
                  reads=["st"], writes=["st"])
                V(lambda e: e.memset(st[:, 7:8], 1.0), reads=[], writes=["sty"])
                for _ in range(6):
                    V(lambda e: e.tensor_tensor(out=st[:, 8:9], in0=st[:, 7:8],
                                                in1=st[:, 7:8], op=A.mult),
                      reads=["sty"], writes=["st8"])
                    V(lambda e: e.tensor_tensor(out=st[:, 8:9], in0=st[:, 8:9],
                                                in1=st[:, 6:7], op=A.mult),
                      reads=["st8", "st"], writes=["st8"])
                    V(lambda e: e.tensor_scalar(out=st[:, 8:9], in0=st[:, 8:9],
                                                scalar1=-0.5, scalar2=1.5,
                                                op0=A.mult, op1=A.add),
                      reads=["st8"], writes=["st8"])
                    V(lambda e: e.tensor_tensor(out=st[:, 7:8], in0=st[:, 7:8],
                                                in1=st[:, 8:9], op=A.mult),
                      reads=["sty", "st8"], writes=["sty"])
                V(lambda e: e.tensor_tensor(out=st[:, 9:10], in0=st[:, 2:3],
                                            in1=st[:, 7:8], op=A.mult),
                  reads=["st", "sty"], writes=["st9"])
                V(lambda e: e.tensor_scalar(out=st[:, 9:10], in0=st[:, 9:10],
                                            scalar1=-1.0, scalar2=None, op0=A.mult),
                  reads=["st9"], writes=["st9"])
                V(lambda e, xb=XT[xslot]: e.tensor_scalar(
                    out=SCR[:, 0:D], in0=xb[:], scalar1=st[:, 7:8], scalar2=st[:, 9:10],
                    op0=A.mult, op1=A.add),
                  reads=[xn_, "sty", "st9"], writes=["scr"])
                S_(lambda e: e.activation(out=XnH[:], in_=SCR[:, 0:D], func=AF.Copy, scale=1.0),
                   reads=["scr"], writes=["xnh"])
                V(lambda e: e.tensor_tensor(out=XnL[:], in0=SCR[:, 0:D], in1=XnH[:], op=A.subtract),
                  reads=["scr", "xnh"], writes=["xnl"])
                for src, dst, rbuf, wbuf in ((XnH, XnTH, "xnh", "xnth"),
                                             (XnL, XnTL, "xnl", "xntl")):
                    for g8 in range(2):
                        for b8 in range(8):
                            k = g8 * 8 + b8
                            transp(psC[:, b8 * P:(b8 + 1) * P], src[:, k * P:(k + 1) * P],
                                   IDb[:], start=(b8 == 0), stop=(b8 == 7),
                                   reads=[rbuf, "IDb"], writes=["psC"])
                        V(lambda e, g8=g8, jj=jj, dst=dst: e.tensor_copy(
                            dst[:, jj * D + g8 * 1024:jj * D + (g8 + 1) * 1024], psC[:]),
                          reads=["psC"], writes=[wbuf])

                for k in range(KC):
                    ws = k % 2
                    wh_, wl_ = f"wb{ws}h", f"wb{ws}l"
                    dma("S", f"dW{ws}", WB[ws][:, 0:DQKV],
                        sWqkvH[k * P:(k + 1) * P, :], [wh_, "sWqkv"], [wh_])
                    dma("Y", f"dWl{ws}", WB[ws][:, DQKV:2 * DQKV],
                        sWqkvL[k * P:(k + 1) * P, :], [wl_, "sWqkv"], [wl_])
                    xh = XnTH[:, jj * D + k * P:jj * D + (k + 1) * P]
                    xl = XnTL[:, jj * D + k * P:jj * D + (k + 1) * P]
                    st0 = (k == 0)
                    for dc in range(6):
                        wh = WB[ws][:, dc * 512:(dc + 1) * 512]
                        wl = WB[ws][:, DQKV + dc * 512:DQKV + (dc + 1) * 512]
                        pn = f"ps6_{dc}"
                        mm(ps6[:, dc * 512:(dc + 1) * 512], xh, wh, st0, False,
                           ["xnth", wh_], [pn])
                        mm(ps6[:, dc * 512:(dc + 1) * 512], xh, wl, False, False,
                           ["xnth", wl_], [pn])
                    mm(psA[:, 0:NE], xh, WrH[:, k * NE:(k + 1) * NE], st0, False,
                       ["xnth", "WrH"], ["psA"])
                    mm(psA[:, 0:NE], xh, WrL[:, k * NE:(k + 1) * NE], False, False,
                       ["xnth", "WrL"], ["psA"])
                    for dc in range(6):
                        wh = WB[ws][:, dc * 512:(dc + 1) * 512]
                        pn = f"ps6_{dc}"
                        mm(ps6[:, dc * 512:(dc + 1) * 512], xl, wh, False, False,
                           ["xntl", wh_], [pn])
                    mm(psA[:, 0:NE], xl, WrH[:, k * NE:(k + 1) * NE], False, False,
                       ["xntl", "WrH"], ["psA"], tick=True)
                for dc in range(6):
                    pn = f"ps6_{dc}"
                    mm(ps6[:, dc * 512:(dc + 1) * 512], OnesR[:],
                       Brow[0:1, dc * 512:(dc + 1) * 512], False, False,
                       ["browh", "OnesR"], [pn])
                    mm(ps6[:, dc * 512:(dc + 1) * 512], OnesR[:],
                       Brow[0:1, DQKV + dc * 512:DQKV + (dc + 1) * 512], False, True,
                       ["browl", "OnesR"], [pn])
                mm(psA[:, 0:NE], OnesR[:], BrowR[0:1, 0:NE], False, False,
                   ["browr", "OnesR"], ["psA"])
                mm(psA[:, 0:NE], OnesR[:], BrowR[0:1, NE:2 * NE], False, True,
                   ["browr", "OnesR"], ["psA"])

                V(lambda e: e.reduce_max(out=st[:, 10:11], in_=psA[:, 0:NE],
                                         axis=AXX, negate=True),
                  reads=["psA"], writes=["st10"])
                S_(lambda e: e.activation(out=STILE[:], in_=psA[:, 0:NE], func=AF.Exp,
                                          bias=st[:, 10:11], scale=1.0,
                                          accum_out=st[:, 11:12]),
                   reads=["psA", "st10"], writes=["stile", "st11"])
                V(lambda e: e.reciprocal(out=st[:, 12:13], in_=st[:, 11:12]),
                  reads=["st11"], writes=["st12"])
                V(lambda e: e.tensor_scalar(out=STILE[:], in0=STILE[:],
                                            scalar1=st[:, 12:13], scalar2=None, op0=A.mult),
                  reads=["stile", "st12"], writes=["stile"])
                transp(psA[0:NE, 0:P], STILE[:], IDf[:], True, True,
                       ["stile", "IDf"], ["psA"])
                V(lambda e: e.tensor_copy(SCR[0:NE, 0:P], psA[0:NE, 0:P]),
                  reads=["psA"], writes=["scr"])
                dma("G", "gS", sSin[:, tt * P:(tt + 1) * P], SCR[0:NE, 0:P],
                    ["scr"], ["dSin"])

                slot = tp % RING
                co_c = jj * DH
                for ch in range(4):
                    pq = ps6[:, ch * 512:(ch + 1) * 512]
                    pn = f"ps6_{ch}"
                    cb_ap = apf(RopC, co_c, [[0, 4], [1, DH]])
                    sb_lo = apf(RopS, co_c, [[0, 4], [1, 64]])
                    sb_hi = apf(RopS, co_c + 64, [[0, 4], [1, 64]])
                    V(lambda e, pq=pq, cb_ap=cb_ap: e.tensor_tensor(
                        out=QROT[:].rearrange("p (h c) -> p h c", h=4),
                        in0=pq.rearrange("p (h c) -> p h c", h=4),
                        in1=cb_ap, op=A.mult),
                      reads=[pn, "ropC"], writes=["qrot"])
                    V(lambda e, pq=pq, sb_lo=sb_lo: e.tensor_tensor(
                        out=QB[:].rearrange("p (h c) -> p h c", h=4)[:, :, 0:64],
                        in0=pq.rearrange("p (h c) -> p h c", h=4)[:, :, 64:128],
                        in1=sb_lo, op=A.mult),
                      reads=[pn, "ropS"], writes=["qb"])
                    V(lambda e, pq=pq, sb_hi=sb_hi: e.tensor_tensor(
                        out=QB[:].rearrange("p (h c) -> p h c", h=4)[:, :, 64:128],
                        in0=pq.rearrange("p (h c) -> p h c", h=4)[:, :, 0:64],
                        in1=sb_hi, op=A.mult),
                      reads=[pn, "ropS"], writes=["qb"])
                    V(lambda e: e.tensor_tensor(out=QROT[:], in0=QROT[:], in1=QB[:], op=A.add),
                      reads=["qrot", "qb"], writes=["qrot"])
                    V(lambda e: e.tensor_tensor(out=QB[:], in0=QROT[:], in1=QROT[:], op=A.mult),
                      reads=["qrot"], writes=["qb"])
                    V(lambda e, ch=ch: e.tensor_reduce(
                        out=QN2[:, ch * 4:(ch + 1) * 4],
                        in_=QB[:].rearrange("p (h c) -> p h c", h=4), axis=AXX, op=A.add),
                      reads=["qb"], writes=["qn2"])
                    for b4 in range(4):
                        transp(psA[:, b4 * P:(b4 + 1) * P], QROT[:, b4 * P:(b4 + 1) * P],
                               IDf[:], start=(b4 == 0), stop=(b4 == 3),
                               reads=["qrot", "IDf"], writes=["psA"])
                    V(lambda e, ch=ch: e.tensor_copy(QT[:, ch * 512:(ch + 1) * 512], psA[:]),
                      reads=["psA"], writes=["qt"])
                pk = ps6[:, 2048:2560]
                cb_ap = apf(RopC, co_c, [[0, 4], [1, DH]])
                sb_lo = apf(RopS, co_c, [[0, 4], [1, 64]])
                sb_hi = apf(RopS, co_c + 64, [[0, 4], [1, 64]])
                V(lambda e, cb_ap=cb_ap: e.tensor_tensor(
                    out=QROT[:].rearrange("p (h c) -> p h c", h=4),
                    in0=pk.rearrange("p (h c) -> p h c", h=4),
                    in1=cb_ap, op=A.mult),
                  reads=["ps6_4", "ropC"], writes=["qrot"])
                V(lambda e, sb_lo=sb_lo: e.tensor_tensor(
                    out=QB[:].rearrange("p (h c) -> p h c", h=4)[:, :, 0:64],
                    in0=pk.rearrange("p (h c) -> p h c", h=4)[:, :, 64:128],
                    in1=sb_lo, op=A.mult),
                  reads=["ps6_4", "ropS"], writes=["qb"])
                V(lambda e, sb_hi=sb_hi: e.tensor_tensor(
                    out=QB[:].rearrange("p (h c) -> p h c", h=4)[:, :, 64:128],
                    in0=pk.rearrange("p (h c) -> p h c", h=4)[:, :, 0:64],
                    in1=sb_hi, op=A.mult),
                  reads=["ps6_4", "ropS"], writes=["qb"])
                V(lambda e: e.tensor_tensor(out=QROT[:], in0=QROT[:], in1=QB[:], op=A.add),
                  reads=["qrot", "qb"], writes=["qrot"])
                V(lambda e: e.tensor_tensor(out=QB[:], in0=QROT[:], in1=QROT[:], op=A.mult),
                  reads=["qrot"], writes=["qb"])
                V(lambda e: e.tensor_reduce(
                    out=st[:, 16:20], in_=QB[:].rearrange("p (g c) -> p g c", g=4),
                    axis=AXX, op=A.add),
                  reads=["qb"], writes=["st16"])
                if tp == 0:
                    V(lambda e: e.tensor_copy(KN2[:], st[:, 16:20]),
                      reads=["st16"], writes=["kn2"])
                else:
                    V(lambda e: e.tensor_tensor(out=KN2[:], in0=KN2[:],
                                                in1=st[:, 16:20], op=A.max),
                      reads=["kn2", "st16"], writes=["kn2"])
                for g in range(4):
                    transp(psA[:, g * P:(g + 1) * P], QROT[:, g * P:(g + 1) * P],
                           IDf[:], start=(g == 0), stop=(g == 3),
                           reads=["qrot", "IDf"], writes=["psA"])
                V(lambda e, slot=slot: e.tensor_copy(
                    KT[:].rearrange("p (g s c) -> p g s c", g=NG, s=RING)[:, :, slot, :],
                    psA[:].rearrange("p (g c) -> p g c", g=4)),
                  reads=["psA"], writes=[f"kt{slot}"])
                if tp == 0:
                    V(lambda e: e.tensor_copy(
                        KTS[:].rearrange("p (g c) -> p g c", g=NG)[:, :, 0:4],
                        psA[:].rearrange("p (g c) -> p g c", g=4)[:, :, 0:4]),
                      reads=["psA", "KTS"], writes=["KTS"])
                V(lambda e, slot=slot: e.tensor_copy(
                    VA[:].rearrange("p (g s c) -> p g s c", g=NG, s=RING)[:, :, slot, 0:128],
                    ps6[:, 2560:3072].rearrange("p (g c) -> p g c", g=4)),
                  reads=["ps6_5", "vaone"], writes=[f"va{slot}"])
                if tp == 0:
                    V(lambda e: e.tensor_copy(
                        VAS[0:4, :].rearrange("p (g c) -> p g c", g=NG)[:, :, 0:128],
                        ps6[0:4, 2560:3072].rearrange("p (g c) -> p g c", g=4)),
                      reads=["ps6_5", "VAS"], writes=["VAS"])

                if tp >= 5:
                    slots = [("sink", None, MSINK, "MSINK")]
                    for c in range(tp - 4, tp + 1):
                        msk, mn = (MLO, "MLO") if c == tp - 4 else \
                                  ((MHI, "MHI") if c == tp else (None, None))
                        slots.append(("ring", c, msk, mn))
                else:
                    slots = []
                    for c in range(0, tp + 1):
                        if c == tp:
                            msk, mn = MHI, "MHI"
                        elif tp == 4 and c == 0:
                            msk, mn = MLOS, "MLOS"
                        else:
                            msk, mn = None, None
                        slots.append(("ring", c, msk, mn))
                nsl = len(slots)

                transp(psA[0:NH, 0:P], QN2[:], IDf[:], True, True, ["qn2", "IDf"], ["psA"])
                V(lambda e: e.reduce_max(out=st[0:NH, 20:21], in_=psA[0:NH, 0:P], axis=AXX),
                  reads=["psA"], writes=["st20"])
                transp(psA[0:NG, 0:P], KN2[:], IDf[:], True, True, ["kn2", "IDf"], ["psA"])
                V(lambda e: e.reduce_max(out=st[0:NG, 21:22], in_=psA[0:NG, 0:P], axis=AXX),
                  reads=["psA"], writes=["st21"])
                transp(psA[0:1, 0:NH], st[0:NH, 20:21], IDf[0:NH, 0:NH], True, True,
                       ["st20", "IDf"], ["psA"])
                V(lambda e: e.tensor_copy(SCR[0:1, 0:NH], psA[0:1, 0:NH]),
                  reads=["psA"], writes=["scr"])
                transp(psA[0:1, 0:NG], st[0:NG, 21:22], IDf[0:NG, 0:NG], True, True,
                       ["st21", "IDf"], ["psA"])
                V(lambda e: e.tensor_copy(SCR[0:1, 16:16 + NG], psA[0:1, 0:NG]),
                  reads=["psA", "scr"], writes=["scr"])
                V(lambda e: e.tensor_reduce(
                    out=SCR[0:1, 24:28], in_=SCR[0:1, 0:16].rearrange("p (g h) -> p g h", g=4),
                    axis=AXX, op=A.max),
                  reads=["scr"], writes=["scr"])
                V(lambda e: e.tensor_tensor(
                    out=SCR[0:1, 32:36], in0=SCR[0:1, 24:28],
                    in1=SCR[0:1, 16:20], op=A.add),
                  reads=["scr"], writes=["scr"])
                V(lambda e: e.tensor_scalar(out=SCR[0:1, 32:36], in0=SCR[0:1, 32:36],
                                            scalar1=-0.5 * SCALE, scalar2=None, op0=A.mult),
                  reads=["scr"], writes=["scr"])
                mm(psA[:, 0:NG], OnesF[:], SCR[0:1, 32:36], True, True,
                   ["scr", "OnesF"], ["psA"])
                V(lambda e: e.tensor_copy(CB[:, 0:NG], psA[:, 0:NG]), reads=["psA"], writes=["cb"])

                for g in range(NG):
                    order = ([4, 5, 0, 1, 2, 3] if nsl == 6 else list(range(nsl)))
                    for si in order:
                        kind, c, msk, mn = slots[si]
                        if kind == "sink":
                            lhs = KTS[:].rearrange("p (g2 c2) -> p g2 c2", g2=NG)[:, g, :]
                            lrd = ["KTS"]
                        else:
                            lhs = KT[:].rearrange("p (g2 s c2) -> p g2 s c2",
                                                  g2=NG, s=RING)[:, g, c % RING, :]
                            lrd = [f"kt{c % RING}"]
                        pn = f"ps6_{si}"
                        for h in range(4):
                            hh = g * 4 + h
                            mm(ps6[:, si * 512 + h * P: si * 512 + (h + 1) * P],
                               lhs, QT[:, hh * P:(hh + 1) * P],
                               start=(h == 0), stop=(msk is None and h == 3),
                               reads=lrd + ["qt"], writes=[pn])
                        if msk is not None:
                            for h in range(4):
                                mm(ps6[:, si * 512 + h * P: si * 512 + (h + 1) * P],
                                   IDb[:], msk[:], start=False, stop=(h == 3),
                                   reads=["IDb", mn], writes=[pn])
                    S_(lambda e, g=g, nsl=nsl: e.activation(
                        out=EB[:, 0:nsl * 512],
                        in_=ps6[:, 0:nsl * 512],
                        func=AF.Exp, bias=CB[:, g:g + 1], scale=SCALE),
                       reads=[f"ps6_{si}" for si in range(nsl)] + ["cb"],
                       writes=["eb0", "eb1", "eb2", "eb3"])
                    for h in range(4):
                        hh = g * 4 + h
                        psx = ps6[:, h * 512:h * 512 + 132]
                        pname = f"ps6_{h}"
                        for si, (kind, c, msk, mn) in enumerate(slots):
                            if kind == "sink":
                                rhs = VAS[:].rearrange("p (g2 c2) -> p g2 c2",
                                                       g2=NG)[:, g, :]
                                rrd = ["VAS"]
                            else:
                                rhs = VA[:].rearrange("p (g2 s c2) -> p g2 s c2",
                                                      g2=NG, s=RING)[:, g, c % RING, :]
                                rrd = [f"va{c % RING}"]
                            mm(psx,
                               EB[:, si * 512 + h * P: si * 512 + (h + 1) * P],
                               rhs, start=(si == 0), stop=(si == nsl - 1),
                               reads=["eb0", "eb1", "eb2", "eb3"] + rrd,
                               writes=[pname])
                        V(lambda e, psx=psx: e.reciprocal(out=RCP[:, 0:1],
                                                          in_=psx[:, 131:132]),
                          reads=[pname], writes=["rcp"])
                        V(lambda e, psx=psx, hh=hh: e.tensor_scalar(
                            out=ATH[:, hh * P:(hh + 1) * P], in0=psx[:, 0:P],
                            scalar1=RCP[:, 0:1], scalar2=None, op0=A.mult),
                          reads=[pname, "rcp"], writes=["ath"])
                        V(lambda e, psx=psx, hh=hh: e.scalar_tensor_tensor(
                            out=ATL[:, hh * P:(hh + 1) * P], in0=psx[:, 0:P],
                            scalar=RCP[:, 0:1], in1=ATH[:, hh * P:(hh + 1) * P],
                            op0=A.mult, op1=A.subtract),
                          reads=[pname, "rcp", "ath"], writes=["atl"])

                for half, (src, rbuf) in enumerate(((ATH, "ath"), (ATL, "atl"))):
                    for g8 in range(2):
                        for b8 in range(8):
                            k = g8 * 8 + b8
                            transp(psC[:, b8 * P:(b8 + 1) * P], src[:, k * P:(k + 1) * P],
                                   IDb[:], start=(b8 == 0), stop=(b8 == 7),
                                   reads=[rbuf, "IDb"], writes=["psC"])
                        V(lambda e, g8=g8: e.tensor_copy(ATT[:, g8 * 1024:(g8 + 1) * 1024],
                                                         psC[:]),
                          reads=["psC"], writes=["att"])
                    dma("G", "gA",
                        sAtT[half, :, jj, :, :].rearrange("k p q -> p k q"),
                        ATT[:].rearrange("p (k q) -> p k q", k=KC),
                        ["att"], ["sAtT"])

            dma("Y", "dB", Brow[0:1, 0:D], sBrows[2:3, 0:D], ["browh", "sBrows"], ["browh"])
            dma("Y", "dB", Brow[0:1, DQKV:DQKV + D], sBrows[3:4, 0:D],
                ["browl", "sBrows"], ["browl"])
            for dc4 in range(4):
                co = dc4 * 512
                for k in range(KC):
                    asl = k % 2
                    ah_, al_ = f"at{asl}h", f"at{asl}l"
                    dma("S", f"dA{asl}", AT[asl][:, 0:512].rearrange("p (t q) -> p t q", t=4),
                        sAtT[0, k, :, :, :].rearrange("t p q -> p t q"),
                        [ah_, "sAtT"], [ah_])
                    dma("Y", f"dAl{asl}", AT[asl][:, 512:1024].rearrange("p (t q) -> p t q", t=4),
                        sAtT[1, k, :, :, :].rearrange("t p q -> p t q"),
                        [al_, "sAtT"], [al_])
                    wsl = k % 2
                    wh_, wl_ = f"wo{wsl}h", f"wo{wsl}l"
                    dma("S", f"dO{wsl}", WO[wsl][:, 0:512],
                        sWoH[k * P:(k + 1) * P, co:co + 512], [wh_, "sWo"], [wh_])
                    dma("Y", f"dOl{wsl}", WO[wsl][:, 512:1024],
                        sWoL[k * P:(k + 1) * P, co:co + 512], [wl_, "sWo"], [wl_])
                    for q4 in range(4):
                        ah = AT[asl][:, q4 * P:(q4 + 1) * P]
                        al = AT[asl][:, 512 + q4 * P:512 + (q4 + 1) * P]
                        bn = f"ps6_{q4}"
                        out = ps6[:, q4 * 512:(q4 + 1) * 512]
                        mm(out, ah, WO[wsl][:, 0:512], (k == 0), False, [ah_, wh_], [bn])
                        mm(out, ah, WO[wsl][:, 512:1024], False, False, [ah_, wl_], [bn])
                        mm(out, al, WO[wsl][:, 0:512], False, False, [al_, wh_], [bn],
                           tick=(q4 == 3))
                for q4 in range(4):
                    bn = f"ps6_{q4}"
                    out = ps6[:, q4 * 512:(q4 + 1) * 512]
                    mm(out, OnesR[:], Brow[0:1, co:co + 512], False, False,
                       ["browh", "OnesR"], [bn])
                    mm(out, OnesR[:], Brow[0:1, DQKV + co:DQKV + co + 512], False, True,
                       ["browl", "OnesR"], [bn])
                    psl = EB[:, q4 * 512:(q4 + 1) * 512]
                    pn2 = f"eb{q4}"
                    tt2 = seq * TPS + sb4[q4]
                    xs = q4 % 2
                    xn2 = f"xt{xs}"
                    dma("Y", f"dX{xs}", XT[xs][:, 0:512],
                        dx[tt2 * P:(tt2 + 1) * P, co:co + 512], [xn2], [xn2])
                    V(lambda e, out=out, psl=psl, xs=xs: e.tensor_tensor(
                        out=psl, in0=out, in1=XT[xs][:, 0:512], op=A.add),
                      reads=[bn, xn2], writes=[pn2])
                    dma("G", "gP", sPP[tt2 * P:(tt2 + 1) * P, co:co + 512],
                        psl, [pn2], ["sPP"])

            dma("Y", "dB", Brow[0:1, 0:D], sBrows[4:5, 0:D], ["browh", "sBrows"], ["browh"])
            dma("Y", "dB", Brow[0:1, DQKV:DQKV + D], sBrows[5:6, 0:D],
                ["browl", "sBrows"], ["browl"])
            for dc4 in range(4):
                co = dc4 * 512
                for k in range(KC):
                    wsl = k % 2
                    wh_, wl_ = f"we{wsl}h", f"we{wsl}l"
                    dma("S", f"dE{wsl}", WE[wsl][:, 0:512],
                        sWeH[k * P:(k + 1) * P, co:co + 512], [wh_, "sWe"], [wh_])
                    dma("Y", f"dEl{wsl}", WE[wsl][:, 512:1024],
                        sWeL[k * P:(k + 1) * P, co:co + 512], [wl_, "sWe"], [wl_])
                    for jj in range(4):
                        xh = XnTH[:, jj * D + k * P:jj * D + (k + 1) * P]
                        xl = XnTL[:, jj * D + k * P:jj * D + (k + 1) * P]
                        bn = f"ps6_{jj}"
                        out = ps6[:, jj * 512:(jj + 1) * 512]
                        mm(out, xh, WE[wsl][:, 0:512], (k == 0), False, ["xnth", wh_], [bn])
                        mm(out, xh, WE[wsl][:, 512:1024], False, False, ["xnth", wl_], [bn])
                        mm(out, xl, WE[wsl][:, 0:512], False, False, ["xntl", wh_], [bn],
                           tick=(jj == 3))
                for jj in range(4):
                    bn = f"ps6_{jj}"
                    out = ps6[:, jj * 512:(jj + 1) * 512]
                    mm(out, OnesR[:], Brow[0:1, co:co + 512], False, False,
                       ["browh", "OnesR"], [bn])
                    mm(out, OnesR[:], Brow[0:1, DQKV + co:DQKV + co + 512], False, True,
                       ["browl", "OnesR"], [bn])
                    psl = EB[:, jj * 512:(jj + 1) * 512]
                    pn2 = f"eb{jj}"
                    V(lambda e, out=out, psl=psl: e.tensor_copy(psl, out),
                      reads=[bn], writes=[pn2])
                    tt2 = seq * TPS + sb4[jj]
                    dma("G", "gF", sFfn[tt2 * P:(tt2 + 1) * P, co:co + 512],
                        psl, [pn2], ["sFfn"])

    if KP < 4:
        with nc.Block() as block:
            sch.realize(block)
        es.close()
        return nc
    sch.emit("G", lambda e: e.collective_compute(
        "AllGather", A.bypass, replica_groups=[list(range(NCORES))],
        ins=[hSin.ap().opt()], outs=[hSout.ap().opt()]),
        reads=["dSin"], writes=["ccout"], sem="cc", amt=1, tick=False)
    dma("Y", "dR", SBLK[:], sSout.rearrange("r e2 f -> (r e2) f"), ["ccout"], ["sblk"])
    V(lambda e: e.memset(LO[:], 0.0), reads=[], writes=["lo"])
    V(lambda e: e.memset(HI[:], 1.0), reads=[], writes=["hi"])
    for _ in range(NBIS):
        V(lambda e: e.tensor_tensor(out=MID[:], in0=LO[:], in1=HI[:], op=A.add),
          reads=["lo", "hi"], writes=["mid"])
        V(lambda e: e.tensor_scalar(out=MID[:], in0=MID[:], scalar1=0.5,
                                    scalar2=None, op0=A.mult),
          reads=["mid"], writes=["mid"])
        for sl in range(4):
            V(lambda e, sl=sl: e.tensor_scalar(
                out=SCR[:, 0:1024], in0=SBLK[:, sl * 1024:(sl + 1) * 1024],
                scalar1=MID[:], scalar2=None, op0=A.is_gt, op1=A.add,
                accum_out=STAT[:, 24 + sl:25 + sl]),
              reads=["sblk", "mid"], writes=["scr", "stc"])
        V(lambda e: e.tensor_reduce(out=STAT[:, 28:29], in_=STAT[:, 24:28],
                                    axis=AXX, op=A.add),
          reads=["stc"], writes=["cnt"])
        mm(psA[:, 0:1], Bmat[:], STAT[:, 28:29], True, True, ["cnt", "Bmat"], ["psA"])
        V(lambda e: e.tensor_scalar(out=STAT[:, 29:30], in0=psA[:, 0:1],
                                    scalar1=float(TOPK), scalar2=None, op0=A.is_ge),
          reads=["psA"], writes=["ge"])
        V(lambda e: e.tensor_tensor(out=STAT[:, 30:31], in0=MID[:], in1=LO[:],
                                    op=A.subtract),
          reads=["mid", "lo"], writes=["t30"])
        V(lambda e: e.tensor_tensor(out=STAT[:, 30:31], in0=STAT[:, 30:31],
                                    in1=STAT[:, 29:30], op=A.mult),
          reads=["t30", "ge"], writes=["t30"])
        V(lambda e: e.tensor_tensor(out=LO[:], in0=LO[:], in1=STAT[:, 30:31], op=A.add),
          reads=["lo", "t30"], writes=["lo"])
        V(lambda e: e.tensor_tensor(out=STAT[:, 31:32], in0=MID[:], in1=HI[:],
                                    op=A.subtract),
          reads=["mid", "hi"], writes=["t31"])
        V(lambda e: e.tensor_tensor(out=STAT[:, 32:33], in0=STAT[:, 31:32],
                                    in1=STAT[:, 29:30], op=A.mult),
          reads=["t31", "ge"], writes=["t32"])
        V(lambda e: e.tensor_tensor(out=STAT[:, 31:32], in0=STAT[:, 31:32],
                                    in1=STAT[:, 32:33], op=A.subtract),
          reads=["t31", "t32"], writes=["t31"])
        V(lambda e: e.tensor_tensor(out=HI[:], in0=HI[:], in1=STAT[:, 31:32], op=A.add),
          reads=["hi", "t31"], writes=["hi"])
    V(lambda e: e.scalar_tensor_tensor(out=SBLK[:], in0=SBLK[:], scalar=LO[:],
                                       in1=SBLK[:], op0=A.is_gt, op1=A.mult),
      reads=["sblk", "lo"], writes=["sblk"])
    for sl in range(8):
        mm(psA[0:8, 0:512], Cmat[:, 0:8], SBLK[:, sl * 512:(sl + 1) * 512],
           True, True, ["sblk", "Cmat"], ["psA"])
        V(lambda e, sl=sl: e.tensor_copy(SCR[0:8, sl * 512:(sl + 1) * 512], psA[0:8, 0:512]),
          reads=["psA"], writes=["scr"])
    for s128 in range(NT):
        mm(psA[:, 0:1], SCR[0:8, s128 * P:(s128 + 1) * P], Oneh[:], True, True,
           ["scr", "Oneh"], ["psA"])
        V(lambda e, s128=s128: e.tensor_copy(WCOL[:, s128:s128 + 1], psA[:, 0:1]),
          reads=["psA"], writes=["wcol"])

    if KP < 5:
        with nc.Block() as block:
            sch.realize(block)
        es.close()
        return nc
    FFHs = [SCR[:, 0:1024], SCR[:, 1024:2048]]
    PPHs = [SCR[:, 2048:3072], SCR[:, 3072:4096]]
    first_touch = {"ffh0": True, "ffh1": True, "pph0": True, "pph1": True}
    for tt in range(NT):
        for h2 in range(2):
            co = h2 * 1024
            sl = (tt * 2 + h2) % 2
            xn_ = f"xt{sl}"
            fn_ = f"ffh{sl}"
            pn_ = f"pph{sl}"
            fw = [fn_] + (["scr"] if first_touch[fn_] else [])
            pw = [pn_] + (["scr"] if first_touch[pn_] else [])
            first_touch[fn_] = first_touch[pn_] = False
            dma("Y", f"dA{sl}", FFHs[sl], sFfn[tt * P:(tt + 1) * P, co:co + 1024],
                [fn_, "sFfn"], fw)
            dma("Y", f"dO{sl}", PPHs[sl], sPP[tt * P:(tt + 1) * P, co:co + 1024],
                [pn_, "sPP"], pw)
            V(lambda e, sl=sl, tt=tt: e.scalar_tensor_tensor(
                out=PPHs[sl], in0=FFHs[sl], scalar=WCOL[:, tt:tt + 1],
                in1=PPHs[sl], op0=A.mult, op1=A.add),
              reads=[fn_, "wcol", pn_], writes=[pn_])
            dma("G", "gO", dOut[tt * P:(tt + 1) * P, co:co + 1024], PPHs[sl],
                [pn_], ["outw"])

    sch.wait_all("G", [("gO", sch.cnt["gO"])])

    with nc.Block() as block:
        sch.realize(block)
    es.close()
    return nc


def _consts():
    import ml_dtypes
    bf = lambda a: np.asarray(a, dtype=ml_dtypes.bfloat16)
    i = np.arange(P)
    u = i[:, None]
    q = i[None, :]
    IDf = np.eye(P, dtype=np.float32)
    MLO = np.where(u >= q, 0.0, NEG).astype(np.float32)
    MLOS = np.where((u >= q) | (u < SINK), 0.0, NEG).astype(np.float32)
    MHI = np.where(u <= q, 0.0, NEG).astype(np.float32)
    MSINK = np.where(np.repeat(u < SINK, P, axis=1), 0.0, NEG).astype(np.float32)
    t = np.arange(T, dtype=np.float64)
    inv = 1.0 / (10000.0 ** (np.arange(DH // 2, dtype=np.float64) * 2.0 / DH))
    ang = t[:, None] * inv[None, :]
    cos = np.cos(ang)
    sin = np.sin(ang)
    ROPC = np.concatenate([cos, cos], axis=1).astype(np.float32)
    ROPS = np.concatenate([-sin, sin], axis=1).astype(np.float32)
    Bm = np.zeros((P, P), dtype=np.float32)
    for p in range(P):
        for qq in range(P):
            if p % 16 == qq % 16:
                Bm[p, qq] = 1.0
    Cm = np.zeros((P, 8), dtype=np.float32)
    for p in range(P):
        Cm[p, p // 16] = 1.0
    return dict(IDf=IDf, IDb=bf(IDf), MLO=bf(MLO), MLOS=bf(MLOS), MHI=bf(MHI),
                MSINK=bf(MSINK), ROPC=ROPC, ROPS=ROPS, Bmat=Bm, Cmat=Cm,
                OnesRow=bf(np.ones((1, P), dtype=np.float32)),
                OnesRowF=np.ones((1, P), dtype=np.float32))


def kernel(L0_attn_x, L0_attn_Wq, L0_attn_bq, L0_attn_Wk, L0_attn_bk,
           L0_attn_Wv, L0_attn_bv, L0_attn_Wo, L0_attn_bo,
           L0_ffn_router_w, L0_ffn_router_b, L0_ffn_expert_w, L0_ffn_expert_b):
    nc = build_program()
    consts = _consts()
    x = np.asarray(L0_attn_x, dtype=np.float32)
    in_maps = []
    for c in range(NCORES):
        oneh = np.zeros((8, 1), dtype=np.float32)
        oneh[c, 0] = 1.0
        m = {
            "x": np.ascontiguousarray(x[c * BPC:(c + 1) * BPC].reshape(BPC * T, D)),
            "Wq": np.asarray(L0_attn_Wq, dtype=np.float32),
            "bq": np.asarray(L0_attn_bq, dtype=np.float32),
            "Wk": np.asarray(L0_attn_Wk, dtype=np.float32),
            "bk": np.asarray(L0_attn_bk, dtype=np.float32),
            "Wv": np.asarray(L0_attn_Wv, dtype=np.float32),
            "bv": np.asarray(L0_attn_bv, dtype=np.float32),
            "Wo": np.asarray(L0_attn_Wo, dtype=np.float32),
            "bo": np.asarray(L0_attn_bo, dtype=np.float32),
            "Wr": np.asarray(L0_ffn_router_w, dtype=np.float32),
            "br": np.asarray(L0_ffn_router_b, dtype=np.float32),
            "We": np.asarray(L0_ffn_expert_w, dtype=np.float32),
            "be": np.asarray(L0_ffn_expert_b, dtype=np.float32),
            "Oneh": oneh,
        }
        m.update(consts)
        in_maps.append(m)
    res = run_bass_kernel_spmd(nc, in_maps, core_ids=list(range(NCORES)))
    outs = [np.asarray(r["out"]).reshape(BPC, T, D) for r in res.results]
    return np.concatenate(outs, axis=0)


if __name__ == "__main__":
    build_program()
    print("built ok")


# version 10
# speedup vs baseline: 1.0097x; 1.0097x over previous
import os
import numpy as np
from contextlib import ExitStack

import concourse.bass as bass
import concourse.mybir as mb
from concourse.bass_utils import run_bass_kernel_spmd
from concourse.alu_op_type import AluOpType as A

F32 = mb.dt.float32
BF16 = mb.dt.bfloat16
AXX = mb.AxisListType.X
AF = mb.ActivationFunctionType

P = 128
D = 2048
T = 2048
NCORES = 8
BPC = 2
NT = BPC * T // P
TPS = T // P
KC = D // P
NH, NG, DH = 16, 4, 128
WIN, SINK = 512, 4
NE = 16
BS = 16 * T
TOPK = BS // NE
SCALE = DH ** -0.5
EPS = 1e-5
DQKV = D + 1024
NEG = -1.0e30
RING = 5
NBIS = 40

ENGS = ("T", "V", "S", "G", "Y")


class Sched:

    def __init__(self, nc, es):
        self.nc = nc
        self.es = es
        self.ops = {e: [] for e in ENGS}
        self.sems = {}
        self.cnt = {}
        for e in ENGS:
            self.mksem("tk" + e)
        self.bufs = {}
        self.sat = {e: {} for e in ENGS}
        self.pending = {e: [] for e in ENGS}

    def mksem(self, name):
        self.sems[name] = self.es.enter_context(self.nc.semaphore(name))
        self.cnt[name] = 0
        return name

    def emit(self, eng, fn, reads=(), writes=(), sem=None, amt=1, tick=True,
             extra_waits=()):
        need = {}
        for s, c in extra_waits:
            if need.get(s, 0) < c:
                need[s] = c

        def add(mark):
            if mark is not None and need.get(mark[0], 0) < mark[1]:
                need[mark[0]] = mark[1]

        for b in reads:
            st = self.bufs.get(b)
            if st:
                add(st.get("w"))
        for b in writes:
            st = self.bufs.get(b)
            if st:
                add(st.get("w"))
                for m in st.get("r", ()):
                    add(m)
        sat = self.sat[eng]
        waits = [(s, c) for s, c in need.items() if sat.get(s, 0) < c]
        for s, c in waits:
            sat[s] = c
        if sem is None and tick:
            sem = "tk" + eng
        if sem is not None:
            self.cnt[sem] += amt
            mark = (sem, self.cnt[sem])
        else:
            mark = None
        self.ops[eng].append((waits, fn, sem, amt))
        for b in writes:
            self.bufs[b] = {"w": mark, "r": []}
        if mark is None:
            self.pending[eng].extend(reads)
            self.pending[eng].extend(writes)
        else:
            for b in list(reads) + self.pending[eng]:
                st = self.bufs.setdefault(b, {"w": None, "r": []})
                st["r"].append(mark)
                if st.get("w") is None and b in self.pending[eng]:
                    pass
            for b in self.pending[eng]:
                st = self.bufs.setdefault(b, {"w": None, "r": []})
            self.pending[eng] = []
        return mark

    def wait_all(self, eng, marks):
        sat = self.sat[eng]
        ws = []
        for s, c in marks:
            if sat.get(s, 0) < c:
                ws.append((s, c))
                sat[s] = c
        if ws:
            self.ops[eng].append((ws, None, None, 0))

    def realize(self, block):
        def runner(ename):
            def run(eng):
                for waits, fn, sem, amt in self.ops[ename]:
                    for s, c in waits:
                        eng.wait_ge(self.sems[s], c)
                    if fn is not None:
                        inst = fn(eng)
                        if sem is not None:
                            inst.then_inc(self.sems[sem], amt)
            return run

        block.tensor(runner("T"))
        block.vector(runner("V"))
        block.scalar(runner("S"))
        block.gpsimd(runner("G"))
        block.sync(runner("Y"))


def build_program():
    nc = bass.Bass("TRN2", target_bir_lowering=False)
    es = ExitStack()

    def din(name, shape, dt):
        return nc.dram_tensor(name, shape, dt, kind="ExternalInput").ap()

    dx = din("x", [BPC * T, D], F32)
    dWq = din("Wq", [D, D], F32)
    dbq = din("bq", [D], F32)
    dWk = din("Wk", [D, 512], F32)
    dbk = din("bk", [512], F32)
    dWv = din("Wv", [D, 512], F32)
    dbv = din("bv", [512], F32)
    dWo = din("Wo", [D, D], F32)
    dbo = din("bo", [D], F32)
    dWr = din("Wr", [D, NE], F32)
    dbr = din("br", [NE], F32)
    dWe = din("We", [D, D], F32)
    dbe = din("be", [D], F32)
    dIDf = din("IDf", [P, P], F32)
    dIDb = din("IDb", [P, P], BF16)
    dMLO = din("MLO", [P, P], BF16)
    dMLOS = din("MLOS", [P, P], BF16)
    dMHI = din("MHI", [P, P], BF16)
    dMSINK = din("MSINK", [P, P], BF16)
    dROPC = din("ROPC", [T, DH], F32)
    dROPS = din("ROPS", [T, DH], F32)
    dBmat = din("Bmat", [P, P], F32)
    dCmat = din("Cmat", [P, 8], F32)
    dOneh = din("Oneh", [8, 1], F32)
    dOnes = din("OnesRow", [1, P], BF16)
    dOnesF = din("OnesRowF", [1, P], F32)
    dOut = nc.dram_tensor("out", [BPC * T, D], F32, kind="ExternalOutput").ap()

    if int(os.environ.get("KNOSCR", "0")):
        class _FakeAp:
            def __getitem__(self, *a):
                return None
            def rearrange(self, *a, **k):
                return None
            def ap(self):
                return None
            def opt(self):
                return None
        _f = _FakeAp()
        sWqkvH = sWqkvL = sWoH = sWoL = sWeH = sWeL = _f
        sBrows = sAtT = sFfn = sPP = sSin = sSout = _f
        hSin = hSout = _f
    else:
        sWqkvH = nc.dram_tensor("sWqkvH", [D, DQKV], BF16).ap()
        sWqkvL = nc.dram_tensor("sWqkvL", [D, DQKV], BF16).ap()
        sWoH = nc.dram_tensor("sWoH", [D, D], BF16).ap()
        sWoL = nc.dram_tensor("sWoL", [D, D], BF16).ap()
        sWeH = nc.dram_tensor("sWeH", [D, D], BF16).ap()
        sWeL = nc.dram_tensor("sWeL", [D, D], BF16).ap()
        sBrows = nc.dram_tensor("sBrows", [6, DQKV], BF16).ap()
        sAtT = nc.dram_tensor("sAtT", [2, KC, 4, P, P], BF16).ap()
        KDBG = int(os.environ.get("KDBG", "0"))
        kind = dict(kind="ExternalOutput") if KDBG else {}
        sFfn = nc.dram_tensor("sFfn", [NT * P, D], F32, **kind).ap()
        sPP = nc.dram_tensor("sPP", [NT * P, D], F32, **kind).ap()
        hSin = nc.dram_tensor("sSin", [NE, BPC * T], F32, **kind)
        hSout = nc.dram_tensor("sSout", [NCORES, NE, BPC * T], F32)
        sSin = hSin.ap()
        sSout = hSout.ap()

    if int(os.environ.get("KBARE", "0")):
        with nc.sbuf_tensor("bx", [64, 64], F32) as bx, \
             nc.semaphore("bd") as bd, nc.semaphore("bg") as bg, nc.Block() as block:
            @block.sync
            def _(sync):
                sync.dma_start(out=bx[:], in_=dx[0:64, 0:64]).then_inc(bd, 16)
            @block.gpsimd
            def _(g):
                g.wait_ge(bd, 16)
                g.dma_start(out=dOut[0:64, 0:64], in_=bx[:]).then_inc(bg, 16)
                g.wait_ge(bg, 16)
        return nc
    KSB = int(os.environ.get("KSB", "0"))
    def sb(name, shape, dt):
        if KSB:
            shape = [1, 4]
        return es.enter_context(nc.sbuf_tensor("b_" + name, shape, dt))

    IDf = sb("IDf", [P, P], F32)
    IDb = sb("IDb", [P, P], BF16)
    MLO = sb("MLO", [P, P], BF16)
    MLOS = sb("MLOS", [P, P], BF16)
    MHI = sb("MHI", [P, P], BF16)
    MSINK = sb("MSINK", [P, P], BF16)
    Bmat = sb("Bmat", [P, P], F32)
    Cmat = sb("Cmat", [P, 8], F32)
    Oneh = sb("Oneh", [8, 1], F32)
    OnesR = sb("OnesR", [1, P], BF16)
    OnesF = sb("OnesF", [1, P], F32)
    RopC = sb("RopC", [P, 4 * DH], F32)
    RopS = sb("RopS", [P, 4 * DH], F32)
    WrH = sb("WrH", [P, KC * NE], BF16)
    WrL = sb("WrL", [P, KC * NE], BF16)
    BrowR = sb("BrowR", [1, 2 * NE], BF16)
    Brow = sb("Brow", [1, 2 * DQKV], BF16)
    WB = [sb(f"WB{i}", [P, 2 * DQKV], BF16) for i in range(2)]
    WO = [sb(f"WOs{i}", [P, 1024], BF16) for i in range(2)]
    WE = [sb(f"WEs{i}", [P, 1024], BF16) for i in range(2)]
    AT = [sb(f"ATs{i}", [P, 1024], BF16) for i in range(2)]
    XT = [sb(f"XTs{i}", [P, D], F32) for i in range(2)]
    SCR = sb("SCR", [P, 2 * D], F32)
    XnH = sb("XnH", [P, D], BF16)
    XnL = sb("XnL", [P, D], BF16)
    XnTH = sb("XnTH", [P, 4 * D], BF16)
    XnTL = sb("XnTL", [P, 4 * D], BF16)
    QB = sb("QB", [P, 512], F32)
    QROT = sb("QROT", [P, 512], F32)
    QT = sb("QT", [P, NH * P], F32)
    KT = sb("KT", [P, NG * RING * P], F32)
    KTS = sb("KTS", [P, NG * P], F32)
    VA = sb("VA", [P, NG * RING * 132], F32)
    VAS = sb("VAS", [P, NG * 132], F32)
    EB = sb("EB", [P, 4 * 6 * P], F32)
    ATH = sb("ATH", [P, D], BF16)
    ATL = sb("ATL", [P, D], BF16)
    ATT = sb("ATT", [P, D], BF16)
    STAT = sb("STAT", [P, 40], F32)
    QN2 = sb("QN2", [P, NH], F32)
    KN2 = sb("KN2", [P, NG], F32)
    CB = sb("CB", [P, NH], F32)
    RCP = sb("RCP", [P, 2], F32)
    STILE = sb("STILE", [P, NE], F32)
    SBLK = sb("SBLK", [P, 4096], F32)
    LO = sb("LO", [P, 1], F32)
    HI = sb("HI", [P, 1], F32)
    MID = sb("MID", [P, 1], F32)
    WCOL = sb("WCOL", [P, NT], F32)

    KPS = int(os.environ.get("KPS", "0"))
    if KPS:
        ps6 = es.enter_context(nc.psum_tensor("ps6", [P, 512], F32))
        psA = ps6
        psC = es.enter_context(nc.psum_tensor("psC", [P, 512], BF16))
    else:
        ps6 = es.enter_context(nc.psum_tensor("ps6", [P, 3072], F32))
        psA = es.enter_context(nc.psum_tensor("psA", [P, 512], F32))
        psC = es.enter_context(nc.psum_tensor("psC", [P, 1024], BF16))

    sch = Sched(nc, es)
    for s in (["dC", "dW0", "dW1", "gWl0", "gWl1", "dX0", "dX1", "dR", "dB"]
              + [f"d{p}{i}" for p in ("A", "Al", "O", "Ol", "E", "El") for i in range(4)]
              + ["gS", "gA", "gF", "gP", "gO", "gW", "cc"]):
        sch.mksem(s)

    def dma(eng, sem, out, in_, reads, writes):
        pw = [(sem, sch.cnt[sem])] if sch.cnt[sem] > 0 else []
        sch.emit(eng, lambda e, o=out, i=in_: e.dma_start(out=o, in_=i),
                 reads=reads, writes=writes, sem=sem, amt=16, tick=False,
                 extra_waits=pw)

    def mm(out, lhsT, rhs, start, stop, reads, writes, tick=None):
        sch.emit("T", lambda e, o=out, l=lhsT, r=rhs, st=start, sp=stop:
                 e.matmul(o, l, r, start=st, stop=sp),
                 reads=reads, writes=writes, tick=(stop if tick is None else tick))

    def transp(out, in_, ident, start, stop, reads, writes):
        sch.emit("T", lambda e, o=out, i=in_, d=ident, st=start, sp=stop:
                 e.matmul(o, i, d, start=st, stop=sp, is_transpose=True),
                 reads=reads, writes=writes, tick=stop)

    def V(fn, reads=(), writes=()):
        sch.emit("V", fn, reads=reads, writes=writes)

    def S_(fn, reads=(), writes=()):
        sch.emit("S", fn, reads=reads, writes=writes)

    def apf(t, off, dims):
        base = t[:]
        return bass.AP(base.tensor, off, [list(base.ap[0])] + [list(d) for d in dims])

    KNC = int(os.environ.get("KNC", "99"))
    for buf, src, nm in [(IDf, dIDf, "IDf"), (IDb, dIDb, "IDb"), (MLO, dMLO, "MLO"),
                         (MLOS, dMLOS, "MLOS"), (MHI, dMHI, "MHI"), (MSINK, dMSINK, "MSINK"),
                         (Bmat, dBmat, "Bmat"), (Cmat, dCmat, "Cmat"), (Oneh, dOneh, "Oneh"),
                         (OnesR, dOnes, "OnesR"), (OnesF, dOnesF, "OnesF")][:KNC]:
        dma("Y", "dC", buf[:], src, [], [nm])

    KST = int(os.environ.get("KST", "9"))
    def split_weight(src_ap, dst_h, dst_l, ncols, dname):
        if KST < 1:
            return
        for k in range(KC):
            xb = XT[k % 2]
            xn_ = f"xt{k % 2}"
            dma("Y", f"dX{k % 2}", xb[:, 0:ncols], src_ap[k * P:(k + 1) * P, :], [xn_], [xn_])
            V(lambda e, xb=xb, n=ncols: e.tensor_copy(XnH[:, 0:n], xb[:, 0:n]),
              reads=[xn_], writes=["xnh"])
            V(lambda e, xb=xb, n=ncols: e.tensor_tensor(
                out=XnL[:, 0:n], in0=xb[:, 0:n], in1=XnH[:, 0:n], op=A.subtract),
              reads=[xn_, "xnh"], writes=["xnl"])
            dma("G", "gW", dst_h[k * P:(k + 1) * P, :], XnH[:, 0:ncols], ["xnh"], [dname])
            dma("G", "gW", dst_l[k * P:(k + 1) * P, :], XnL[:, 0:ncols], ["xnl"], [dname])

    split_weight(dWq, sWqkvH[:, 0:D], sWqkvL[:, 0:D], D, "sWqkv")
    split_weight(dWk, sWqkvH[:, D:D + 512], sWqkvL[:, D:D + 512], 512, "sWqkv")
    split_weight(dWv, sWqkvH[:, D + 512:DQKV], sWqkvL[:, D + 512:DQKV], 512, "sWqkv")
    split_weight(dWo, sWoH, sWoL, D, "sWo")
    split_weight(dWe, sWeH, sWeL, D, "sWe")

    for k in (range(KC) if KST >= 2 else []):
        dma("Y", "dR", SCR[:, 0:NE], dWr[k * P:(k + 1) * P, :], ["scr"], ["scr"])
        V(lambda e, k=k: e.tensor_copy(WrH[:, k * NE:(k + 1) * NE], SCR[:, 0:NE]),
          reads=["scr"], writes=["WrH"])
        V(lambda e, k=k: e.tensor_tensor(
            out=WrL[:, k * NE:(k + 1) * NE], in0=SCR[:, 0:NE],
            in1=WrH[:, k * NE:(k + 1) * NE], op=A.subtract),
          reads=["scr", "WrH"], writes=["WrL"])

    def split_bias(parts, row_h, row_l):
        if KST < 3:
            return
        for ap1, ncols, off in parts:
            dma("Y", "dB", SCR[0:1, off:off + ncols],
                ap1.rearrange("(a n) -> a n", a=1), ["scr"], ["scr"])
        V(lambda e: e.tensor_copy(Brow[0:1, 0:DQKV], SCR[0:1, 0:DQKV]),
          reads=["scr"], writes=["browh"])
        V(lambda e: e.tensor_tensor(
            out=Brow[0:1, DQKV:2 * DQKV], in0=SCR[0:1, 0:DQKV],
            in1=Brow[0:1, 0:DQKV], op=A.subtract),
          reads=["scr", "browh"], writes=["browl"])
        dma("G", "gW", row_h, Brow[0:1, 0:DQKV], ["browh"], ["sBrows"])
        dma("G", "gW", row_l, Brow[0:1, DQKV:2 * DQKV], ["browl"], ["sBrows"])

    split_bias([(dbq, D, 0), (dbk, 512, D), (dbv, 512, D + 512)], sBrows[0:1, :], sBrows[1:2, :])
    split_bias([(dbo, D, 0)], sBrows[2:3, :], sBrows[3:4, :])
    split_bias([(dbe, D, 0)], sBrows[4:5, :], sBrows[5:6, :])
    if KST >= 3:
        dma("Y", "dB", SCR[0:1, 0:NE], dbr.rearrange("(a n) -> a n", a=1), ["scr"], ["scr"])
        V(lambda e: e.tensor_copy(BrowR[0:1, 0:NE], SCR[0:1, 0:NE]),
          reads=["scr"], writes=["browr"])
        V(lambda e: e.tensor_tensor(
            out=BrowR[0:1, NE:2 * NE], in0=SCR[0:1, 0:NE],
            in1=BrowR[0:1, 0:NE], op=A.subtract),
          reads=["scr", "browr"], writes=["browr"])

    if KST < 4:
        with nc.Block() as block:
            sch.realize(block)
        es.close()
        return nc
    V(lambda e: e.memset(VA[:], 0.0), reads=[], writes=["va0", "va1", "va2", "va3", "va4", "vaone"])
    V(lambda e: e.memset(
        VA[:].rearrange("p (gs c) -> p gs c", c=132)[:, :, 131:132], 1.0),
      reads=["vaone"], writes=["vaone"])
    V(lambda e: e.memset(VAS[:], 0.0), reads=[], writes=["VAS"])
    V(lambda e: e.memset(
        VAS[:].rearrange("p (g c) -> p g c", c=132)[:, :, 131:132], 1.0),
      reads=["VAS"], writes=["VAS"])
    V(lambda e: e.memset(KTS[:], 0.0), reads=[], writes=["KTS"])

    KP = int(os.environ.get("KP", "9"))
    NSEQ = 0 if KP < 2 else (1 if KP == 2 else BPC)
    for seq in range(NSEQ):
        tok0 = seq * T
        for sbi in range(TPS // 4):
            sb4 = [sbi * 4 + j for j in range(4)]
            dma("Y", "dR", RopC[:].rearrange("p (j c) -> p j c", j=4),
                dROPC.rearrange("(j p) c -> p j c", p=P)[:, sbi * 4:(sbi + 1) * 4, :],
                ["ropC"], ["ropC"])
            dma("Y", "dR", RopS[:].rearrange("p (j c) -> p j c", j=4),
                dROPS.rearrange("(j p) c -> p j c", p=P)[:, sbi * 4:(sbi + 1) * 4, :],
                ["ropS"], ["ropS"])

            dma("Y", "dB", Brow[0:1, 0:DQKV], sBrows[0:1, :], ["browh", "sBrows"], ["browh"])
            dma("Y", "dB", Brow[0:1, DQKV:2 * DQKV], sBrows[1:2, :],
                ["browl", "sBrows"], ["browl"])
            for jj, tp in enumerate(sb4):
                tt = seq * TPS + tp
                xslot = tt % 2
                xn_ = f"xt{xslot}"
                dma("Y", f"dX{xslot}", XT[xslot][:],
                    dx[tok0 + tp * P: tok0 + (tp + 1) * P, :], [xn_], [xn_])
                st = STAT
                V(lambda e, xb=XT[xslot]: e.reduce_sum(out=st[:, 0:1], in_=xb[:], axis=AXX),
                  reads=[xn_], writes=["st"])
                S_(lambda e, xb=XT[xslot]: e.activation(
                    out=SCR[:, 0:D], in_=xb[:], func=AF.Square, scale=1.0,
                    accum_out=st[:, 1:2]),
                   reads=[xn_], writes=["scr", "st"])
                V(lambda e: e.tensor_scalar(out=st[:, 2:3], in0=st[:, 0:1],
                                            scalar1=1.0 / D, scalar2=None, op0=A.mult),
                  reads=["st"], writes=["st"])
                V(lambda e: e.tensor_tensor(out=st[:, 3:4], in0=st[:, 2:3],
                                            in1=st[:, 2:3], op=A.mult),
                  reads=["st"], writes=["st"])
                V(lambda e: e.tensor_scalar(out=st[:, 4:5], in0=st[:, 1:2],
                                            scalar1=1.0 / D, scalar2=None, op0=A.mult),
                  reads=["st"], writes=["st"])
                V(lambda e: e.tensor_tensor(out=st[:, 5:6], in0=st[:, 4:5],
                                            in1=st[:, 3:4], op=A.subtract),
                  reads=["st"], writes=["st"])
                V(lambda e: e.tensor_scalar(out=st[:, 6:7], in0=st[:, 5:6],
                                            scalar1=EPS, scalar2=None, op0=A.add),
                  reads=["st"], writes=["st"])
                V(lambda e: e.memset(st[:, 7:8], 1.0), reads=[], writes=["sty"])
                for _ in range(6):
                    V(lambda e: e.tensor_tensor(out=st[:, 8:9], in0=st[:, 7:8],
                                                in1=st[:, 7:8], op=A.mult),
                      reads=["sty"], writes=["st8"])
                    V(lambda e: e.tensor_tensor(out=st[:, 8:9], in0=st[:, 8:9],
                                                in1=st[:, 6:7], op=A.mult),
                      reads=["st8", "st"], writes=["st8"])
                    V(lambda e: e.tensor_scalar(out=st[:, 8:9], in0=st[:, 8:9],
                                                scalar1=-0.5, scalar2=1.5,
                                                op0=A.mult, op1=A.add),
                      reads=["st8"], writes=["st8"])
                    V(lambda e: e.tensor_tensor(out=st[:, 7:8], in0=st[:, 7:8],
                                                in1=st[:, 8:9], op=A.mult),
                      reads=["sty", "st8"], writes=["sty"])
                V(lambda e: e.tensor_tensor(out=st[:, 9:10], in0=st[:, 2:3],
                                            in1=st[:, 7:8], op=A.mult),
                  reads=["st", "sty"], writes=["st9"])
                V(lambda e: e.tensor_scalar(out=st[:, 9:10], in0=st[:, 9:10],
                                            scalar1=-1.0, scalar2=None, op0=A.mult),
                  reads=["st9"], writes=["st9"])
                V(lambda e, xb=XT[xslot]: e.tensor_scalar(
                    out=SCR[:, 0:D], in0=xb[:], scalar1=st[:, 7:8], scalar2=st[:, 9:10],
                    op0=A.mult, op1=A.add),
                  reads=[xn_, "sty", "st9"], writes=["scr"])
                S_(lambda e: e.activation(out=XnH[:], in_=SCR[:, 0:D], func=AF.Copy, scale=1.0),
                   reads=["scr"], writes=["xnh"])
                V(lambda e: e.tensor_tensor(out=XnL[:], in0=SCR[:, 0:D], in1=XnH[:], op=A.subtract),
                  reads=["scr", "xnh"], writes=["xnl"])
                for src, dst, rbuf, wbuf in ((XnH, XnTH, "xnh", "xnth"),
                                             (XnL, XnTL, "xnl", "xntl")):
                    for g8 in range(2):
                        for b8 in range(8):
                            k = g8 * 8 + b8
                            transp(psC[:, b8 * P:(b8 + 1) * P], src[:, k * P:(k + 1) * P],
                                   IDb[:], start=(b8 == 0), stop=(b8 == 7),
                                   reads=[rbuf, "IDb"], writes=["psC"])
                        V(lambda e, g8=g8, jj=jj, dst=dst: e.tensor_copy(
                            dst[:, jj * D + g8 * 1024:jj * D + (g8 + 1) * 1024], psC[:]),
                          reads=["psC"], writes=[wbuf])

                for k in range(KC):
                    ws = k % 2
                    wh_, wl_ = f"wb{ws}h", f"wb{ws}l"
                    dma("S", f"dW{ws}", WB[ws][:, 0:DQKV],
                        sWqkvH[k * P:(k + 1) * P, :], [wh_, "sWqkv"], [wh_])
                    dma("G", f"gWl{ws}", WB[ws][:, DQKV:2 * DQKV],
                        sWqkvL[k * P:(k + 1) * P, :], [wl_, "sWqkv"], [wl_])
                    xh = XnTH[:, jj * D + k * P:jj * D + (k + 1) * P]
                    xl = XnTL[:, jj * D + k * P:jj * D + (k + 1) * P]
                    st0 = (k == 0)
                    for dc in range(6):
                        wh = WB[ws][:, dc * 512:(dc + 1) * 512]
                        wl = WB[ws][:, DQKV + dc * 512:DQKV + (dc + 1) * 512]
                        pn = f"ps6_{dc}"
                        mm(ps6[:, dc * 512:(dc + 1) * 512], xh, wh, st0, False,
                           ["xnth", wh_], [pn])
                        mm(ps6[:, dc * 512:(dc + 1) * 512], xh, wl, False, False,
                           ["xnth", wl_], [pn])
                    mm(psA[:, 0:NE], xh, WrH[:, k * NE:(k + 1) * NE], st0, False,
                       ["xnth", "WrH"], ["psA"])
                    mm(psA[:, 0:NE], xh, WrL[:, k * NE:(k + 1) * NE], False, False,
                       ["xnth", "WrL"], ["psA"])
                    for dc in range(6):
                        wh = WB[ws][:, dc * 512:(dc + 1) * 512]
                        pn = f"ps6_{dc}"
                        mm(ps6[:, dc * 512:(dc + 1) * 512], xl, wh, False, False,
                           ["xntl", wh_], [pn])
                    mm(psA[:, 0:NE], xl, WrH[:, k * NE:(k + 1) * NE], False, False,
                       ["xntl", "WrH"], ["psA"], tick=True)
                for dc in range(6):
                    pn = f"ps6_{dc}"
                    mm(ps6[:, dc * 512:(dc + 1) * 512], OnesR[:],
                       Brow[0:1, dc * 512:(dc + 1) * 512], False, False,
                       ["browh", "OnesR"], [pn])
                    mm(ps6[:, dc * 512:(dc + 1) * 512], OnesR[:],
                       Brow[0:1, DQKV + dc * 512:DQKV + (dc + 1) * 512], False, True,
                       ["browl", "OnesR"], [pn])
                mm(psA[:, 0:NE], OnesR[:], BrowR[0:1, 0:NE], False, False,
                   ["browr", "OnesR"], ["psA"])
                mm(psA[:, 0:NE], OnesR[:], BrowR[0:1, NE:2 * NE], False, True,
                   ["browr", "OnesR"], ["psA"])

                V(lambda e: e.reduce_max(out=st[:, 10:11], in_=psA[:, 0:NE],
                                         axis=AXX, negate=True),
                  reads=["psA"], writes=["st10"])
                S_(lambda e: e.activation(out=STILE[:], in_=psA[:, 0:NE], func=AF.Exp,
                                          bias=st[:, 10:11], scale=1.0,
                                          accum_out=st[:, 11:12]),
                   reads=["psA", "st10"], writes=["stile", "st11"])
                V(lambda e: e.reciprocal(out=st[:, 12:13], in_=st[:, 11:12]),
                  reads=["st11"], writes=["st12"])
                V(lambda e: e.tensor_scalar(out=STILE[:], in0=STILE[:],
                                            scalar1=st[:, 12:13], scalar2=None, op0=A.mult),
                  reads=["stile", "st12"], writes=["stile"])
                transp(psA[0:NE, 0:P], STILE[:], IDf[:], True, True,
                       ["stile", "IDf"], ["psA"])
                V(lambda e: e.tensor_copy(SCR[0:NE, 0:P], psA[0:NE, 0:P]),
                  reads=["psA"], writes=["scr"])
                dma("G", "gS", sSin[:, tt * P:(tt + 1) * P], SCR[0:NE, 0:P],
                    ["scr"], ["dSin"])

                slot = tp % RING
                co_c = jj * DH
                for ch in range(4):
                    pq = ps6[:, ch * 512:(ch + 1) * 512]
                    pn = f"ps6_{ch}"
                    cb_ap = apf(RopC, co_c, [[0, 4], [1, DH]])
                    sb_lo = apf(RopS, co_c, [[0, 4], [1, 64]])
                    sb_hi = apf(RopS, co_c + 64, [[0, 4], [1, 64]])
                    V(lambda e, pq=pq, cb_ap=cb_ap: e.tensor_tensor(
                        out=QROT[:].rearrange("p (h c) -> p h c", h=4),
                        in0=pq.rearrange("p (h c) -> p h c", h=4),
                        in1=cb_ap, op=A.mult),
                      reads=[pn, "ropC"], writes=["qrot"])
                    V(lambda e, pq=pq, sb_lo=sb_lo: e.tensor_tensor(
                        out=QB[:].rearrange("p (h c) -> p h c", h=4)[:, :, 0:64],
                        in0=pq.rearrange("p (h c) -> p h c", h=4)[:, :, 64:128],
                        in1=sb_lo, op=A.mult),
                      reads=[pn, "ropS"], writes=["qb"])
                    V(lambda e, pq=pq, sb_hi=sb_hi: e.tensor_tensor(
                        out=QB[:].rearrange("p (h c) -> p h c", h=4)[:, :, 64:128],
                        in0=pq.rearrange("p (h c) -> p h c", h=4)[:, :, 0:64],
                        in1=sb_hi, op=A.mult),
                      reads=[pn, "ropS"], writes=["qb"])
                    V(lambda e: e.tensor_tensor(out=QROT[:], in0=QROT[:], in1=QB[:], op=A.add),
                      reads=["qrot", "qb"], writes=["qrot"])
                    V(lambda e: e.tensor_tensor(out=QB[:], in0=QROT[:], in1=QROT[:], op=A.mult),
                      reads=["qrot"], writes=["qb"])
                    V(lambda e, ch=ch: e.tensor_reduce(
                        out=QN2[:, ch * 4:(ch + 1) * 4],
                        in_=QB[:].rearrange("p (h c) -> p h c", h=4), axis=AXX, op=A.add),
                      reads=["qb"], writes=["qn2"])
                    for b4 in range(4):
                        transp(psA[:, b4 * P:(b4 + 1) * P], QROT[:, b4 * P:(b4 + 1) * P],
                               IDf[:], start=(b4 == 0), stop=(b4 == 3),
                               reads=["qrot", "IDf"], writes=["psA"])
                    V(lambda e, ch=ch: e.tensor_copy(QT[:, ch * 512:(ch + 1) * 512], psA[:]),
                      reads=["psA"], writes=["qt"])
                pk = ps6[:, 2048:2560]
                cb_ap = apf(RopC, co_c, [[0, 4], [1, DH]])
                sb_lo = apf(RopS, co_c, [[0, 4], [1, 64]])
                sb_hi = apf(RopS, co_c + 64, [[0, 4], [1, 64]])
                V(lambda e, cb_ap=cb_ap: e.tensor_tensor(
                    out=QROT[:].rearrange("p (h c) -> p h c", h=4),
                    in0=pk.rearrange("p (h c) -> p h c", h=4),
                    in1=cb_ap, op=A.mult),
                  reads=["ps6_4", "ropC"], writes=["qrot"])
                V(lambda e, sb_lo=sb_lo: e.tensor_tensor(
                    out=QB[:].rearrange("p (h c) -> p h c", h=4)[:, :, 0:64],
                    in0=pk.rearrange("p (h c) -> p h c", h=4)[:, :, 64:128],
                    in1=sb_lo, op=A.mult),
                  reads=["ps6_4", "ropS"], writes=["qb"])
                V(lambda e, sb_hi=sb_hi: e.tensor_tensor(
                    out=QB[:].rearrange("p (h c) -> p h c", h=4)[:, :, 64:128],
                    in0=pk.rearrange("p (h c) -> p h c", h=4)[:, :, 0:64],
                    in1=sb_hi, op=A.mult),
                  reads=["ps6_4", "ropS"], writes=["qb"])
                V(lambda e: e.tensor_tensor(out=QROT[:], in0=QROT[:], in1=QB[:], op=A.add),
                  reads=["qrot", "qb"], writes=["qrot"])
                V(lambda e: e.tensor_tensor(out=QB[:], in0=QROT[:], in1=QROT[:], op=A.mult),
                  reads=["qrot"], writes=["qb"])
                V(lambda e: e.tensor_reduce(
                    out=st[:, 16:20], in_=QB[:].rearrange("p (g c) -> p g c", g=4),
                    axis=AXX, op=A.add),
                  reads=["qb"], writes=["st16"])
                if tp == 0:
                    V(lambda e: e.tensor_copy(KN2[:], st[:, 16:20]),
                      reads=["st16"], writes=["kn2"])
                else:
                    V(lambda e: e.tensor_tensor(out=KN2[:], in0=KN2[:],
                                                in1=st[:, 16:20], op=A.max),
                      reads=["kn2", "st16"], writes=["kn2"])
                for g in range(4):
                    transp(psA[:, g * P:(g + 1) * P], QROT[:, g * P:(g + 1) * P],
                           IDf[:], start=(g == 0), stop=(g == 3),
                           reads=["qrot", "IDf"], writes=["psA"])
                V(lambda e, slot=slot: e.tensor_copy(
                    KT[:].rearrange("p (g s c) -> p g s c", g=NG, s=RING)[:, :, slot, :],
                    psA[:].rearrange("p (g c) -> p g c", g=4)),
                  reads=["psA"], writes=[f"kt{slot}"])
                if tp == 0:
                    V(lambda e: e.tensor_copy(
                        KTS[:].rearrange("p (g c) -> p g c", g=NG)[:, :, 0:4],
                        psA[:].rearrange("p (g c) -> p g c", g=4)[:, :, 0:4]),
                      reads=["psA", "KTS"], writes=["KTS"])
                V(lambda e, slot=slot: e.tensor_copy(
                    VA[:].rearrange("p (g s c) -> p g s c", g=NG, s=RING)[:, :, slot, 0:128],
                    ps6[:, 2560:3072].rearrange("p (g c) -> p g c", g=4)),
                  reads=["ps6_5", "vaone"], writes=[f"va{slot}"])
                if tp == 0:
                    V(lambda e: e.tensor_copy(
                        VAS[0:4, :].rearrange("p (g c) -> p g c", g=NG)[:, :, 0:128],
                        ps6[0:4, 2560:3072].rearrange("p (g c) -> p g c", g=4)),
                      reads=["ps6_5", "VAS"], writes=["VAS"])

                if tp >= 5:
                    slots = [("sink", None, MSINK, "MSINK")]
                    for c in range(tp - 4, tp + 1):
                        msk, mn = (MLO, "MLO") if c == tp - 4 else \
                                  ((MHI, "MHI") if c == tp else (None, None))
                        slots.append(("ring", c, msk, mn))
                else:
                    slots = []
                    for c in range(0, tp + 1):
                        if c == tp:
                            msk, mn = MHI, "MHI"
                        elif tp == 4 and c == 0:
                            msk, mn = MLOS, "MLOS"
                        else:
                            msk, mn = None, None
                        slots.append(("ring", c, msk, mn))
                nsl = len(slots)

                transp(psA[0:NH, 0:P], QN2[:], IDf[:], True, True, ["qn2", "IDf"], ["psA"])
                V(lambda e: e.reduce_max(out=st[0:NH, 20:21], in_=psA[0:NH, 0:P], axis=AXX),
                  reads=["psA"], writes=["st20"])
                transp(psA[0:NG, 0:P], KN2[:], IDf[:], True, True, ["kn2", "IDf"], ["psA"])
                V(lambda e: e.reduce_max(out=st[0:NG, 21:22], in_=psA[0:NG, 0:P], axis=AXX),
                  reads=["psA"], writes=["st21"])
                transp(psA[0:1, 0:NH], st[0:NH, 20:21], IDf[0:NH, 0:NH], True, True,
                       ["st20", "IDf"], ["psA"])
                V(lambda e: e.tensor_copy(SCR[0:1, 0:NH], psA[0:1, 0:NH]),
                  reads=["psA"], writes=["scr"])
                transp(psA[0:1, 0:NG], st[0:NG, 21:22], IDf[0:NG, 0:NG], True, True,
                       ["st21", "IDf"], ["psA"])
                V(lambda e: e.tensor_copy(SCR[0:1, 16:16 + NG], psA[0:1, 0:NG]),
                  reads=["psA", "scr"], writes=["scr"])
                V(lambda e: e.tensor_reduce(
                    out=SCR[0:1, 24:28], in_=SCR[0:1, 0:16].rearrange("p (g h) -> p g h", g=4),
                    axis=AXX, op=A.max),
                  reads=["scr"], writes=["scr"])
                V(lambda e: e.tensor_tensor(
                    out=SCR[0:1, 32:36], in0=SCR[0:1, 24:28],
                    in1=SCR[0:1, 16:20], op=A.add),
                  reads=["scr"], writes=["scr"])
                V(lambda e: e.tensor_scalar(out=SCR[0:1, 32:36], in0=SCR[0:1, 32:36],
                                            scalar1=-0.5 * SCALE, scalar2=None, op0=A.mult),
                  reads=["scr"], writes=["scr"])
                mm(psA[:, 0:NG], OnesF[:], SCR[0:1, 32:36], True, True,
                   ["scr", "OnesF"], ["psA"])
                V(lambda e: e.tensor_copy(CB[:, 0:NG], psA[:, 0:NG]), reads=["psA"], writes=["cb"])

                for g in range(NG):
                    order = ([4, 5, 0, 1, 2, 3] if nsl == 6 else list(range(nsl)))
                    for si in order:
                        kind, c, msk, mn = slots[si]
                        if kind == "sink":
                            lhs = KTS[:].rearrange("p (g2 c2) -> p g2 c2", g2=NG)[:, g, :]
                            lrd = ["KTS"]
                        else:
                            lhs = KT[:].rearrange("p (g2 s c2) -> p g2 s c2",
                                                  g2=NG, s=RING)[:, g, c % RING, :]
                            lrd = [f"kt{c % RING}"]
                        pn = f"ps6_{si}"
                        for h in range(4):
                            hh = g * 4 + h
                            mm(ps6[:, si * 512 + h * P: si * 512 + (h + 1) * P],
                               lhs, QT[:, hh * P:(hh + 1) * P],
                               start=(h == 0), stop=(msk is None and h == 3),
                               reads=lrd + ["qt"], writes=[pn])
                        if msk is not None:
                            for h in range(4):
                                mm(ps6[:, si * 512 + h * P: si * 512 + (h + 1) * P],
                                   IDb[:], msk[:], start=False, stop=(h == 3),
                                   reads=["IDb", mn], writes=[pn])
                    S_(lambda e, g=g, nsl=nsl: e.activation(
                        out=EB[:, 0:nsl * 512],
                        in_=ps6[:, 0:nsl * 512],
                        func=AF.Exp, bias=CB[:, g:g + 1], scale=SCALE),
                       reads=[f"ps6_{si}" for si in range(nsl)] + ["cb"],
                       writes=["eb0", "eb1", "eb2", "eb3"])
                    for h in range(4):
                        hh = g * 4 + h
                        psx = ps6[:, h * 512:h * 512 + 132]
                        pname = f"ps6_{h}"
                        for si, (kind, c, msk, mn) in enumerate(slots):
                            if kind == "sink":
                                rhs = VAS[:].rearrange("p (g2 c2) -> p g2 c2",
                                                       g2=NG)[:, g, :]
                                rrd = ["VAS"]
                            else:
                                rhs = VA[:].rearrange("p (g2 s c2) -> p g2 s c2",
                                                      g2=NG, s=RING)[:, g, c % RING, :]
                                rrd = [f"va{c % RING}"]
                            mm(psx,
                               EB[:, si * 512 + h * P: si * 512 + (h + 1) * P],
                               rhs, start=(si == 0), stop=(si == nsl - 1),
                               reads=["eb0", "eb1", "eb2", "eb3"] + rrd,
                               writes=[pname])
                        V(lambda e, psx=psx: e.reciprocal(out=RCP[:, 0:1],
                                                          in_=psx[:, 131:132]),
                          reads=[pname], writes=["rcp"])
                        V(lambda e, psx=psx, hh=hh: e.tensor_scalar(
                            out=ATH[:, hh * P:(hh + 1) * P], in0=psx[:, 0:P],
                            scalar1=RCP[:, 0:1], scalar2=None, op0=A.mult),
                          reads=[pname, "rcp"], writes=["ath"])
                        V(lambda e, psx=psx, hh=hh: e.scalar_tensor_tensor(
                            out=ATL[:, hh * P:(hh + 1) * P], in0=psx[:, 0:P],
                            scalar=RCP[:, 0:1], in1=ATH[:, hh * P:(hh + 1) * P],
                            op0=A.mult, op1=A.subtract),
                          reads=[pname, "rcp", "ath"], writes=["atl"])

                for half, (src, rbuf) in enumerate(((ATH, "ath"), (ATL, "atl"))):
                    for g8 in range(2):
                        for b8 in range(8):
                            k = g8 * 8 + b8
                            transp(psC[:, b8 * P:(b8 + 1) * P], src[:, k * P:(k + 1) * P],
                                   IDb[:], start=(b8 == 0), stop=(b8 == 7),
                                   reads=[rbuf, "IDb"], writes=["psC"])
                        V(lambda e, g8=g8: e.tensor_copy(ATT[:, g8 * 1024:(g8 + 1) * 1024],
                                                         psC[:]),
                          reads=["psC"], writes=["att"])
                    dma("G", "gA",
                        sAtT[half, :, jj, :, :].rearrange("k p q -> p k q"),
                        ATT[:].rearrange("p (k q) -> p k q", k=KC),
                        ["att"], ["sAtT"])

            dma("Y", "dB", Brow[0:1, 0:D], sBrows[2:3, 0:D], ["browh", "sBrows"], ["browh"])
            dma("Y", "dB", Brow[0:1, DQKV:DQKV + D], sBrows[3:4, 0:D],
                ["browl", "sBrows"], ["browl"])
            for dc4 in range(4):
                co = dc4 * 512
                for k in range(KC):
                    asl = k % 2
                    ah_, al_ = f"at{asl}h", f"at{asl}l"
                    dma("S", f"dA{asl}", AT[asl][:, 0:512].rearrange("p (t q) -> p t q", t=4),
                        sAtT[0, k, :, :, :].rearrange("t p q -> p t q"),
                        [ah_, "sAtT"], [ah_])
                    dma("Y", f"dAl{asl}", AT[asl][:, 512:1024].rearrange("p (t q) -> p t q", t=4),
                        sAtT[1, k, :, :, :].rearrange("t p q -> p t q"),
                        [al_, "sAtT"], [al_])
                    wsl = k % 2
                    wh_, wl_ = f"wo{wsl}h", f"wo{wsl}l"
                    dma("S", f"dO{wsl}", WO[wsl][:, 0:512],
                        sWoH[k * P:(k + 1) * P, co:co + 512], [wh_, "sWo"], [wh_])
                    dma("Y", f"dOl{wsl}", WO[wsl][:, 512:1024],
                        sWoL[k * P:(k + 1) * P, co:co + 512], [wl_, "sWo"], [wl_])
                    for q4 in range(4):
                        ah = AT[asl][:, q4 * P:(q4 + 1) * P]
                        al = AT[asl][:, 512 + q4 * P:512 + (q4 + 1) * P]
                        bn = f"ps6_{q4}"
                        out = ps6[:, q4 * 512:(q4 + 1) * 512]
                        mm(out, ah, WO[wsl][:, 0:512], (k == 0), False, [ah_, wh_], [bn])
                        mm(out, ah, WO[wsl][:, 512:1024], False, False, [ah_, wl_], [bn])
                        mm(out, al, WO[wsl][:, 0:512], False, False, [al_, wh_], [bn],
                           tick=(q4 == 3))
                for q4 in range(4):
                    bn = f"ps6_{q4}"
                    out = ps6[:, q4 * 512:(q4 + 1) * 512]
                    mm(out, OnesR[:], Brow[0:1, co:co + 512], False, False,
                       ["browh", "OnesR"], [bn])
                    mm(out, OnesR[:], Brow[0:1, DQKV + co:DQKV + co + 512], False, True,
                       ["browl", "OnesR"], [bn])
                    psl = EB[:, q4 * 512:(q4 + 1) * 512]
                    pn2 = f"eb{q4}"
                    tt2 = seq * TPS + sb4[q4]
                    xs = q4 % 2
                    xn2 = f"xt{xs}"
                    dma("Y", f"dX{xs}", XT[xs][:, 0:512],
                        dx[tt2 * P:(tt2 + 1) * P, co:co + 512], [xn2], [xn2])
                    V(lambda e, out=out, psl=psl, xs=xs: e.tensor_tensor(
                        out=psl, in0=out, in1=XT[xs][:, 0:512], op=A.add),
                      reads=[bn, xn2], writes=[pn2])
                    dma("G", "gP", sPP[tt2 * P:(tt2 + 1) * P, co:co + 512],
                        psl, [pn2], ["sPP"])

            dma("Y", "dB", Brow[0:1, 0:D], sBrows[4:5, 0:D], ["browh", "sBrows"], ["browh"])
            dma("Y", "dB", Brow[0:1, DQKV:DQKV + D], sBrows[5:6, 0:D],
                ["browl", "sBrows"], ["browl"])
            for dc4 in range(4):
                co = dc4 * 512
                for k in range(KC):
                    wsl = k % 2
                    wh_, wl_ = f"we{wsl}h", f"we{wsl}l"
                    dma("S", f"dE{wsl}", WE[wsl][:, 0:512],
                        sWeH[k * P:(k + 1) * P, co:co + 512], [wh_, "sWe"], [wh_])
                    dma("Y", f"dEl{wsl}", WE[wsl][:, 512:1024],
                        sWeL[k * P:(k + 1) * P, co:co + 512], [wl_, "sWe"], [wl_])
                    for jj in range(4):
                        xh = XnTH[:, jj * D + k * P:jj * D + (k + 1) * P]
                        xl = XnTL[:, jj * D + k * P:jj * D + (k + 1) * P]
                        bn = f"ps6_{jj}"
                        out = ps6[:, jj * 512:(jj + 1) * 512]
                        mm(out, xh, WE[wsl][:, 0:512], (k == 0), False, ["xnth", wh_], [bn])
                        mm(out, xh, WE[wsl][:, 512:1024], False, False, ["xnth", wl_], [bn])
                        mm(out, xl, WE[wsl][:, 0:512], False, False, ["xntl", wh_], [bn],
                           tick=(jj == 3))
                for jj in range(4):
                    bn = f"ps6_{jj}"
                    out = ps6[:, jj * 512:(jj + 1) * 512]
                    mm(out, OnesR[:], Brow[0:1, co:co + 512], False, False,
                       ["browh", "OnesR"], [bn])
                    mm(out, OnesR[:], Brow[0:1, DQKV + co:DQKV + co + 512], False, True,
                       ["browl", "OnesR"], [bn])
                    psl = EB[:, jj * 512:(jj + 1) * 512]
                    pn2 = f"eb{jj}"
                    V(lambda e, out=out, psl=psl: e.tensor_copy(psl, out),
                      reads=[bn], writes=[pn2])
                    tt2 = seq * TPS + sb4[jj]
                    dma("G", "gF", sFfn[tt2 * P:(tt2 + 1) * P, co:co + 512],
                        psl, [pn2], ["sFfn"])

    if KP < 4:
        with nc.Block() as block:
            sch.realize(block)
        es.close()
        return nc
    sch.emit("G", lambda e: e.collective_compute(
        "AllGather", A.bypass, replica_groups=[list(range(NCORES))],
        ins=[hSin.ap().opt()], outs=[hSout.ap().opt()]),
        reads=["dSin"], writes=["ccout"], sem="cc", amt=1, tick=False)
    dma("Y", "dR", SBLK[:], sSout.rearrange("r e2 f -> (r e2) f"), ["ccout"], ["sblk"])
    V(lambda e: e.memset(LO[:], 0.0), reads=[], writes=["lo"])
    V(lambda e: e.memset(HI[:], 1.0), reads=[], writes=["hi"])
    for _ in range(NBIS):
        V(lambda e: e.tensor_tensor(out=MID[:], in0=LO[:], in1=HI[:], op=A.add),
          reads=["lo", "hi"], writes=["mid"])
        V(lambda e: e.tensor_scalar(out=MID[:], in0=MID[:], scalar1=0.5,
                                    scalar2=None, op0=A.mult),
          reads=["mid"], writes=["mid"])
        for sl in range(4):
            V(lambda e, sl=sl: e.tensor_scalar(
                out=SCR[:, 0:1024], in0=SBLK[:, sl * 1024:(sl + 1) * 1024],
                scalar1=MID[:], scalar2=None, op0=A.is_gt, op1=A.add,
                accum_out=STAT[:, 24 + sl:25 + sl]),
              reads=["sblk", "mid"], writes=["scr", "stc"])
        V(lambda e: e.tensor_reduce(out=STAT[:, 28:29], in_=STAT[:, 24:28],
                                    axis=AXX, op=A.add),
          reads=["stc"], writes=["cnt"])
        mm(psA[:, 0:1], Bmat[:], STAT[:, 28:29], True, True, ["cnt", "Bmat"], ["psA"])
        V(lambda e: e.tensor_scalar(out=STAT[:, 29:30], in0=psA[:, 0:1],
                                    scalar1=float(TOPK), scalar2=None, op0=A.is_ge),
          reads=["psA"], writes=["ge"])
        V(lambda e: e.tensor_tensor(out=STAT[:, 30:31], in0=MID[:], in1=LO[:],
                                    op=A.subtract),
          reads=["mid", "lo"], writes=["t30"])
        V(lambda e: e.tensor_tensor(out=STAT[:, 30:31], in0=STAT[:, 30:31],
                                    in1=STAT[:, 29:30], op=A.mult),
          reads=["t30", "ge"], writes=["t30"])
        V(lambda e: e.tensor_tensor(out=LO[:], in0=LO[:], in1=STAT[:, 30:31], op=A.add),
          reads=["lo", "t30"], writes=["lo"])
        V(lambda e: e.tensor_tensor(out=STAT[:, 31:32], in0=MID[:], in1=HI[:],
                                    op=A.subtract),
          reads=["mid", "hi"], writes=["t31"])
        V(lambda e: e.tensor_tensor(out=STAT[:, 32:33], in0=STAT[:, 31:32],
                                    in1=STAT[:, 29:30], op=A.mult),
          reads=["t31", "ge"], writes=["t32"])
        V(lambda e: e.tensor_tensor(out=STAT[:, 31:32], in0=STAT[:, 31:32],
                                    in1=STAT[:, 32:33], op=A.subtract),
          reads=["t31", "t32"], writes=["t31"])
        V(lambda e: e.tensor_tensor(out=HI[:], in0=HI[:], in1=STAT[:, 31:32], op=A.add),
          reads=["hi", "t31"], writes=["hi"])
    V(lambda e: e.scalar_tensor_tensor(out=SBLK[:], in0=SBLK[:], scalar=LO[:],
                                       in1=SBLK[:], op0=A.is_gt, op1=A.mult),
      reads=["sblk", "lo"], writes=["sblk"])
    for sl in range(8):
        mm(psA[0:8, 0:512], Cmat[:, 0:8], SBLK[:, sl * 512:(sl + 1) * 512],
           True, True, ["sblk", "Cmat"], ["psA"])
        V(lambda e, sl=sl: e.tensor_copy(SCR[0:8, sl * 512:(sl + 1) * 512], psA[0:8, 0:512]),
          reads=["psA"], writes=["scr"])
    for s128 in range(NT):
        mm(psA[:, 0:1], SCR[0:8, s128 * P:(s128 + 1) * P], Oneh[:], True, True,
           ["scr", "Oneh"], ["psA"])
        V(lambda e, s128=s128: e.tensor_copy(WCOL[:, s128:s128 + 1], psA[:, 0:1]),
          reads=["psA"], writes=["wcol"])

    if KP < 5:
        with nc.Block() as block:
            sch.realize(block)
        es.close()
        return nc
    FFHs = [SCR[:, 0:1024], SCR[:, 1024:2048]]
    PPHs = [SCR[:, 2048:3072], SCR[:, 3072:4096]]
    first_touch = {"ffh0": True, "ffh1": True, "pph0": True, "pph1": True}
    for tt in range(NT):
        for h2 in range(2):
            co = h2 * 1024
            sl = (tt * 2 + h2) % 2
            xn_ = f"xt{sl}"
            fn_ = f"ffh{sl}"
            pn_ = f"pph{sl}"
            fw = [fn_] + (["scr"] if first_touch[fn_] else [])
            pw = [pn_] + (["scr"] if first_touch[pn_] else [])
            first_touch[fn_] = first_touch[pn_] = False
            dma("Y", f"dA{sl}", FFHs[sl], sFfn[tt * P:(tt + 1) * P, co:co + 1024],
                [fn_, "sFfn"], fw)
            dma("Y", f"dO{sl}", PPHs[sl], sPP[tt * P:(tt + 1) * P, co:co + 1024],
                [pn_, "sPP"], pw)
            V(lambda e, sl=sl, tt=tt: e.scalar_tensor_tensor(
                out=PPHs[sl], in0=FFHs[sl], scalar=WCOL[:, tt:tt + 1],
                in1=PPHs[sl], op0=A.mult, op1=A.add),
              reads=[fn_, "wcol", pn_], writes=[pn_])
            dma("G", "gO", dOut[tt * P:(tt + 1) * P, co:co + 1024], PPHs[sl],
                [pn_], ["outw"])

    sch.wait_all("G", [("gO", sch.cnt["gO"])])

    with nc.Block() as block:
        sch.realize(block)
    es.close()
    return nc


def _consts():
    import ml_dtypes
    bf = lambda a: np.asarray(a, dtype=ml_dtypes.bfloat16)
    i = np.arange(P)
    u = i[:, None]
    q = i[None, :]
    IDf = np.eye(P, dtype=np.float32)
    MLO = np.where(u >= q, 0.0, NEG).astype(np.float32)
    MLOS = np.where((u >= q) | (u < SINK), 0.0, NEG).astype(np.float32)
    MHI = np.where(u <= q, 0.0, NEG).astype(np.float32)
    MSINK = np.where(np.repeat(u < SINK, P, axis=1), 0.0, NEG).astype(np.float32)
    t = np.arange(T, dtype=np.float64)
    inv = 1.0 / (10000.0 ** (np.arange(DH // 2, dtype=np.float64) * 2.0 / DH))
    ang = t[:, None] * inv[None, :]
    cos = np.cos(ang)
    sin = np.sin(ang)
    ROPC = np.concatenate([cos, cos], axis=1).astype(np.float32)
    ROPS = np.concatenate([-sin, sin], axis=1).astype(np.float32)
    Bm = np.zeros((P, P), dtype=np.float32)
    for p in range(P):
        for qq in range(P):
            if p % 16 == qq % 16:
                Bm[p, qq] = 1.0
    Cm = np.zeros((P, 8), dtype=np.float32)
    for p in range(P):
        Cm[p, p // 16] = 1.0
    return dict(IDf=IDf, IDb=bf(IDf), MLO=bf(MLO), MLOS=bf(MLOS), MHI=bf(MHI),
                MSINK=bf(MSINK), ROPC=ROPC, ROPS=ROPS, Bmat=Bm, Cmat=Cm,
                OnesRow=bf(np.ones((1, P), dtype=np.float32)),
                OnesRowF=np.ones((1, P), dtype=np.float32))


def kernel(L0_attn_x, L0_attn_Wq, L0_attn_bq, L0_attn_Wk, L0_attn_bk,
           L0_attn_Wv, L0_attn_bv, L0_attn_Wo, L0_attn_bo,
           L0_ffn_router_w, L0_ffn_router_b, L0_ffn_expert_w, L0_ffn_expert_b):
    nc = build_program()
    consts = _consts()
    x = np.asarray(L0_attn_x, dtype=np.float32)
    in_maps = []
    for c in range(NCORES):
        oneh = np.zeros((8, 1), dtype=np.float32)
        oneh[c, 0] = 1.0
        m = {
            "x": np.ascontiguousarray(x[c * BPC:(c + 1) * BPC].reshape(BPC * T, D)),
            "Wq": np.asarray(L0_attn_Wq, dtype=np.float32),
            "bq": np.asarray(L0_attn_bq, dtype=np.float32),
            "Wk": np.asarray(L0_attn_Wk, dtype=np.float32),
            "bk": np.asarray(L0_attn_bk, dtype=np.float32),
            "Wv": np.asarray(L0_attn_Wv, dtype=np.float32),
            "bv": np.asarray(L0_attn_bv, dtype=np.float32),
            "Wo": np.asarray(L0_attn_Wo, dtype=np.float32),
            "bo": np.asarray(L0_attn_bo, dtype=np.float32),
            "Wr": np.asarray(L0_ffn_router_w, dtype=np.float32),
            "br": np.asarray(L0_ffn_router_b, dtype=np.float32),
            "We": np.asarray(L0_ffn_expert_w, dtype=np.float32),
            "be": np.asarray(L0_ffn_expert_b, dtype=np.float32),
            "Oneh": oneh,
        }
        m.update(consts)
        in_maps.append(m)
    res = run_bass_kernel_spmd(nc, in_maps, core_ids=list(range(NCORES)))
    outs = [np.asarray(r["out"]).reshape(BPC, T, D) for r in res.results]
    return np.concatenate(outs, axis=0)


if __name__ == "__main__":
    build_program()
    print("built ok")


# version 12
# speedup vs baseline: 1.0133x; 1.0035x over previous
import os
import numpy as np
from contextlib import ExitStack

import concourse.bass as bass
import concourse.mybir as mb
import bass_rust
from concourse.bass_utils import run_bass_kernel_spmd
from concourse.alu_op_type import AluOpType as A

F32 = mb.dt.float32
BF16 = mb.dt.bfloat16
AXX = mb.AxisListType.X
AF = mb.ActivationFunctionType

P = 128
D = 2048
T = 2048
NCORES = 8
BPC = 2
NT = BPC * T // P
TPS = T // P
KC = D // P
NH, NG, DH = 16, 4, 128
WIN, SINK = 512, 4
NE = 16
BS = 16 * T
TOPK = BS // NE
SCALE = DH ** -0.5
EPS = 1e-5
DQKV = D + 1024
NEG = -1.0e30
RING = 5
NBIS = 34

ENGS = ("T", "V", "S", "G", "Y")


class Sched:

    def __init__(self, nc, es):
        self.nc = nc
        self.es = es
        self.ops = {e: [] for e in ENGS}
        self.sems = {}
        self.cnt = {}
        for e in ENGS:
            self.mksem("tk" + e)
        self.bufs = {}
        self.sat = {e: {} for e in ENGS}
        self.pending = {e: [] for e in ENGS}

    def mksem(self, name):
        self.sems[name] = self.es.enter_context(self.nc.semaphore(name))
        self.cnt[name] = 0
        return name

    def emit(self, eng, fn, reads=(), writes=(), sem=None, amt=1, tick=True,
             extra_waits=()):
        need = {}
        for s, c in extra_waits:
            if need.get(s, 0) < c:
                need[s] = c

        def add(mark):
            if mark is not None and need.get(mark[0], 0) < mark[1]:
                need[mark[0]] = mark[1]

        for b in reads:
            st = self.bufs.get(b)
            if st:
                add(st.get("w"))
        for b in writes:
            st = self.bufs.get(b)
            if st:
                add(st.get("w"))
                for m in st.get("r", ()):
                    add(m)
        sat = self.sat[eng]
        waits = [(s, c) for s, c in need.items() if sat.get(s, 0) < c]
        for s, c in waits:
            sat[s] = c
        if sem is None and tick:
            sem = "tk" + eng
        if sem is not None:
            self.cnt[sem] += amt
            mark = (sem, self.cnt[sem])
        else:
            mark = None
        self.ops[eng].append((waits, fn, sem, amt))
        for b in writes:
            self.bufs[b] = {"w": mark, "r": []}
        if mark is None:
            self.pending[eng].extend(reads)
            self.pending[eng].extend(writes)
        else:
            for b in list(reads) + self.pending[eng]:
                st = self.bufs.setdefault(b, {"w": None, "r": []})
                st["r"].append(mark)
                if st.get("w") is None and b in self.pending[eng]:
                    pass
            for b in self.pending[eng]:
                st = self.bufs.setdefault(b, {"w": None, "r": []})
            self.pending[eng] = []
        return mark

    def wait_all(self, eng, marks):
        sat = self.sat[eng]
        ws = []
        for s, c in marks:
            if sat.get(s, 0) < c:
                ws.append((s, c))
                sat[s] = c
        if ws:
            self.ops[eng].append((ws, None, None, 0))

    def realize(self, block):
        def runner(ename):
            def run(eng):
                for waits, fn, sem, amt in self.ops[ename]:
                    for s, c in waits:
                        eng.wait_ge(self.sems[s], c)
                    if fn is not None:
                        inst = fn(eng)
                        if sem is not None:
                            inst.then_inc(self.sems[sem], amt)
            return run

        block.tensor(runner("T"))
        block.vector(runner("V"))
        block.scalar(runner("S"))
        block.gpsimd(runner("G"))
        block.sync(runner("Y"))


def build_program():
    nc = bass.Bass("TRN2", target_bir_lowering=False)
    es = ExitStack()

    def din(name, shape, dt):
        return nc.dram_tensor(name, shape, dt, kind="ExternalInput").ap()

    dx = din("x", [BPC * T, D], F32)
    dWq = din("Wq", [D, D], F32)
    dbq = din("bq", [D], F32)
    dWk = din("Wk", [D, 512], F32)
    dbk = din("bk", [512], F32)
    dWv = din("Wv", [D, 512], F32)
    dbv = din("bv", [512], F32)
    dWo = din("Wo", [D, D], F32)
    dbo = din("bo", [D], F32)
    dWr = din("Wr", [D, NE], F32)
    dbr = din("br", [NE], F32)
    dWe = din("We", [D, D], F32)
    dbe = din("be", [D], F32)
    dIDf = din("IDf", [P, P], F32)
    dIDb = din("IDb", [P, P], BF16)
    dMLO = din("MLO", [P, P], BF16)
    dMLOS = din("MLOS", [P, P], BF16)
    dMHI = din("MHI", [P, P], BF16)
    dMSINK = din("MSINK", [P, P], BF16)
    dROPC = din("ROPC", [T, DH], F32)
    dROPS = din("ROPS", [T, DH], F32)
    dBmat = din("Bmat", [P, P], F32)
    dCmat = din("Cmat", [P, 8], F32)
    dOneh = din("Oneh", [8, 1], F32)
    dOnes = din("OnesRow", [1, P], BF16)
    dOnesF = din("OnesRowF", [1, P], F32)
    dOut = nc.dram_tensor("out", [BPC * T, D], F32, kind="ExternalOutput").ap()

    if int(os.environ.get("KNOSCR", "0")):
        class _FakeAp:
            def __getitem__(self, *a):
                return None
            def rearrange(self, *a, **k):
                return None
            def ap(self):
                return None
            def opt(self):
                return None
        _f = _FakeAp()
        sWqkvH = sWqkvL = sWoH = sWoL = sWeH = sWeL = _f
        sBrows = sAtT = sFfn = sPP = sSin = sSout = _f
        hSin = hSout = _f
    else:
        sWqkvH = nc.dram_tensor("sWqkvH", [D, DQKV], BF16).ap()
        sWqkvL = nc.dram_tensor("sWqkvL", [D, DQKV], BF16).ap()
        sWoH = nc.dram_tensor("sWoH", [D, D], BF16).ap()
        sWoL = nc.dram_tensor("sWoL", [D, D], BF16).ap()
        sWeH = nc.dram_tensor("sWeH", [D, D], BF16).ap()
        sWeL = nc.dram_tensor("sWeL", [D, D], BF16).ap()
        sBrows = nc.dram_tensor("sBrows", [6, DQKV], BF16).ap()
        sAtT = nc.dram_tensor("sAtT", [2, KC, 4, P, P], BF16).ap()
        KDBG = int(os.environ.get("KDBG", "0"))
        kind = dict(kind="ExternalOutput") if KDBG else {}
        sFfn = nc.dram_tensor("sFfn", [NT * P, D], F32, **kind).ap()
        sPP = nc.dram_tensor("sPP", [NT * P, D], F32, **kind).ap()
        hSin = nc.dram_tensor("sSin", [NE, BPC * T], F32, **kind)
        hSout = nc.dram_tensor("sSout", [NCORES, NE, BPC * T], F32)
        sSin = hSin.ap()
        sSout = hSout.ap()

    if int(os.environ.get("KBARE", "0")):
        with nc.sbuf_tensor("bx", [64, 64], F32) as bx, \
             nc.semaphore("bd") as bd, nc.semaphore("bg") as bg, nc.Block() as block:
            @block.sync
            def _(sync):
                sync.dma_start(out=bx[:], in_=dx[0:64, 0:64]).then_inc(bd, 16)
            @block.gpsimd
            def _(g):
                g.wait_ge(bd, 16)
                g.dma_start(out=dOut[0:64, 0:64], in_=bx[:]).then_inc(bg, 16)
                g.wait_ge(bg, 16)
        return nc
    KSB = int(os.environ.get("KSB", "0"))
    def sb(name, shape, dt):
        if KSB:
            shape = [1, 4]
        return es.enter_context(nc.sbuf_tensor("b_" + name, shape, dt))

    IDf = sb("IDf", [P, P], F32)
    IDb = sb("IDb", [P, P], BF16)
    MLO = sb("MLO", [P, P], BF16)
    MLOS = sb("MLOS", [P, P], BF16)
    MHI = sb("MHI", [P, P], BF16)
    MSINK = sb("MSINK", [P, P], BF16)
    Bmat = sb("Bmat", [P, P], F32)
    Cmat = sb("Cmat", [P, 8], F32)
    Oneh = sb("Oneh", [8, 1], F32)
    OnesR = sb("OnesR", [1, P], BF16)
    OnesF = sb("OnesF", [1, P], F32)
    RopC = sb("RopC", [P, 4 * DH], F32)
    RopS = sb("RopS", [P, 4 * DH], F32)
    WrH = sb("WrH", [P, KC * NE], BF16)
    WrL = sb("WrL", [P, KC * NE], BF16)
    BrowR = sb("BrowR", [1, 2 * NE], BF16)
    Brow = sb("Brow", [1, 2 * DQKV], BF16)
    WB = [sb(f"WB{i}", [P, 2 * DQKV], BF16) for i in range(2)]
    WO = [sb(f"WOs{i}", [P, 1024], BF16) for i in range(2)]
    WE = [sb(f"WEs{i}", [P, 1024], BF16) for i in range(2)]
    AT = [sb(f"ATs{i}", [P, 1024], BF16) for i in range(2)]
    XT = [sb(f"XTs{i}", [P, D], F32) for i in range(2)]
    SCR = sb("SCR", [P, 2 * D], F32)
    XnTH = sb("XnTH", [P, 4 * D], BF16)
    XnTL = sb("XnTL", [P, 4 * D], BF16)
    QB = sb("QB", [P, 512], F32)
    QROT = sb("QROT", [P, 512], F32)
    QT = sb("QT", [P, NH * P], F32)
    KT = sb("KT", [P, NG * RING * P], F32)
    KTS = sb("KTS", [P, NG * P], F32)
    VA = sb("VA", [P, NG * RING * 132], F32)
    VAS = sb("VAS", [P, NG * 132], F32)
    EB = sb("EB", [P, 4 * 6 * P], F32)
    ATF = sb("ATF", [P, D], F32)
    ATT = sb("ATT", [P, D], BF16)
    ATT2 = sb("ATT2", [P, D], BF16)
    STAT = sb("STAT", [P, 40], F32)
    QN2 = sb("QN2", [P, NH], F32)
    KN2 = sb("KN2", [P, NG], F32)
    CB = sb("CB", [P, NH], F32)
    RCP = sb("RCP", [P, 2], F32)
    STILE = sb("STILE", [P, NE], F32)
    SBLK = sb("SBLK", [P, 4096], F32)
    LO = sb("LO", [P, 1], F32)
    HI = sb("HI", [P, 1], F32)
    MID = sb("MID", [P, 1], F32)
    WCOL = sb("WCOL", [P, NT], F32)

    ps6 = es.enter_context(nc.psum_tensor("ps6", [P, 3072], F32))
    psA = es.enter_context(nc.psum_tensor("psA", [P, 512], F32))
    psB = es.enter_context(nc.psum_tensor("psB", [P, 512], F32))

    sch = Sched(nc, es)
    for s in (["dC", "dW0", "dW1", "gWl0", "gWl1", "dX0", "dX1", "dR", "dB"]
              + [f"d{p}{i}" for p in ("A", "Al", "O", "Ol", "E", "El") for i in range(4)]
              + ["gS", "gA", "gF", "gP", "gO", "gW", "cc"]):
        sch.mksem(s)

    def dma(eng, sem, out, in_, reads, writes):
        pw = [(sem, sch.cnt[sem])] if sch.cnt[sem] > 0 else []
        sch.emit(eng, lambda e, o=out, i=in_: e.dma_start(out=o, in_=i),
                 reads=reads, writes=writes, sem=sem, amt=16, tick=False,
                 extra_waits=pw)

    def mm(out, lhsT, rhs, start, stop, reads, writes, tick=None):
        sch.emit("T", lambda e, o=out, l=lhsT, r=rhs, st=start, sp=stop:
                 e.matmul(o, l, r, start=st, stop=sp),
                 reads=reads, writes=writes, tick=(stop if tick is None else tick))

    def transp(out, in_, ident, start, stop, reads, writes):
        sch.emit("T", lambda e, o=out, i=in_, d=ident, st=start, sp=stop:
                 e.matmul(o, i, d, start=st, stop=sp, is_transpose=True),
                 reads=reads, writes=writes, tick=stop)

    def V(fn, reads=(), writes=()):
        sch.emit("V", fn, reads=reads, writes=writes)

    def S_(fn, reads=(), writes=()):
        sch.emit("S", fn, reads=reads, writes=writes)

    def apf(t, off, dims):
        base = t[:]
        return bass.AP(base.tensor, off, [list(base.ap[0])] + [list(d) for d in dims])

    KNC = int(os.environ.get("KNC", "99"))
    for buf, src, nm in [(IDf, dIDf, "IDf"), (IDb, dIDb, "IDb"), (MLO, dMLO, "MLO"),
                         (MLOS, dMLOS, "MLOS"), (MHI, dMHI, "MHI"), (MSINK, dMSINK, "MSINK"),
                         (Bmat, dBmat, "Bmat"), (Cmat, dCmat, "Cmat"), (Oneh, dOneh, "Oneh"),
                         (OnesR, dOnes, "OnesR"), (OnesF, dOnesF, "OnesF")][:KNC]:
        dma("Y", "dC", buf[:], src, [], [nm])

    KST = int(os.environ.get("KST", "9"))
    def split_weight(src_ap, dst_h, dst_l, ncols, dname):
        if KST < 1:
            return
        for k in range(KC):
            xb = XT[k % 2]
            xn_ = f"xt{k % 2}"
            dma("Y", f"dX{k % 2}", xb[:, 0:ncols], src_ap[k * P:(k + 1) * P, :], [xn_], [xn_])
            V(lambda e, xb=xb, n=ncols: e.tensor_copy(ATT[:, 0:n], xb[:, 0:n]),
              reads=[xn_], writes=["att"])
            V(lambda e, xb=xb, n=ncols: e.tensor_tensor(
                out=ATT2[:, 0:n], in0=xb[:, 0:n], in1=ATT[:, 0:n], op=A.subtract),
              reads=[xn_, "att"], writes=["att2"])
            dma("G", "gW", dst_h[k * P:(k + 1) * P, :], ATT[:, 0:ncols], ["att"], [dname])
            dma("G", "gW", dst_l[k * P:(k + 1) * P, :], ATT2[:, 0:ncols], ["att2"], [dname])

    split_weight(dWq, sWqkvH[:, 0:D], sWqkvL[:, 0:D], D, "sWqkv")
    split_weight(dWk, sWqkvH[:, D:D + 512], sWqkvL[:, D:D + 512], 512, "sWqkv")
    split_weight(dWv, sWqkvH[:, D + 512:DQKV], sWqkvL[:, D + 512:DQKV], 512, "sWqkv")
    split_weight(dWo, sWoH, sWoL, D, "sWo")
    split_weight(dWe, sWeH, sWeL, D, "sWe")

    for k in (range(KC) if KST >= 2 else []):
        dma("Y", "dR", SCR[:, 0:NE], dWr[k * P:(k + 1) * P, :], ["scr"], ["scr"])
        V(lambda e, k=k: e.tensor_copy(WrH[:, k * NE:(k + 1) * NE], SCR[:, 0:NE]),
          reads=["scr"], writes=["WrH"])
        V(lambda e, k=k: e.tensor_tensor(
            out=WrL[:, k * NE:(k + 1) * NE], in0=SCR[:, 0:NE],
            in1=WrH[:, k * NE:(k + 1) * NE], op=A.subtract),
          reads=["scr", "WrH"], writes=["WrL"])

    def split_bias(parts, row_h, row_l):
        if KST < 3:
            return
        for ap1, ncols, off in parts:
            dma("Y", "dB", SCR[0:1, off:off + ncols],
                ap1.rearrange("(a n) -> a n", a=1), ["scr"], ["scr"])
        V(lambda e: e.tensor_copy(Brow[0:1, 0:DQKV], SCR[0:1, 0:DQKV]),
          reads=["scr"], writes=["browh"])
        V(lambda e: e.tensor_tensor(
            out=Brow[0:1, DQKV:2 * DQKV], in0=SCR[0:1, 0:DQKV],
            in1=Brow[0:1, 0:DQKV], op=A.subtract),
          reads=["scr", "browh"], writes=["browl"])
        dma("G", "gW", row_h, Brow[0:1, 0:DQKV], ["browh"], ["sBrows"])
        dma("G", "gW", row_l, Brow[0:1, DQKV:2 * DQKV], ["browl"], ["sBrows"])

    split_bias([(dbq, D, 0), (dbk, 512, D), (dbv, 512, D + 512)], sBrows[0:1, :], sBrows[1:2, :])
    split_bias([(dbo, D, 0)], sBrows[2:3, :], sBrows[3:4, :])
    split_bias([(dbe, D, 0)], sBrows[4:5, :], sBrows[5:6, :])
    if KST >= 3:
        dma("Y", "dB", SCR[0:1, 0:NE], dbr.rearrange("(a n) -> a n", a=1), ["scr"], ["scr"])
        V(lambda e: e.tensor_copy(BrowR[0:1, 0:NE], SCR[0:1, 0:NE]),
          reads=["scr"], writes=["browr"])
        V(lambda e: e.tensor_tensor(
            out=BrowR[0:1, NE:2 * NE], in0=SCR[0:1, 0:NE],
            in1=BrowR[0:1, 0:NE], op=A.subtract),
          reads=["scr", "browr"], writes=["browr"])

    if KST < 4:
        with nc.Block() as block:
            sch.realize(block)
        es.close()
        return nc
    V(lambda e: e.memset(VA[:], 0.0), reads=[], writes=["va0", "va1", "va2", "va3", "va4", "vaone"])
    V(lambda e: e.memset(
        VA[:].rearrange("p (gs c) -> p gs c", c=132)[:, :, 131:132], 1.0),
      reads=["vaone"], writes=["vaone"])
    V(lambda e: e.memset(VAS[:], 0.0), reads=[], writes=["VAS"])
    V(lambda e: e.memset(
        VAS[:].rearrange("p (g c) -> p g c", c=132)[:, :, 131:132], 1.0),
      reads=["VAS"], writes=["VAS"])
    V(lambda e: e.memset(KTS[:], 0.0), reads=[], writes=["KTS"])

    KP = int(os.environ.get("KP", "9"))
    NSEQ = 0 if KP < 2 else (1 if KP == 2 else BPC)
    for seq in range(NSEQ):
        tok0 = seq * T
        for sbi in range(TPS // 4):
            sb4 = [sbi * 4 + j for j in range(4)]
            dma("Y", "dR", RopC[:].rearrange("p (j c) -> p j c", j=4),
                dROPC.rearrange("(j p) c -> p j c", p=P)[:, sbi * 4:(sbi + 1) * 4, :],
                ["ropC"], ["ropC"])
            dma("Y", "dR", RopS[:].rearrange("p (j c) -> p j c", j=4),
                dROPS.rearrange("(j p) c -> p j c", p=P)[:, sbi * 4:(sbi + 1) * 4, :],
                ["ropS"], ["ropS"])

            dma("Y", "dB", Brow[0:1, 0:DQKV], sBrows[0:1, :], ["browh", "sBrows"], ["browh"])
            dma("Y", "dB", Brow[0:1, DQKV:2 * DQKV], sBrows[1:2, :],
                ["browl", "sBrows"], ["browl"])
            for jj, tp in enumerate(sb4):
                tt = seq * TPS + tp
                xslot = tt % 2
                xn_ = f"xt{xslot}"
                dma("Y", f"dX{xslot}", XT[xslot][:],
                    dx[tok0 + tp * P: tok0 + (tp + 1) * P, :], [xn_], [xn_])
                st = STAT
                V(lambda e, xb=XT[xslot]: e.reduce_sum(out=st[:, 0:1], in_=xb[:], axis=AXX),
                  reads=[xn_], writes=["st"])
                S_(lambda e, xb=XT[xslot]: e.activation(
                    out=SCR[:, 0:D], in_=xb[:], func=AF.Square, scale=1.0,
                    accum_out=st[:, 1:2]),
                   reads=[xn_], writes=["scr", "st"])
                V(lambda e: e.tensor_scalar(out=st[:, 2:3], in0=st[:, 0:1],
                                            scalar1=1.0 / D, scalar2=None, op0=A.mult),
                  reads=["st"], writes=["st"])
                V(lambda e: e.tensor_tensor(out=st[:, 3:4], in0=st[:, 2:3],
                                            in1=st[:, 2:3], op=A.mult),
                  reads=["st"], writes=["st"])
                V(lambda e: e.tensor_scalar(out=st[:, 4:5], in0=st[:, 1:2],
                                            scalar1=1.0 / D, scalar2=None, op0=A.mult),
                  reads=["st"], writes=["st"])
                V(lambda e: e.tensor_tensor(out=st[:, 5:6], in0=st[:, 4:5],
                                            in1=st[:, 3:4], op=A.subtract),
                  reads=["st"], writes=["st"])
                V(lambda e: e.tensor_scalar(out=st[:, 6:7], in0=st[:, 5:6],
                                            scalar1=EPS, scalar2=None, op0=A.add),
                  reads=["st"], writes=["st"])
                V(lambda e: e.memset(st[:, 7:8], 1.0), reads=[], writes=["sty"])
                for _ in range(6):
                    V(lambda e: e.tensor_tensor(out=st[:, 8:9], in0=st[:, 7:8],
                                                in1=st[:, 7:8], op=A.mult),
                      reads=["sty"], writes=["st8"])
                    V(lambda e: e.tensor_tensor(out=st[:, 8:9], in0=st[:, 8:9],
                                                in1=st[:, 6:7], op=A.mult),
                      reads=["st8", "st"], writes=["st8"])
                    V(lambda e: e.tensor_scalar(out=st[:, 8:9], in0=st[:, 8:9],
                                                scalar1=-0.5, scalar2=1.5,
                                                op0=A.mult, op1=A.add),
                      reads=["st8"], writes=["st8"])
                    V(lambda e: e.tensor_tensor(out=st[:, 7:8], in0=st[:, 7:8],
                                                in1=st[:, 8:9], op=A.mult),
                      reads=["sty", "st8"], writes=["sty"])
                V(lambda e: e.tensor_tensor(out=st[:, 9:10], in0=st[:, 2:3],
                                            in1=st[:, 7:8], op=A.mult),
                  reads=["st", "sty"], writes=["st9"])
                V(lambda e: e.tensor_scalar(out=st[:, 9:10], in0=st[:, 9:10],
                                            scalar1=-1.0, scalar2=None, op0=A.mult),
                  reads=["st9"], writes=["st9"])
                V(lambda e, xb=XT[xslot]: e.tensor_scalar(
                    out=SCR[:, 0:D], in0=xb[:], scalar1=st[:, 7:8], scalar2=st[:, 9:10],
                    op0=A.mult, op1=A.add),
                  reads=[xn_, "sty", "st9"], writes=["scr"])
                for g4 in range(4):
                    for b4 in range(4):
                        k = g4 * 4 + b4
                        transp(psB[:, b4 * P:(b4 + 1) * P], SCR[:, k * P:(k + 1) * P],
                               IDf[:], start=(b4 == 0), stop=(b4 == 3),
                               reads=["scr", "IDf"], writes=["psB"])
                    V(lambda e, g4=g4, jj=jj: e.tensor_copy(
                        XnTH[:, jj * D + g4 * 512:jj * D + (g4 + 1) * 512], psB[:]),
                      reads=["psB"], writes=["xnth"])
                    V(lambda e, g4=g4, jj=jj: e.tensor_tensor(
                        out=XnTL[:, jj * D + g4 * 512:jj * D + (g4 + 1) * 512],
                        in0=psB[:],
                        in1=XnTH[:, jj * D + g4 * 512:jj * D + (g4 + 1) * 512],
                        op=A.subtract),
                      reads=["psB", "xnth"], writes=["xntl"])

                for k in range(KC):
                    ws = k % 2
                    wh_, wl_ = f"wb{ws}h", f"wb{ws}l"
                    dma("S", f"dW{ws}", WB[ws][:, 0:DQKV],
                        sWqkvH[k * P:(k + 1) * P, :], [wh_, "sWqkv"], [wh_])
                    dma("G", f"gWl{ws}", WB[ws][:, DQKV:2 * DQKV],
                        sWqkvL[k * P:(k + 1) * P, :], [wl_, "sWqkv"], [wl_])
                    xh = XnTH[:, jj * D + k * P:jj * D + (k + 1) * P]
                    xl = XnTL[:, jj * D + k * P:jj * D + (k + 1) * P]
                    st0 = (k == 0)
                    for dc in range(6):
                        wh = WB[ws][:, dc * 512:(dc + 1) * 512]
                        wl = WB[ws][:, DQKV + dc * 512:DQKV + (dc + 1) * 512]
                        pn = f"ps6_{dc}"
                        mm(ps6[:, dc * 512:(dc + 1) * 512], xh, wh, st0, False,
                           ["xnth", wh_], [pn])
                        mm(ps6[:, dc * 512:(dc + 1) * 512], xh, wl, False, False,
                           ["xnth", wl_], [pn])
                    mm(psA[:, 0:NE], xh, WrH[:, k * NE:(k + 1) * NE], st0, False,
                       ["xnth", "WrH"], ["psA"])
                    mm(psA[:, 0:NE], xh, WrL[:, k * NE:(k + 1) * NE], False, False,
                       ["xnth", "WrL"], ["psA"])
                    for dc in range(6):
                        wh = WB[ws][:, dc * 512:(dc + 1) * 512]
                        pn = f"ps6_{dc}"
                        mm(ps6[:, dc * 512:(dc + 1) * 512], xl, wh, False, False,
                           ["xntl", wh_], [pn])
                    mm(psA[:, 0:NE], xl, WrH[:, k * NE:(k + 1) * NE], False, False,
                       ["xntl", "WrH"], ["psA"], tick=True)
                for dc in range(6):
                    pn = f"ps6_{dc}"
                    mm(ps6[:, dc * 512:(dc + 1) * 512], OnesR[:],
                       Brow[0:1, dc * 512:(dc + 1) * 512], False, False,
                       ["browh", "OnesR"], [pn])
                    mm(ps6[:, dc * 512:(dc + 1) * 512], OnesR[:],
                       Brow[0:1, DQKV + dc * 512:DQKV + (dc + 1) * 512], False, True,
                       ["browl", "OnesR"], [pn])
                mm(psA[:, 0:NE], OnesR[:], BrowR[0:1, 0:NE], False, False,
                   ["browr", "OnesR"], ["psA"])
                mm(psA[:, 0:NE], OnesR[:], BrowR[0:1, NE:2 * NE], False, True,
                   ["browr", "OnesR"], ["psA"])

                V(lambda e: e.reduce_max(out=st[:, 10:11], in_=psA[:, 0:NE],
                                         axis=AXX, negate=True),
                  reads=["psA"], writes=["st10"])
                S_(lambda e: e.activation(out=STILE[:], in_=psA[:, 0:NE], func=AF.Exp,
                                          bias=st[:, 10:11], scale=1.0,
                                          accum_out=st[:, 11:12]),
                   reads=["psA", "st10"], writes=["stile", "st11"])
                V(lambda e: e.reciprocal(out=st[:, 12:13], in_=st[:, 11:12]),
                  reads=["st11"], writes=["st12"])
                V(lambda e: e.tensor_scalar(out=STILE[:], in0=STILE[:],
                                            scalar1=st[:, 12:13], scalar2=None, op0=A.mult),
                  reads=["stile", "st12"], writes=["stile"])
                transp(psA[0:NE, 0:P], STILE[:], IDf[:], True, True,
                       ["stile", "IDf"], ["psA"])
                V(lambda e: e.tensor_copy(SCR[0:NE, 0:P], psA[0:NE, 0:P]),
                  reads=["psA"], writes=["scr"])
                dma("G", "gS", sSin[:, tt * P:(tt + 1) * P], SCR[0:NE, 0:P],
                    ["scr"], ["dSin"])

                slot = tp % RING
                co_c = jj * DH
                for ch in range(4):
                    pq = ps6[:, ch * 512:(ch + 1) * 512]
                    pn = f"ps6_{ch}"
                    cb_ap = apf(RopC, co_c, [[0, 4], [1, DH]])
                    sb_lo = apf(RopS, co_c, [[0, 4], [1, 64]])
                    sb_hi = apf(RopS, co_c + 64, [[0, 4], [1, 64]])
                    V(lambda e, pq=pq, cb_ap=cb_ap: e.tensor_tensor(
                        out=QROT[:].rearrange("p (h c) -> p h c", h=4),
                        in0=pq.rearrange("p (h c) -> p h c", h=4),
                        in1=cb_ap, op=A.mult),
                      reads=[pn, "ropC"], writes=["qrot"])
                    V(lambda e, pq=pq, sb_lo=sb_lo: e.tensor_tensor(
                        out=QB[:].rearrange("p (h c) -> p h c", h=4)[:, :, 0:64],
                        in0=pq.rearrange("p (h c) -> p h c", h=4)[:, :, 64:128],
                        in1=sb_lo, op=A.mult),
                      reads=[pn, "ropS"], writes=["qb"])
                    V(lambda e, pq=pq, sb_hi=sb_hi: e.tensor_tensor(
                        out=QB[:].rearrange("p (h c) -> p h c", h=4)[:, :, 64:128],
                        in0=pq.rearrange("p (h c) -> p h c", h=4)[:, :, 0:64],
                        in1=sb_hi, op=A.mult),
                      reads=[pn, "ropS"], writes=["qb"])
                    V(lambda e: e.tensor_tensor(out=QROT[:], in0=QROT[:], in1=QB[:], op=A.add),
                      reads=["qrot", "qb"], writes=["qrot"])
                    V(lambda e: e.tensor_tensor(out=QB[:], in0=QROT[:], in1=QROT[:], op=A.mult),
                      reads=["qrot"], writes=["qb"])
                    V(lambda e, ch=ch: e.tensor_reduce(
                        out=QN2[:, ch * 4:(ch + 1) * 4],
                        in_=QB[:].rearrange("p (h c) -> p h c", h=4), axis=AXX, op=A.add),
                      reads=["qb"], writes=["qn2"])
                    for b4 in range(4):
                        transp(psA[:, b4 * P:(b4 + 1) * P], QROT[:, b4 * P:(b4 + 1) * P],
                               IDf[:], start=(b4 == 0), stop=(b4 == 3),
                               reads=["qrot", "IDf"], writes=["psA"])
                    V(lambda e, ch=ch: e.tensor_copy(QT[:, ch * 512:(ch + 1) * 512], psA[:]),
                      reads=["psA"], writes=["qt"])
                pk = ps6[:, 2048:2560]
                cb_ap = apf(RopC, co_c, [[0, 4], [1, DH]])
                sb_lo = apf(RopS, co_c, [[0, 4], [1, 64]])
                sb_hi = apf(RopS, co_c + 64, [[0, 4], [1, 64]])
                V(lambda e, cb_ap=cb_ap: e.tensor_tensor(
                    out=QROT[:].rearrange("p (h c) -> p h c", h=4),
                    in0=pk.rearrange("p (h c) -> p h c", h=4),
                    in1=cb_ap, op=A.mult),
                  reads=["ps6_4", "ropC"], writes=["qrot"])
                V(lambda e, sb_lo=sb_lo: e.tensor_tensor(
                    out=QB[:].rearrange("p (h c) -> p h c", h=4)[:, :, 0:64],
                    in0=pk.rearrange("p (h c) -> p h c", h=4)[:, :, 64:128],
                    in1=sb_lo, op=A.mult),
                  reads=["ps6_4", "ropS"], writes=["qb"])
                V(lambda e, sb_hi=sb_hi: e.tensor_tensor(
                    out=QB[:].rearrange("p (h c) -> p h c", h=4)[:, :, 64:128],
                    in0=pk.rearrange("p (h c) -> p h c", h=4)[:, :, 0:64],
                    in1=sb_hi, op=A.mult),
                  reads=["ps6_4", "ropS"], writes=["qb"])
                V(lambda e: e.tensor_tensor(out=QROT[:], in0=QROT[:], in1=QB[:], op=A.add),
                  reads=["qrot", "qb"], writes=["qrot"])
                V(lambda e: e.tensor_tensor(out=QB[:], in0=QROT[:], in1=QROT[:], op=A.mult),
                  reads=["qrot"], writes=["qb"])
                V(lambda e: e.tensor_reduce(
                    out=st[:, 16:20], in_=QB[:].rearrange("p (g c) -> p g c", g=4),
                    axis=AXX, op=A.add),
                  reads=["qb"], writes=["st16"])
                if tp == 0:
                    V(lambda e: e.tensor_copy(KN2[:], st[:, 16:20]),
                      reads=["st16"], writes=["kn2"])
                else:
                    V(lambda e: e.tensor_tensor(out=KN2[:], in0=KN2[:],
                                                in1=st[:, 16:20], op=A.max),
                      reads=["kn2", "st16"], writes=["kn2"])
                for g in range(4):
                    transp(psA[:, g * P:(g + 1) * P], QROT[:, g * P:(g + 1) * P],
                           IDf[:], start=(g == 0), stop=(g == 3),
                           reads=["qrot", "IDf"], writes=["psA"])
                V(lambda e, slot=slot: e.tensor_copy(
                    KT[:].rearrange("p (g s c) -> p g s c", g=NG, s=RING)[:, :, slot, :],
                    psA[:].rearrange("p (g c) -> p g c", g=4)),
                  reads=["psA"], writes=[f"kt{slot}"])
                if tp == 0:
                    V(lambda e: e.tensor_copy(
                        KTS[:].rearrange("p (g c) -> p g c", g=NG)[:, :, 0:4],
                        psA[:].rearrange("p (g c) -> p g c", g=4)[:, :, 0:4]),
                      reads=["psA", "KTS"], writes=["KTS"])
                V(lambda e, slot=slot: e.tensor_copy(
                    VA[:].rearrange("p (g s c) -> p g s c", g=NG, s=RING)[:, :, slot, 0:128],
                    ps6[:, 2560:3072].rearrange("p (g c) -> p g c", g=4)),
                  reads=["ps6_5", "vaone"], writes=[f"va{slot}"])
                if tp == 0:
                    V(lambda e: e.tensor_copy(
                        VAS[0:4, :].rearrange("p (g c) -> p g c", g=NG)[:, :, 0:128],
                        ps6[0:4, 2560:3072].rearrange("p (g c) -> p g c", g=4)),
                      reads=["ps6_5", "VAS"], writes=["VAS"])

                if tp >= 5:
                    slots = [("sink", None, MSINK, "MSINK")]
                    for c in range(tp - 4, tp + 1):
                        msk, mn = (MLO, "MLO") if c == tp - 4 else \
                                  ((MHI, "MHI") if c == tp else (None, None))
                        slots.append(("ring", c, msk, mn))
                else:
                    slots = []
                    for c in range(0, tp + 1):
                        if c == tp:
                            msk, mn = MHI, "MHI"
                        elif tp == 4 and c == 0:
                            msk, mn = MLOS, "MLOS"
                        else:
                            msk, mn = None, None
                        slots.append(("ring", c, msk, mn))
                nsl = len(slots)

                transp(psA[0:NH, 0:P], QN2[:], IDf[:], True, True, ["qn2", "IDf"], ["psA"])
                V(lambda e: e.reduce_max(out=st[0:NH, 20:21], in_=psA[0:NH, 0:P], axis=AXX),
                  reads=["psA"], writes=["st20"])
                transp(psA[0:NG, 0:P], KN2[:], IDf[:], True, True, ["kn2", "IDf"], ["psA"])
                V(lambda e: e.reduce_max(out=st[0:NG, 21:22], in_=psA[0:NG, 0:P], axis=AXX),
                  reads=["psA"], writes=["st21"])
                transp(psA[0:1, 0:NH], st[0:NH, 20:21], IDf[0:NH, 0:NH], True, True,
                       ["st20", "IDf"], ["psA"])
                V(lambda e: e.tensor_copy(SCR[0:1, 0:NH], psA[0:1, 0:NH]),
                  reads=["psA"], writes=["scr"])
                transp(psA[0:1, 0:NG], st[0:NG, 21:22], IDf[0:NG, 0:NG], True, True,
                       ["st21", "IDf"], ["psA"])
                V(lambda e: e.tensor_copy(SCR[0:1, 16:16 + NG], psA[0:1, 0:NG]),
                  reads=["psA", "scr"], writes=["scr"])
                V(lambda e: e.tensor_reduce(
                    out=SCR[0:1, 24:28], in_=SCR[0:1, 0:16].rearrange("p (g h) -> p g h", g=4),
                    axis=AXX, op=A.max),
                  reads=["scr"], writes=["scr"])
                V(lambda e: e.tensor_tensor(
                    out=SCR[0:1, 32:36], in0=SCR[0:1, 24:28],
                    in1=SCR[0:1, 16:20], op=A.add),
                  reads=["scr"], writes=["scr"])
                V(lambda e: e.tensor_scalar(out=SCR[0:1, 32:36], in0=SCR[0:1, 32:36],
                                            scalar1=-0.5 * SCALE, scalar2=None, op0=A.mult),
                  reads=["scr"], writes=["scr"])
                mm(psA[:, 0:NG], OnesF[:], SCR[0:1, 32:36], True, True,
                   ["scr", "OnesF"], ["psA"])
                V(lambda e: e.tensor_copy(CB[:, 0:NG], psA[:, 0:NG]), reads=["psA"], writes=["cb"])

                for g in range(NG):
                    order = ([4, 5, 0, 1, 2, 3] if nsl == 6 else list(range(nsl)))
                    for si in order:
                        kind, c, msk, mn = slots[si]
                        if kind == "sink":
                            lhs = KTS[:].rearrange("p (g2 c2) -> p g2 c2", g2=NG)[:, g, :]
                            lrd = ["KTS"]
                        else:
                            lhs = KT[:].rearrange("p (g2 s c2) -> p g2 s c2",
                                                  g2=NG, s=RING)[:, g, c % RING, :]
                            lrd = [f"kt{c % RING}"]
                        pn = f"ps6_{si}"
                        for h in range(4):
                            hh = g * 4 + h
                            mm(ps6[:, si * 512 + h * P: si * 512 + (h + 1) * P],
                               lhs, QT[:, hh * P:(hh + 1) * P],
                               start=(h == 0), stop=(msk is None and h == 3),
                               reads=lrd + ["qt"], writes=[pn])
                        if msk is not None:
                            for h in range(4):
                                mm(ps6[:, si * 512 + h * P: si * 512 + (h + 1) * P],
                                   IDb[:], msk[:], start=False, stop=(h == 3),
                                   reads=["IDb", mn], writes=[pn])
                    S_(lambda e, g=g, nsl=nsl: e.activation(
                        out=EB[:, 0:nsl * 512],
                        in_=ps6[:, 0:nsl * 512],
                        func=AF.Exp, bias=CB[:, g:g + 1], scale=SCALE),
                       reads=[f"ps6_{si}" for si in range(nsl)] + ["cb"],
                       writes=["eb0", "eb1", "eb2", "eb3"])
                    for h in range(4):
                        hh = g * 4 + h
                        psx = (psA if h % 2 == 0 else psB)[:, 0:132]
                        pname = "psA" if h % 2 == 0 else "psB"
                        for si, (kind, c, msk, mn) in enumerate(slots):
                            if kind == "sink":
                                rhs = VAS[:].rearrange("p (g2 c2) -> p g2 c2",
                                                       g2=NG)[:, g, :]
                                rrd = ["VAS"]
                            else:
                                rhs = VA[:].rearrange("p (g2 s c2) -> p g2 s c2",
                                                      g2=NG, s=RING)[:, g, c % RING, :]
                                rrd = [f"va{c % RING}"]
                            mm(psx,
                               EB[:, si * 512 + h * P: si * 512 + (h + 1) * P],
                               rhs, start=(si == 0), stop=(si == nsl - 1),
                               reads=["eb0", "eb1", "eb2", "eb3"] + rrd,
                               writes=[pname])
                        V(lambda e, psx=psx: e.reciprocal(out=RCP[:, 0:1],
                                                          in_=psx[:, 131:132]),
                          reads=[pname], writes=["rcp"])
                        V(lambda e, psx=psx, hh=hh: e.tensor_scalar(
                            out=ATF[:, hh * P:(hh + 1) * P], in0=psx[:, 0:P],
                            scalar1=RCP[:, 0:1], scalar2=None, op0=A.mult),
                          reads=[pname, "rcp"], writes=["atf"])

                for g4 in range(4):
                    for b4 in range(4):
                        k = g4 * 4 + b4
                        transp(psB[:, b4 * P:(b4 + 1) * P], ATF[:, k * P:(k + 1) * P],
                               IDf[:], start=(b4 == 0), stop=(b4 == 3),
                               reads=["atf", "IDf"], writes=["psB"])
                    V(lambda e, g4=g4: e.tensor_copy(ATT[:, g4 * 512:(g4 + 1) * 512], psB[:]),
                      reads=["psB"], writes=["att"])
                    V(lambda e, g4=g4: e.tensor_tensor(
                        out=ATT2[:, g4 * 512:(g4 + 1) * 512], in0=psB[:],
                        in1=ATT[:, g4 * 512:(g4 + 1) * 512], op=A.subtract),
                      reads=["psB", "att"], writes=["att2"])
                dma("G", "gA",
                    sAtT[0, :, jj, :, :].rearrange("k p q -> p k q"),
                    ATT[:].rearrange("p (k q) -> p k q", k=KC),
                    ["att"], ["sAtT"])
                dma("G", "gA",
                    sAtT[1, :, jj, :, :].rearrange("k p q -> p k q"),
                    ATT2[:].rearrange("p (k q) -> p k q", k=KC),
                    ["att2"], ["sAtT"])

            dma("Y", "dB", Brow[0:1, 0:D], sBrows[2:3, 0:D], ["browh", "sBrows"], ["browh"])
            dma("Y", "dB", Brow[0:1, DQKV:DQKV + D], sBrows[3:4, 0:D],
                ["browl", "sBrows"], ["browl"])
            for dc4 in range(4):
                co = dc4 * 512
                for k in range(KC):
                    asl = k % 2
                    ah_, al_ = f"at{asl}h", f"at{asl}l"
                    dma("S", f"dA{asl}", AT[asl][:, 0:512].rearrange("p (t q) -> p t q", t=4),
                        sAtT[0, k, :, :, :].rearrange("t p q -> p t q"),
                        [ah_, "sAtT"], [ah_])
                    dma("Y", f"dAl{asl}", AT[asl][:, 512:1024].rearrange("p (t q) -> p t q", t=4),
                        sAtT[1, k, :, :, :].rearrange("t p q -> p t q"),
                        [al_, "sAtT"], [al_])
                    wsl = k % 2
                    wh_, wl_ = f"wo{wsl}h", f"wo{wsl}l"
                    dma("S", f"dO{wsl}", WO[wsl][:, 0:512],
                        sWoH[k * P:(k + 1) * P, co:co + 512], [wh_, "sWo"], [wh_])
                    dma("Y", f"dOl{wsl}", WO[wsl][:, 512:1024],
                        sWoL[k * P:(k + 1) * P, co:co + 512], [wl_, "sWo"], [wl_])
                    for q4 in range(4):
                        ah = AT[asl][:, q4 * P:(q4 + 1) * P]
                        al = AT[asl][:, 512 + q4 * P:512 + (q4 + 1) * P]
                        bn = f"ps6_{q4}"
                        out = ps6[:, q4 * 512:(q4 + 1) * 512]
                        mm(out, ah, WO[wsl][:, 0:512], (k == 0), False, [ah_, wh_], [bn])
                        mm(out, ah, WO[wsl][:, 512:1024], False, False, [ah_, wl_], [bn])
                        mm(out, al, WO[wsl][:, 0:512], False, False, [al_, wh_], [bn],
                           tick=(q4 == 3))
                for q4 in range(4):
                    bn = f"ps6_{q4}"
                    out = ps6[:, q4 * 512:(q4 + 1) * 512]
                    mm(out, OnesR[:], Brow[0:1, co:co + 512], False, False,
                       ["browh", "OnesR"], [bn])
                    mm(out, OnesR[:], Brow[0:1, DQKV + co:DQKV + co + 512], False, True,
                       ["browl", "OnesR"], [bn])
                    psl = EB[:, q4 * 512:(q4 + 1) * 512]
                    pn2 = f"eb{q4}"
                    tt2 = seq * TPS + sb4[q4]
                    xs = q4 % 2
                    xn2 = f"xt{xs}"
                    dma("Y", f"dX{xs}", XT[xs][:, 0:512],
                        dx[tt2 * P:(tt2 + 1) * P, co:co + 512], [xn2], [xn2])
                    V(lambda e, out=out, psl=psl, xs=xs: e.tensor_tensor(
                        out=psl, in0=out, in1=XT[xs][:, 0:512], op=A.add),
                      reads=[bn, xn2], writes=[pn2])
                    dma("G", "gP", sPP[tt2 * P:(tt2 + 1) * P, co:co + 512],
                        psl, [pn2], ["sPP"])

            dma("Y", "dB", Brow[0:1, 0:D], sBrows[4:5, 0:D], ["browh", "sBrows"], ["browh"])
            dma("Y", "dB", Brow[0:1, DQKV:DQKV + D], sBrows[5:6, 0:D],
                ["browl", "sBrows"], ["browl"])
            for dc4 in range(4):
                co = dc4 * 512
                for k in range(KC):
                    wsl = k % 2
                    wh_, wl_ = f"we{wsl}h", f"we{wsl}l"
                    dma("S", f"dE{wsl}", WE[wsl][:, 0:512],
                        sWeH[k * P:(k + 1) * P, co:co + 512], [wh_, "sWe"], [wh_])
                    dma("Y", f"dEl{wsl}", WE[wsl][:, 512:1024],
                        sWeL[k * P:(k + 1) * P, co:co + 512], [wl_, "sWe"], [wl_])
                    for jj in range(4):
                        xh = XnTH[:, jj * D + k * P:jj * D + (k + 1) * P]
                        xl = XnTL[:, jj * D + k * P:jj * D + (k + 1) * P]
                        bn = f"ps6_{jj}"
                        out = ps6[:, jj * 512:(jj + 1) * 512]
                        mm(out, xh, WE[wsl][:, 0:512], (k == 0), False, ["xnth", wh_], [bn])
                        mm(out, xh, WE[wsl][:, 512:1024], False, False, ["xnth", wl_], [bn])
                        mm(out, xl, WE[wsl][:, 0:512], False, False, ["xntl", wh_], [bn],
                           tick=(jj == 3))
                for jj in range(4):
                    bn = f"ps6_{jj}"
                    out = ps6[:, jj * 512:(jj + 1) * 512]
                    mm(out, OnesR[:], Brow[0:1, co:co + 512], False, False,
                       ["browh", "OnesR"], [bn])
                    mm(out, OnesR[:], Brow[0:1, DQKV + co:DQKV + co + 512], False, True,
                       ["browl", "OnesR"], [bn])
                    psl = EB[:, jj * 512:(jj + 1) * 512]
                    pn2 = f"eb{jj}"
                    V(lambda e, out=out, psl=psl: e.tensor_copy(psl, out),
                      reads=[bn], writes=[pn2])
                    tt2 = seq * TPS + sb4[jj]
                    dma("G", "gF", sFfn[tt2 * P:(tt2 + 1) * P, co:co + 512],
                        psl, [pn2], ["sFfn"])

    if KP < 4:
        with nc.Block() as block:
            sch.realize(block)
        es.close()
        return nc
    sch.emit("G", lambda e: e.collective_compute(
        "AllGather", A.bypass, replica_groups=[list(range(NCORES))],
        ins=[hSin.ap().opt()], outs=[hSout.ap().opt()]),
        reads=["dSin"], writes=["ccout"], sem="cc", amt=1, tick=False)
    dma("Y", "dR", SBLK[:], sSout.rearrange("r e2 f -> (r e2) f"), ["ccout"], ["sblk"])
    V(lambda e: e.memset(LO[:], 0.0), reads=[], writes=["lo"])
    V(lambda e: e.memset(HI[:], 1.0), reads=[], writes=["hi"])
    for _ in range(NBIS):
        V(lambda e: e.tensor_tensor(out=MID[:], in0=LO[:], in1=HI[:], op=A.add),
          reads=["lo", "hi"], writes=["mid"])
        V(lambda e: e.tensor_scalar(out=MID[:], in0=MID[:], scalar1=0.5,
                                    scalar2=None, op0=A.mult),
          reads=["mid"], writes=["mid"])
        for sl in range(4):
            V(lambda e, sl=sl: e.tensor_scalar(
                out=SCR[:, 0:1024], in0=SBLK[:, sl * 1024:(sl + 1) * 1024],
                scalar1=MID[:], scalar2=None, op0=A.is_gt, op1=A.add,
                accum_out=STAT[:, 24 + sl:25 + sl]),
              reads=["sblk", "mid"], writes=["scr", "stc"])
        V(lambda e: e.tensor_reduce(out=STAT[:, 28:29], in_=STAT[:, 24:28],
                                    axis=AXX, op=A.add),
          reads=["stc"], writes=["cnt"])
        mm(psA[:, 0:1], Bmat[:], STAT[:, 28:29], True, True, ["cnt", "Bmat"], ["psA"])
        V(lambda e: e.tensor_scalar(out=STAT[:, 29:30], in0=psA[:, 0:1],
                                    scalar1=float(TOPK), scalar2=None, op0=A.is_ge),
          reads=["psA"], writes=["ge"])
        V(lambda e: e.tensor_tensor(out=STAT[:, 30:31], in0=MID[:], in1=LO[:],
                                    op=A.subtract),
          reads=["mid", "lo"], writes=["t30"])
        V(lambda e: e.tensor_tensor(out=STAT[:, 30:31], in0=STAT[:, 30:31],
                                    in1=STAT[:, 29:30], op=A.mult),
          reads=["t30", "ge"], writes=["t30"])
        V(lambda e: e.tensor_tensor(out=LO[:], in0=LO[:], in1=STAT[:, 30:31], op=A.add),
          reads=["lo", "t30"], writes=["lo"])
        V(lambda e: e.tensor_tensor(out=STAT[:, 31:32], in0=MID[:], in1=HI[:],
                                    op=A.subtract),
          reads=["mid", "hi"], writes=["t31"])
        V(lambda e: e.tensor_tensor(out=STAT[:, 32:33], in0=STAT[:, 31:32],
                                    in1=STAT[:, 29:30], op=A.mult),
          reads=["t31", "ge"], writes=["t32"])
        V(lambda e: e.tensor_tensor(out=STAT[:, 31:32], in0=STAT[:, 31:32],
                                    in1=STAT[:, 32:33], op=A.subtract),
          reads=["t31", "t32"], writes=["t31"])
        V(lambda e: e.tensor_tensor(out=HI[:], in0=HI[:], in1=STAT[:, 31:32], op=A.add),
          reads=["hi", "t31"], writes=["hi"])
    V(lambda e: e.scalar_tensor_tensor(out=SBLK[:], in0=SBLK[:], scalar=LO[:],
                                       in1=SBLK[:], op0=A.is_gt, op1=A.mult),
      reads=["sblk", "lo"], writes=["sblk"])
    for sl in range(8):
        mm(psA[0:8, 0:512], Cmat[:, 0:8], SBLK[:, sl * 512:(sl + 1) * 512],
           True, True, ["sblk", "Cmat"], ["psA"])
        V(lambda e, sl=sl: e.tensor_copy(SCR[0:8, sl * 512:(sl + 1) * 512], psA[0:8, 0:512]),
          reads=["psA"], writes=["scr"])
    for s128 in range(NT):
        mm(psA[:, 0:1], SCR[0:8, s128 * P:(s128 + 1) * P], Oneh[:], True, True,
           ["scr", "Oneh"], ["psA"])
        V(lambda e, s128=s128: e.tensor_copy(WCOL[:, s128:s128 + 1], psA[:, 0:1]),
          reads=["psA"], writes=["wcol"])

    if KP < 5:
        with nc.Block() as block:
            sch.realize(block)
        es.close()
        return nc
    FFHs = [SCR[:, 0:1024], SCR[:, 1024:2048]]
    PPHs = [SCR[:, 2048:3072], SCR[:, 3072:4096]]
    first_touch = {"ffh0": True, "ffh1": True, "pph0": True, "pph1": True}
    for tt in range(NT):
        for h2 in range(2):
            co = h2 * 1024
            sl = (tt * 2 + h2) % 2
            xn_ = f"xt{sl}"
            fn_ = f"ffh{sl}"
            pn_ = f"pph{sl}"
            fw = [fn_] + (["scr"] if first_touch[fn_] else [])
            pw = [pn_] + (["scr"] if first_touch[pn_] else [])
            first_touch[fn_] = first_touch[pn_] = False
            dma("Y", f"dA{sl}", FFHs[sl], sFfn[tt * P:(tt + 1) * P, co:co + 1024],
                [fn_, "sFfn"], fw)
            dma("Y", f"dO{sl}", PPHs[sl], sPP[tt * P:(tt + 1) * P, co:co + 1024],
                [pn_, "sPP"], pw)
            V(lambda e, sl=sl, tt=tt: e.scalar_tensor_tensor(
                out=PPHs[sl], in0=FFHs[sl], scalar=WCOL[:, tt:tt + 1],
                in1=PPHs[sl], op0=A.mult, op1=A.add),
              reads=[fn_, "wcol", pn_], writes=[pn_])
            dma("G", "gO", dOut[tt * P:(tt + 1) * P, co:co + 1024], PPHs[sl],
                [pn_], ["outw"])

    sch.wait_all("G", [("gO", sch.cnt["gO"])])

    with nc.Block() as block:
        sch.realize(block)
    es.close()
    return nc


def _consts():
    import ml_dtypes
    bf = lambda a: np.asarray(a, dtype=ml_dtypes.bfloat16)
    i = np.arange(P)
    u = i[:, None]
    q = i[None, :]
    IDf = np.eye(P, dtype=np.float32)
    MLO = np.where(u >= q, 0.0, NEG).astype(np.float32)
    MLOS = np.where((u >= q) | (u < SINK), 0.0, NEG).astype(np.float32)
    MHI = np.where(u <= q, 0.0, NEG).astype(np.float32)
    MSINK = np.where(np.repeat(u < SINK, P, axis=1), 0.0, NEG).astype(np.float32)
    t = np.arange(T, dtype=np.float64)
    inv = 1.0 / (10000.0 ** (np.arange(DH // 2, dtype=np.float64) * 2.0 / DH))
    ang = t[:, None] * inv[None, :]
    cos = np.cos(ang)
    sin = np.sin(ang)
    ROPC = np.concatenate([cos, cos], axis=1).astype(np.float32)
    ROPS = np.concatenate([-sin, sin], axis=1).astype(np.float32)
    Bm = np.zeros((P, P), dtype=np.float32)
    for p in range(P):
        for qq in range(P):
            if p % 16 == qq % 16:
                Bm[p, qq] = 1.0
    Cm = np.zeros((P, 8), dtype=np.float32)
    for p in range(P):
        Cm[p, p // 16] = 1.0
    return dict(IDf=IDf, IDb=bf(IDf), MLO=bf(MLO), MLOS=bf(MLOS), MHI=bf(MHI),
                MSINK=bf(MSINK), ROPC=ROPC, ROPS=ROPS, Bmat=Bm, Cmat=Cm,
                OnesRow=bf(np.ones((1, P), dtype=np.float32)),
                OnesRowF=np.ones((1, P), dtype=np.float32))


def kernel(L0_attn_x, L0_attn_Wq, L0_attn_bq, L0_attn_Wk, L0_attn_bk,
           L0_attn_Wv, L0_attn_bv, L0_attn_Wo, L0_attn_bo,
           L0_ffn_router_w, L0_ffn_router_b, L0_ffn_expert_w, L0_ffn_expert_b):
    nc = build_program()
    consts = _consts()
    x = np.asarray(L0_attn_x, dtype=np.float32)
    in_maps = []
    for c in range(NCORES):
        oneh = np.zeros((8, 1), dtype=np.float32)
        oneh[c, 0] = 1.0
        m = {
            "x": np.ascontiguousarray(x[c * BPC:(c + 1) * BPC].reshape(BPC * T, D)),
            "Wq": np.asarray(L0_attn_Wq, dtype=np.float32),
            "bq": np.asarray(L0_attn_bq, dtype=np.float32),
            "Wk": np.asarray(L0_attn_Wk, dtype=np.float32),
            "bk": np.asarray(L0_attn_bk, dtype=np.float32),
            "Wv": np.asarray(L0_attn_Wv, dtype=np.float32),
            "bv": np.asarray(L0_attn_bv, dtype=np.float32),
            "Wo": np.asarray(L0_attn_Wo, dtype=np.float32),
            "bo": np.asarray(L0_attn_bo, dtype=np.float32),
            "Wr": np.asarray(L0_ffn_router_w, dtype=np.float32),
            "br": np.asarray(L0_ffn_router_b, dtype=np.float32),
            "We": np.asarray(L0_ffn_expert_w, dtype=np.float32),
            "be": np.asarray(L0_ffn_expert_b, dtype=np.float32),
            "Oneh": oneh,
        }
        m.update(consts)
        in_maps.append(m)
    res = run_bass_kernel_spmd(nc, in_maps, core_ids=list(range(NCORES)))
    outs = [np.asarray(r["out"]).reshape(BPC, T, D) for r in res.results]
    return np.concatenate(outs, axis=0)


if __name__ == "__main__":
    build_program()
    print("built ok")


# version 13 (final)
# speedup vs baseline: 1.0302x; 1.0167x over previous
import os
import numpy as np
from contextlib import ExitStack

import concourse.bass as bass
import concourse.mybir as mb
import bass_rust
from concourse.bass_utils import run_bass_kernel_spmd
from concourse.alu_op_type import AluOpType as A

F32 = mb.dt.float32
BF16 = mb.dt.bfloat16
AXX = mb.AxisListType.X
AF = mb.ActivationFunctionType

P = 128
D = 2048
T = 2048
NCORES = 8
BPC = 2
NT = BPC * T // P
TPS = T // P
KC = D // P
NH, NG, DH = 16, 4, 128
WIN, SINK = 512, 4
NE = 16
BS = 16 * T
TOPK = BS // NE
SCALE = DH ** -0.5
EPS = 1e-5
DQKV = D + 1024
NEG = -1.0e30
RING = 5
NBIS = 34

ENGS = ("T", "V", "S", "G", "Y")


class Sched:

    def __init__(self, nc, es):
        self.nc = nc
        self.es = es
        self.ops = {e: [] for e in ENGS}
        self.sems = {}
        self.cnt = {}
        for e in ENGS:
            self.mksem("tk" + e)
        self.bufs = {}
        self.sat = {e: {} for e in ENGS}
        self.pending = {e: [] for e in ENGS}

    def mksem(self, name):
        self.sems[name] = self.es.enter_context(self.nc.semaphore(name))
        self.cnt[name] = 0
        return name

    def emit(self, eng, fn, reads=(), writes=(), sem=None, amt=1, tick=True,
             extra_waits=()):
        need = {}
        for s, c in extra_waits:
            if need.get(s, 0) < c:
                need[s] = c

        def add(mark):
            if mark is not None and need.get(mark[0], 0) < mark[1]:
                need[mark[0]] = mark[1]

        for b in reads:
            st = self.bufs.get(b)
            if st:
                add(st.get("w"))
        for b in writes:
            st = self.bufs.get(b)
            if st:
                add(st.get("w"))
                for m in st.get("r", ()):
                    add(m)
        sat = self.sat[eng]
        waits = [(s, c) for s, c in need.items() if sat.get(s, 0) < c]
        for s, c in waits:
            sat[s] = c
        if sem is None and tick:
            sem = "tk" + eng
        if sem is not None:
            self.cnt[sem] += amt
            mark = (sem, self.cnt[sem])
        else:
            mark = None
        self.ops[eng].append((waits, fn, sem, amt))
        for b in writes:
            self.bufs[b] = {"w": mark, "r": []}
        if mark is None:
            self.pending[eng].extend(reads)
            self.pending[eng].extend(writes)
        else:
            for b in list(reads) + self.pending[eng]:
                st = self.bufs.setdefault(b, {"w": None, "r": []})
                st["r"].append(mark)
                if st.get("w") is None and b in self.pending[eng]:
                    pass
            for b in self.pending[eng]:
                st = self.bufs.setdefault(b, {"w": None, "r": []})
            self.pending[eng] = []
        return mark

    def wait_all(self, eng, marks):
        sat = self.sat[eng]
        ws = []
        for s, c in marks:
            if sat.get(s, 0) < c:
                ws.append((s, c))
                sat[s] = c
        if ws:
            self.ops[eng].append((ws, None, None, 0))

    def realize(self, block):
        def runner(ename):
            def run(eng):
                for waits, fn, sem, amt in self.ops[ename]:
                    for s, c in waits:
                        eng.wait_ge(self.sems[s], c)
                    if fn is not None:
                        inst = fn(eng)
                        if sem is not None:
                            inst.then_inc(self.sems[sem], amt)
            return run

        block.tensor(runner("T"))
        block.vector(runner("V"))
        block.scalar(runner("S"))
        block.gpsimd(runner("G"))
        block.sync(runner("Y"))


def build_program():
    nc = bass.Bass("TRN2", target_bir_lowering=False)
    es = ExitStack()

    def din(name, shape, dt):
        return nc.dram_tensor(name, shape, dt, kind="ExternalInput").ap()

    dx = din("x", [BPC * T, D], F32)
    dWq = din("Wq", [D, D], F32)
    dbq = din("bq", [D], F32)
    dWk = din("Wk", [D, 512], F32)
    dbk = din("bk", [512], F32)
    dWv = din("Wv", [D, 512], F32)
    dbv = din("bv", [512], F32)
    dWo = din("Wo", [D, D], F32)
    dbo = din("bo", [D], F32)
    dWr = din("Wr", [D, NE], F32)
    dbr = din("br", [NE], F32)
    dWe = din("We", [D, D], F32)
    dbe = din("be", [D], F32)
    dIDf = din("IDf", [P, P], F32)
    dIDb = din("IDb", [P, P], BF16)
    dMLO = din("MLO", [P, P], BF16)
    dMLOS = din("MLOS", [P, P], BF16)
    dMHI = din("MHI", [P, P], BF16)
    dMSINK = din("MSINK", [P, P], BF16)
    dROPC = din("ROPC", [T, DH], F32)
    dROPS = din("ROPS", [T, DH], F32)
    dBmat = din("Bmat", [P, P], F32)
    dCmat = din("Cmat", [P, 8], F32)
    dOneh = din("Oneh", [8, 1], F32)
    dOnes = din("OnesRow", [1, P], BF16)
    dOnesF = din("OnesRowF", [1, P], F32)
    dOut = nc.dram_tensor("out", [BPC * T, D], F32, kind="ExternalOutput").ap()

    if int(os.environ.get("KNOSCR", "0")):
        class _FakeAp:
            def __getitem__(self, *a):
                return None
            def rearrange(self, *a, **k):
                return None
            def ap(self):
                return None
            def opt(self):
                return None
        _f = _FakeAp()
        sWqkvH = sWqkvL = sWoH = sWoL = sWeH = sWeL = _f
        sBrows = sAtT = sFfn = sPP = sSin = sSout = _f
        hSin = hSout = _f
    else:
        sWqkvH = nc.dram_tensor("sWqkvH", [D, DQKV], BF16).ap()
        sWqkvL = nc.dram_tensor("sWqkvL", [D, DQKV], BF16).ap()
        sWoH = nc.dram_tensor("sWoH", [D, D], BF16).ap()
        sWoL = nc.dram_tensor("sWoL", [D, D], BF16).ap()
        sWeH = nc.dram_tensor("sWeH", [D, D], BF16).ap()
        sWeL = nc.dram_tensor("sWeL", [D, D], BF16).ap()
        sBrows = nc.dram_tensor("sBrows", [6, DQKV], BF16).ap()
        sAtT = nc.dram_tensor("sAtT", [2, KC, 4, P, P], BF16).ap()
        KDBG = int(os.environ.get("KDBG", "0"))
        kind = dict(kind="ExternalOutput") if KDBG else {}
        sFfn = nc.dram_tensor("sFfn", [NT * P, D], F32, **kind).ap()
        sPP = nc.dram_tensor("sPP", [NT * P, D], F32, **kind).ap()
        hSin = nc.dram_tensor("sSin", [NE, BPC * T], F32, **kind)
        hSout = nc.dram_tensor("sSout", [NCORES, NE, BPC * T], F32)
        sSin = hSin.ap()
        sSout = hSout.ap()

    if int(os.environ.get("KBARE", "0")):
        with nc.sbuf_tensor("bx", [64, 64], F32) as bx, \
             nc.semaphore("bd") as bd, nc.semaphore("bg") as bg, nc.Block() as block:
            @block.sync
            def _(sync):
                sync.dma_start(out=bx[:], in_=dx[0:64, 0:64]).then_inc(bd, 16)
            @block.gpsimd
            def _(g):
                g.wait_ge(bd, 16)
                g.dma_start(out=dOut[0:64, 0:64], in_=bx[:]).then_inc(bg, 16)
                g.wait_ge(bg, 16)
        return nc
    KSB = int(os.environ.get("KSB", "0"))
    def sb(name, shape, dt):
        if KSB:
            shape = [1, 4]
        return es.enter_context(nc.sbuf_tensor("b_" + name, shape, dt))

    IDf = sb("IDf", [P, P], F32)
    IDb = sb("IDb", [P, P], BF16)
    MLO = sb("MLO", [P, P], BF16)
    MLOS = sb("MLOS", [P, P], BF16)
    MHI = sb("MHI", [P, P], BF16)
    MSINK = sb("MSINK", [P, P], BF16)
    Bmat = sb("Bmat", [P, P], F32)
    Cmat = sb("Cmat", [P, 8], F32)
    Oneh = sb("Oneh", [8, 1], F32)
    OnesR = sb("OnesR", [1, P], BF16)
    OnesF = sb("OnesF", [1, P], F32)
    RopC = sb("RopC", [P, 4 * DH], F32)
    RopS = sb("RopS", [P, 4 * DH], F32)
    WrH = sb("WrH", [P, KC * NE], BF16)
    WrL = sb("WrL", [P, KC * NE], BF16)
    BrowR = sb("BrowR", [1, 2 * NE], BF16)
    Brow = sb("Brow", [1, 2 * DQKV], BF16)
    WB = [sb(f"WB{i}", [P, 2 * DQKV], BF16) for i in range(2)]
    WO = [sb(f"WOs{i}", [P, 1024], BF16) for i in range(2)]
    WE = [sb(f"WEs{i}", [P, 1024], BF16) for i in range(2)]
    AT = [sb(f"ATs{i}", [P, 1024], BF16) for i in range(2)]
    XT = [sb(f"XTs{i}", [P, D], F32) for i in range(2)]
    SCR = sb("SCR", [P, 2 * D], F32)
    XnTH = sb("XnTH", [P, 4 * D], BF16)
    XnTL = sb("XnTL", [P, 4 * D], BF16)
    QB = sb("QB", [P, 512], F32)
    QROT = sb("QROT", [P, 512], F32)
    QT = sb("QT", [P, NH * P], F32)
    KT = sb("KT", [P, NG * RING * P], F32)
    KTS = sb("KTS", [P, NG * P], F32)
    VA = sb("VA", [P, NG * RING * 132], F32)
    VAS = sb("VAS", [P, NG * 132], F32)
    EB = sb("EB", [P, 4 * 6 * P], F32)
    ATF = sb("ATF", [P, D], F32)
    ATT = sb("ATT", [P, D], BF16)
    ATT2 = sb("ATT2", [P, D], BF16)
    STAT = sb("STAT", [P, 40], F32)
    QN2 = sb("QN2", [P, NH], F32)
    KN2 = sb("KN2", [P, NG], F32)
    CB = sb("CB", [P, NH], F32)
    RCP = sb("RCP", [P, 2], F32)
    STILE = sb("STILE", [P, NE], F32)
    SBLK = sb("SBLK", [P, 4096], F32)
    LO = sb("LO", [P, 1], F32)
    HI = sb("HI", [P, 1], F32)
    MID = sb("MID", [P, 1], F32)
    WCOL = sb("WCOL", [P, NT], F32)

    ps6 = es.enter_context(nc.psum_tensor("ps6", [P, 3072], F32))
    psA = es.enter_context(nc.psum_tensor("psA", [P, 512], F32))
    psB = es.enter_context(nc.psum_tensor("psB", [P, 512], F32))

    sch = Sched(nc, es)
    for s in (["dC", "dW0", "dW1", "dWb0", "dWb1", "gWl0", "gWl1", "gWlb0", "gWlb1",
               "dX0", "dX1", "dR", "dB"]
              + [f"d{p}{i}" for p in ("A", "Al", "O", "Ol", "E", "El") for i in range(4)]
              + ["gS", "gA", "gF", "gP", "gO", "gW", "cc"]):
        sch.mksem(s)

    def dma(eng, sem, out, in_, reads, writes):
        pw = [(sem, sch.cnt[sem])] if sch.cnt[sem] > 0 else []
        sch.emit(eng, lambda e, o=out, i=in_: e.dma_start(out=o, in_=i),
                 reads=reads, writes=writes, sem=sem, amt=16, tick=False,
                 extra_waits=pw)

    def mm(out, lhsT, rhs, start, stop, reads, writes, tick=None):
        sch.emit("T", lambda e, o=out, l=lhsT, r=rhs, st=start, sp=stop:
                 e.matmul(o, l, r, start=st, stop=sp),
                 reads=reads, writes=writes, tick=(stop if tick is None else tick))

    def transp(out, in_, ident, start, stop, reads, writes):
        sch.emit("T", lambda e, o=out, i=in_, d=ident, st=start, sp=stop:
                 e.matmul(o, i, d, start=st, stop=sp, is_transpose=True),
                 reads=reads, writes=writes, tick=stop)

    def V(fn, reads=(), writes=()):
        sch.emit("V", fn, reads=reads, writes=writes)

    def S_(fn, reads=(), writes=()):
        sch.emit("S", fn, reads=reads, writes=writes)

    def apf(t, off, dims):
        base = t[:]
        return bass.AP(base.tensor, off, [list(base.ap[0])] + [list(d) for d in dims])

    KNC = int(os.environ.get("KNC", "99"))
    for buf, src, nm in [(IDf, dIDf, "IDf"), (IDb, dIDb, "IDb"), (MLO, dMLO, "MLO"),
                         (MLOS, dMLOS, "MLOS"), (MHI, dMHI, "MHI"), (MSINK, dMSINK, "MSINK"),
                         (Bmat, dBmat, "Bmat"), (Cmat, dCmat, "Cmat"), (Oneh, dOneh, "Oneh"),
                         (OnesR, dOnes, "OnesR"), (OnesF, dOnesF, "OnesF")][:KNC]:
        dma("Y", "dC", buf[:], src, [], [nm])

    KST = int(os.environ.get("KST", "9"))
    def split_weight(src_ap, dst_h, dst_l, ncols, dname):
        if KST < 1:
            return
        for k in range(KC):
            xb = XT[k % 2]
            xn_ = f"xt{k % 2}"
            dma("Y", f"dX{k % 2}", xb[:, 0:ncols], src_ap[k * P:(k + 1) * P, :], [xn_], [xn_])
            V(lambda e, xb=xb, n=ncols: e.tensor_copy(ATT[:, 0:n], xb[:, 0:n]),
              reads=[xn_], writes=["att"])
            V(lambda e, xb=xb, n=ncols: e.tensor_tensor(
                out=ATT2[:, 0:n], in0=xb[:, 0:n], in1=ATT[:, 0:n], op=A.subtract),
              reads=[xn_, "att"], writes=["att2"])
            dma("G", "gW", dst_h[k * P:(k + 1) * P, :], ATT[:, 0:ncols], ["att"], [dname])
            dma("G", "gW", dst_l[k * P:(k + 1) * P, :], ATT2[:, 0:ncols], ["att2"], [dname])

    split_weight(dWq, sWqkvH[:, 0:D], sWqkvL[:, 0:D], D, "sWqkv")
    split_weight(dWk, sWqkvH[:, D:D + 512], sWqkvL[:, D:D + 512], 512, "sWqkv")
    split_weight(dWv, sWqkvH[:, D + 512:DQKV], sWqkvL[:, D + 512:DQKV], 512, "sWqkv")
    split_weight(dWo, sWoH, sWoL, D, "sWo")
    split_weight(dWe, sWeH, sWeL, D, "sWe")

    for k in (range(KC) if KST >= 2 else []):
        dma("Y", "dR", SCR[:, 0:NE], dWr[k * P:(k + 1) * P, :], ["scr"], ["scr"])
        V(lambda e, k=k: e.tensor_copy(WrH[:, k * NE:(k + 1) * NE], SCR[:, 0:NE]),
          reads=["scr"], writes=["WrH"])
        V(lambda e, k=k: e.tensor_tensor(
            out=WrL[:, k * NE:(k + 1) * NE], in0=SCR[:, 0:NE],
            in1=WrH[:, k * NE:(k + 1) * NE], op=A.subtract),
          reads=["scr", "WrH"], writes=["WrL"])

    def split_bias(parts, row_h, row_l):
        if KST < 3:
            return
        for ap1, ncols, off in parts:
            dma("Y", "dB", SCR[0:1, off:off + ncols],
                ap1.rearrange("(a n) -> a n", a=1), ["scr"], ["scr"])
        V(lambda e: e.tensor_copy(Brow[0:1, 0:DQKV], SCR[0:1, 0:DQKV]),
          reads=["scr"], writes=["browh"])
        V(lambda e: e.tensor_tensor(
            out=Brow[0:1, DQKV:2 * DQKV], in0=SCR[0:1, 0:DQKV],
            in1=Brow[0:1, 0:DQKV], op=A.subtract),
          reads=["scr", "browh"], writes=["browl"])
        dma("G", "gW", row_h, Brow[0:1, 0:DQKV], ["browh"], ["sBrows"])
        dma("G", "gW", row_l, Brow[0:1, DQKV:2 * DQKV], ["browl"], ["sBrows"])

    split_bias([(dbq, D, 0), (dbk, 512, D), (dbv, 512, D + 512)], sBrows[0:1, :], sBrows[1:2, :])
    split_bias([(dbo, D, 0)], sBrows[2:3, :], sBrows[3:4, :])
    split_bias([(dbe, D, 0)], sBrows[4:5, :], sBrows[5:6, :])
    if KST >= 3:
        dma("Y", "dB", SCR[0:1, 0:NE], dbr.rearrange("(a n) -> a n", a=1), ["scr"], ["scr"])
        V(lambda e: e.tensor_copy(BrowR[0:1, 0:NE], SCR[0:1, 0:NE]),
          reads=["scr"], writes=["browr"])
        V(lambda e: e.tensor_tensor(
            out=BrowR[0:1, NE:2 * NE], in0=SCR[0:1, 0:NE],
            in1=BrowR[0:1, 0:NE], op=A.subtract),
          reads=["scr", "browr"], writes=["browr"])

    if KST < 4:
        with nc.Block() as block:
            sch.realize(block)
        es.close()
        return nc
    V(lambda e: e.memset(VA[:], 0.0), reads=[], writes=["va0", "va1", "va2", "va3", "va4", "vaone"])
    V(lambda e: e.memset(
        VA[:].rearrange("p (gs c) -> p gs c", c=132)[:, :, 131:132], 1.0),
      reads=["vaone"], writes=["vaone"])
    V(lambda e: e.memset(VAS[:], 0.0), reads=[], writes=["VAS"])
    V(lambda e: e.memset(
        VAS[:].rearrange("p (g c) -> p g c", c=132)[:, :, 131:132], 1.0),
      reads=["VAS"], writes=["VAS"])
    V(lambda e: e.memset(KTS[:], 0.0), reads=[], writes=["KTS"])

    KP = int(os.environ.get("KP", "9"))
    NSEQ = 0 if KP < 2 else (1 if KP == 2 else BPC)
    for seq in range(NSEQ):
        tok0 = seq * T
        for sbi in range(TPS // 4):
            sb4 = [sbi * 4 + j for j in range(4)]
            dma("Y", "dR", RopC[:].rearrange("p (j c) -> p j c", j=4),
                dROPC.rearrange("(j p) c -> p j c", p=P)[:, sbi * 4:(sbi + 1) * 4, :],
                ["ropC"], ["ropC"])
            dma("Y", "dR", RopS[:].rearrange("p (j c) -> p j c", j=4),
                dROPS.rearrange("(j p) c -> p j c", p=P)[:, sbi * 4:(sbi + 1) * 4, :],
                ["ropS"], ["ropS"])

            dma("Y", "dB", Brow[0:1, 0:DQKV], sBrows[0:1, :], ["browh", "sBrows"], ["browh"])
            dma("Y", "dB", Brow[0:1, DQKV:2 * DQKV], sBrows[1:2, :],
                ["browl", "sBrows"], ["browl"])
            for jj, tp in enumerate(sb4):
                tt = seq * TPS + tp
                xslot = tt % 2
                xn_ = f"xt{xslot}"
                dma("Y", f"dX{xslot}", XT[xslot][:],
                    dx[tok0 + tp * P: tok0 + (tp + 1) * P, :], [xn_], [xn_])
                st = STAT
                V(lambda e, xb=XT[xslot]: e.reduce_sum(out=st[:, 0:1], in_=xb[:], axis=AXX),
                  reads=[xn_], writes=["st"])
                S_(lambda e, xb=XT[xslot]: e.activation(
                    out=SCR[:, 0:D], in_=xb[:], func=AF.Square, scale=1.0,
                    accum_out=st[:, 1:2]),
                   reads=[xn_], writes=["scr", "st"])
                V(lambda e: e.tensor_scalar(out=st[:, 2:3], in0=st[:, 0:1],
                                            scalar1=1.0 / D, scalar2=None, op0=A.mult),
                  reads=["st"], writes=["st"])
                V(lambda e: e.tensor_tensor(out=st[:, 3:4], in0=st[:, 2:3],
                                            in1=st[:, 2:3], op=A.mult),
                  reads=["st"], writes=["st"])
                V(lambda e: e.tensor_scalar(out=st[:, 4:5], in0=st[:, 1:2],
                                            scalar1=1.0 / D, scalar2=None, op0=A.mult),
                  reads=["st"], writes=["st"])
                V(lambda e: e.tensor_tensor(out=st[:, 5:6], in0=st[:, 4:5],
                                            in1=st[:, 3:4], op=A.subtract),
                  reads=["st"], writes=["st"])
                V(lambda e: e.tensor_scalar(out=st[:, 6:7], in0=st[:, 5:6],
                                            scalar1=EPS, scalar2=None, op0=A.add),
                  reads=["st"], writes=["st"])
                V(lambda e: e.memset(st[:, 7:8], 1.0), reads=[], writes=["sty"])
                for _ in range(6):
                    V(lambda e: e.tensor_tensor(out=st[:, 8:9], in0=st[:, 7:8],
                                                in1=st[:, 7:8], op=A.mult),
                      reads=["sty"], writes=["st8"])
                    V(lambda e: e.tensor_tensor(out=st[:, 8:9], in0=st[:, 8:9],
                                                in1=st[:, 6:7], op=A.mult),
                      reads=["st8", "st"], writes=["st8"])
                    V(lambda e: e.tensor_scalar(out=st[:, 8:9], in0=st[:, 8:9],
                                                scalar1=-0.5, scalar2=1.5,
                                                op0=A.mult, op1=A.add),
                      reads=["st8"], writes=["st8"])
                    V(lambda e: e.tensor_tensor(out=st[:, 7:8], in0=st[:, 7:8],
                                                in1=st[:, 8:9], op=A.mult),
                      reads=["sty", "st8"], writes=["sty"])
                V(lambda e: e.tensor_tensor(out=st[:, 9:10], in0=st[:, 2:3],
                                            in1=st[:, 7:8], op=A.mult),
                  reads=["st", "sty"], writes=["st9"])
                V(lambda e: e.tensor_scalar(out=st[:, 9:10], in0=st[:, 9:10],
                                            scalar1=-1.0, scalar2=None, op0=A.mult),
                  reads=["st9"], writes=["st9"])
                V(lambda e, xb=XT[xslot]: e.tensor_scalar(
                    out=SCR[:, 0:D], in0=xb[:], scalar1=st[:, 7:8], scalar2=st[:, 9:10],
                    op0=A.mult, op1=A.add),
                  reads=[xn_, "sty", "st9"], writes=["scr"])
                for g4 in range(4):
                    for b4 in range(4):
                        k = g4 * 4 + b4
                        transp(psB[:, b4 * P:(b4 + 1) * P], SCR[:, k * P:(k + 1) * P],
                               IDf[:], start=(b4 == 0), stop=(b4 == 3),
                               reads=["scr", "IDf"], writes=["psB"])
                    V(lambda e, g4=g4, jj=jj: e.tensor_copy(
                        XnTH[:, jj * D + g4 * 512:jj * D + (g4 + 1) * 512], psB[:]),
                      reads=["psB"], writes=["xnth"])
                    V(lambda e, g4=g4, jj=jj: e.tensor_tensor(
                        out=XnTL[:, jj * D + g4 * 512:jj * D + (g4 + 1) * 512],
                        in0=psB[:],
                        in1=XnTH[:, jj * D + g4 * 512:jj * D + (g4 + 1) * 512],
                        op=A.subtract),
                      reads=["psB", "xnth"], writes=["xntl"])

                for k in range(KC):
                    ws = k % 2
                    whA, whB = f"wb{ws}hA", f"wb{ws}hB"
                    wlA, wlB = f"wb{ws}lA", f"wb{ws}lB"
                    dma("S", f"dW{ws}", WB[ws][:, 0:1536],
                        sWqkvH[k * P:(k + 1) * P, 0:1536], [whA, "sWqkv"], [whA])
                    dma("S", f"dWb{ws}", WB[ws][:, 1536:DQKV],
                        sWqkvH[k * P:(k + 1) * P, 1536:DQKV], [whB, "sWqkv"], [whB])
                    dma("G", f"gWl{ws}", WB[ws][:, DQKV:DQKV + 1536],
                        sWqkvL[k * P:(k + 1) * P, 0:1536], [wlA, "sWqkv"], [wlA])
                    dma("G", f"gWlb{ws}", WB[ws][:, DQKV + 1536:2 * DQKV],
                        sWqkvL[k * P:(k + 1) * P, 1536:DQKV], [wlB, "sWqkv"], [wlB])
                    xh = XnTH[:, jj * D + k * P:jj * D + (k + 1) * P]
                    xl = XnTL[:, jj * D + k * P:jj * D + (k + 1) * P]
                    st0 = (k == 0)
                    for dc in range(6):
                        wh_ = whA if dc < 3 else whB
                        wl_ = wlA if dc < 3 else wlB
                        wh = WB[ws][:, dc * 512:(dc + 1) * 512]
                        wl = WB[ws][:, DQKV + dc * 512:DQKV + (dc + 1) * 512]
                        pn = f"ps6_{dc}"
                        mm(ps6[:, dc * 512:(dc + 1) * 512], xh, wh, st0, False,
                           ["xnth", wh_], [pn])
                        mm(ps6[:, dc * 512:(dc + 1) * 512], xh, wl, False, False,
                           ["xnth", wl_], [pn])
                    mm(psA[:, 0:NE], xh, WrH[:, k * NE:(k + 1) * NE], st0, False,
                       ["xnth", "WrH"], ["psA"])
                    mm(psA[:, 0:NE], xh, WrL[:, k * NE:(k + 1) * NE], False, False,
                       ["xnth", "WrL"], ["psA"])
                    for dc in range(6):
                        wh_ = whA if dc < 3 else whB
                        wh = WB[ws][:, dc * 512:(dc + 1) * 512]
                        pn = f"ps6_{dc}"
                        mm(ps6[:, dc * 512:(dc + 1) * 512], xl, wh, False, False,
                           ["xntl", wh_], [pn])
                    mm(psA[:, 0:NE], xl, WrH[:, k * NE:(k + 1) * NE], False, False,
                       ["xntl", "WrH"], ["psA"], tick=True)
                for dc in range(6):
                    pn = f"ps6_{dc}"
                    mm(ps6[:, dc * 512:(dc + 1) * 512], OnesR[:],
                       Brow[0:1, dc * 512:(dc + 1) * 512], False, False,
                       ["browh", "OnesR"], [pn])
                    mm(ps6[:, dc * 512:(dc + 1) * 512], OnesR[:],
                       Brow[0:1, DQKV + dc * 512:DQKV + (dc + 1) * 512], False, True,
                       ["browl", "OnesR"], [pn])
                mm(psA[:, 0:NE], OnesR[:], BrowR[0:1, 0:NE], False, False,
                   ["browr", "OnesR"], ["psA"])
                mm(psA[:, 0:NE], OnesR[:], BrowR[0:1, NE:2 * NE], False, True,
                   ["browr", "OnesR"], ["psA"])

                V(lambda e: e.reduce_max(out=st[:, 10:11], in_=psA[:, 0:NE],
                                         axis=AXX, negate=True),
                  reads=["psA"], writes=["st10"])
                S_(lambda e: e.activation(out=STILE[:], in_=psA[:, 0:NE], func=AF.Exp,
                                          bias=st[:, 10:11], scale=1.0,
                                          accum_out=st[:, 11:12]),
                   reads=["psA", "st10"], writes=["stile", "st11"])
                V(lambda e: e.reciprocal(out=st[:, 12:13], in_=st[:, 11:12]),
                  reads=["st11"], writes=["st12"])
                V(lambda e: e.tensor_scalar(out=STILE[:], in0=STILE[:],
                                            scalar1=st[:, 12:13], scalar2=None, op0=A.mult),
                  reads=["stile", "st12"], writes=["stile"])
                transp(psA[0:NE, 0:P], STILE[:], IDf[:], True, True,
                       ["stile", "IDf"], ["psA"])
                V(lambda e: e.tensor_copy(SCR[0:NE, 0:P], psA[0:NE, 0:P]),
                  reads=["psA"], writes=["scr"])
                dma("G", "gS", sSin[:, tt * P:(tt + 1) * P], SCR[0:NE, 0:P],
                    ["scr"], ["dSin"])

                slot = tp % RING
                co_c = jj * DH
                for ch in range(4):
                    pq = ps6[:, ch * 512:(ch + 1) * 512]
                    pn = f"ps6_{ch}"
                    cb_ap = apf(RopC, co_c, [[0, 4], [1, DH]])
                    sb_lo = apf(RopS, co_c, [[0, 4], [1, 64]])
                    sb_hi = apf(RopS, co_c + 64, [[0, 4], [1, 64]])
                    V(lambda e, pq=pq, cb_ap=cb_ap: e.tensor_tensor(
                        out=QROT[:].rearrange("p (h c) -> p h c", h=4),
                        in0=pq.rearrange("p (h c) -> p h c", h=4),
                        in1=cb_ap, op=A.mult),
                      reads=[pn, "ropC"], writes=["qrot"])
                    V(lambda e, pq=pq, sb_lo=sb_lo: e.tensor_tensor(
                        out=QB[:].rearrange("p (h c) -> p h c", h=4)[:, :, 0:64],
                        in0=pq.rearrange("p (h c) -> p h c", h=4)[:, :, 64:128],
                        in1=sb_lo, op=A.mult),
                      reads=[pn, "ropS"], writes=["qb"])
                    V(lambda e, pq=pq, sb_hi=sb_hi: e.tensor_tensor(
                        out=QB[:].rearrange("p (h c) -> p h c", h=4)[:, :, 64:128],
                        in0=pq.rearrange("p (h c) -> p h c", h=4)[:, :, 0:64],
                        in1=sb_hi, op=A.mult),
                      reads=[pn, "ropS"], writes=["qb"])
                    V(lambda e: e.tensor_tensor(out=QROT[:], in0=QROT[:], in1=QB[:], op=A.add),
                      reads=["qrot", "qb"], writes=["qrot"])
                    V(lambda e: e.tensor_tensor(out=QB[:], in0=QROT[:], in1=QROT[:], op=A.mult),
                      reads=["qrot"], writes=["qb"])
                    V(lambda e, ch=ch: e.tensor_reduce(
                        out=QN2[:, ch * 4:(ch + 1) * 4],
                        in_=QB[:].rearrange("p (h c) -> p h c", h=4), axis=AXX, op=A.add),
                      reads=["qb"], writes=["qn2"])
                    for b4 in range(4):
                        transp(psA[:, b4 * P:(b4 + 1) * P], QROT[:, b4 * P:(b4 + 1) * P],
                               IDf[:], start=(b4 == 0), stop=(b4 == 3),
                               reads=["qrot", "IDf"], writes=["psA"])
                    V(lambda e, ch=ch: e.tensor_copy(QT[:, ch * 512:(ch + 1) * 512], psA[:]),
                      reads=["psA"], writes=["qt"])
                pk = ps6[:, 2048:2560]
                cb_ap = apf(RopC, co_c, [[0, 4], [1, DH]])
                sb_lo = apf(RopS, co_c, [[0, 4], [1, 64]])
                sb_hi = apf(RopS, co_c + 64, [[0, 4], [1, 64]])
                V(lambda e, cb_ap=cb_ap: e.tensor_tensor(
                    out=QROT[:].rearrange("p (h c) -> p h c", h=4),
                    in0=pk.rearrange("p (h c) -> p h c", h=4),
                    in1=cb_ap, op=A.mult),
                  reads=["ps6_4", "ropC"], writes=["qrot"])
                V(lambda e, sb_lo=sb_lo: e.tensor_tensor(
                    out=QB[:].rearrange("p (h c) -> p h c", h=4)[:, :, 0:64],
                    in0=pk.rearrange("p (h c) -> p h c", h=4)[:, :, 64:128],
                    in1=sb_lo, op=A.mult),
                  reads=["ps6_4", "ropS"], writes=["qb"])
                V(lambda e, sb_hi=sb_hi: e.tensor_tensor(
                    out=QB[:].rearrange("p (h c) -> p h c", h=4)[:, :, 64:128],
                    in0=pk.rearrange("p (h c) -> p h c", h=4)[:, :, 0:64],
                    in1=sb_hi, op=A.mult),
                  reads=["ps6_4", "ropS"], writes=["qb"])
                V(lambda e: e.tensor_tensor(out=QROT[:], in0=QROT[:], in1=QB[:], op=A.add),
                  reads=["qrot", "qb"], writes=["qrot"])
                V(lambda e: e.tensor_tensor(out=QB[:], in0=QROT[:], in1=QROT[:], op=A.mult),
                  reads=["qrot"], writes=["qb"])
                V(lambda e: e.tensor_reduce(
                    out=st[:, 16:20], in_=QB[:].rearrange("p (g c) -> p g c", g=4),
                    axis=AXX, op=A.add),
                  reads=["qb"], writes=["st16"])
                if tp == 0:
                    V(lambda e: e.tensor_copy(KN2[:], st[:, 16:20]),
                      reads=["st16"], writes=["kn2"])
                else:
                    V(lambda e: e.tensor_tensor(out=KN2[:], in0=KN2[:],
                                                in1=st[:, 16:20], op=A.max),
                      reads=["kn2", "st16"], writes=["kn2"])
                for g in range(4):
                    transp(psA[:, g * P:(g + 1) * P], QROT[:, g * P:(g + 1) * P],
                           IDf[:], start=(g == 0), stop=(g == 3),
                           reads=["qrot", "IDf"], writes=["psA"])
                V(lambda e, slot=slot: e.tensor_copy(
                    KT[:].rearrange("p (g s c) -> p g s c", g=NG, s=RING)[:, :, slot, :],
                    psA[:].rearrange("p (g c) -> p g c", g=4)),
                  reads=["psA"], writes=[f"kt{slot}"])
                if tp == 0:
                    V(lambda e: e.tensor_copy(
                        KTS[:].rearrange("p (g c) -> p g c", g=NG)[:, :, 0:4],
                        psA[:].rearrange("p (g c) -> p g c", g=4)[:, :, 0:4]),
                      reads=["psA", "KTS"], writes=["KTS"])
                V(lambda e, slot=slot: e.tensor_copy(
                    VA[:].rearrange("p (g s c) -> p g s c", g=NG, s=RING)[:, :, slot, 0:128],
                    ps6[:, 2560:3072].rearrange("p (g c) -> p g c", g=4)),
                  reads=["ps6_5", "vaone"], writes=[f"va{slot}"])
                if tp == 0:
                    V(lambda e: e.tensor_copy(
                        VAS[0:4, :].rearrange("p (g c) -> p g c", g=NG)[:, :, 0:128],
                        ps6[0:4, 2560:3072].rearrange("p (g c) -> p g c", g=4)),
                      reads=["ps6_5", "VAS"], writes=["VAS"])

                if tp >= 5:
                    slots = [("sink", None, MSINK, "MSINK")]
                    for c in range(tp - 4, tp + 1):
                        msk, mn = (MLO, "MLO") if c == tp - 4 else \
                                  ((MHI, "MHI") if c == tp else (None, None))
                        slots.append(("ring", c, msk, mn))
                else:
                    slots = []
                    for c in range(0, tp + 1):
                        if c == tp:
                            msk, mn = MHI, "MHI"
                        elif tp == 4 and c == 0:
                            msk, mn = MLOS, "MLOS"
                        else:
                            msk, mn = None, None
                        slots.append(("ring", c, msk, mn))
                nsl = len(slots)

                transp(psA[0:NH, 0:P], QN2[:], IDf[:], True, True, ["qn2", "IDf"], ["psA"])
                V(lambda e: e.reduce_max(out=st[0:NH, 20:21], in_=psA[0:NH, 0:P], axis=AXX),
                  reads=["psA"], writes=["st20"])
                transp(psA[0:NG, 0:P], KN2[:], IDf[:], True, True, ["kn2", "IDf"], ["psA"])
                V(lambda e: e.reduce_max(out=st[0:NG, 21:22], in_=psA[0:NG, 0:P], axis=AXX),
                  reads=["psA"], writes=["st21"])
                transp(psA[0:1, 0:NH], st[0:NH, 20:21], IDf[0:NH, 0:NH], True, True,
                       ["st20", "IDf"], ["psA"])
                V(lambda e: e.tensor_copy(SCR[0:1, 0:NH], psA[0:1, 0:NH]),
                  reads=["psA"], writes=["scr"])
                transp(psA[0:1, 0:NG], st[0:NG, 21:22], IDf[0:NG, 0:NG], True, True,
                       ["st21", "IDf"], ["psA"])
                V(lambda e: e.tensor_copy(SCR[0:1, 16:16 + NG], psA[0:1, 0:NG]),
                  reads=["psA", "scr"], writes=["scr"])
                V(lambda e: e.tensor_reduce(
                    out=SCR[0:1, 24:28], in_=SCR[0:1, 0:16].rearrange("p (g h) -> p g h", g=4),
                    axis=AXX, op=A.max),
                  reads=["scr"], writes=["scr"])
                V(lambda e: e.tensor_tensor(
                    out=SCR[0:1, 32:36], in0=SCR[0:1, 24:28],
                    in1=SCR[0:1, 16:20], op=A.add),
                  reads=["scr"], writes=["scr"])
                V(lambda e: e.tensor_scalar(out=SCR[0:1, 32:36], in0=SCR[0:1, 32:36],
                                            scalar1=-0.5 * SCALE, scalar2=None, op0=A.mult),
                  reads=["scr"], writes=["scr"])
                mm(psA[:, 0:NG], OnesF[:], SCR[0:1, 32:36], True, True,
                   ["scr", "OnesF"], ["psA"])
                V(lambda e: e.tensor_copy(CB[:, 0:NG], psA[:, 0:NG]), reads=["psA"], writes=["cb"])

                for g in range(NG):
                    order = ([4, 5, 0, 1, 2, 3] if nsl == 6 else list(range(nsl)))
                    for si in order:
                        kind, c, msk, mn = slots[si]
                        if kind == "sink":
                            lhs = KTS[:].rearrange("p (g2 c2) -> p g2 c2", g2=NG)[:, g, :]
                            lrd = ["KTS"]
                        else:
                            lhs = KT[:].rearrange("p (g2 s c2) -> p g2 s c2",
                                                  g2=NG, s=RING)[:, g, c % RING, :]
                            lrd = [f"kt{c % RING}"]
                        pn = f"ps6_{si}"
                        for h in range(4):
                            hh = g * 4 + h
                            mm(ps6[:, si * 512 + h * P: si * 512 + (h + 1) * P],
                               lhs, QT[:, hh * P:(hh + 1) * P],
                               start=(h == 0), stop=(msk is None and h == 3),
                               reads=lrd + ["qt"], writes=[pn])
                        if msk is not None:
                            for h in range(4):
                                mm(ps6[:, si * 512 + h * P: si * 512 + (h + 1) * P],
                                   IDb[:], msk[:], start=False, stop=(h == 3),
                                   reads=["IDb", mn], writes=[pn])
                    S_(lambda e, g=g, nsl=nsl: e.activation(
                        out=EB[:, 0:nsl * 512],
                        in_=ps6[:, 0:nsl * 512],
                        func=AF.Exp, bias=CB[:, g:g + 1], scale=SCALE),
                       reads=[f"ps6_{si}" for si in range(nsl)] + ["cb"],
                       writes=["eb0", "eb1", "eb2", "eb3"])
                    for h in range(4):
                        hh = g * 4 + h
                        psx = (psA if h % 2 == 0 else psB)[:, 0:132]
                        pname = "psA" if h % 2 == 0 else "psB"
                        for si, (kind, c, msk, mn) in enumerate(slots):
                            if kind == "sink":
                                rhs = VAS[:].rearrange("p (g2 c2) -> p g2 c2",
                                                       g2=NG)[:, g, :]
                                rrd = ["VAS"]
                            else:
                                rhs = VA[:].rearrange("p (g2 s c2) -> p g2 s c2",
                                                      g2=NG, s=RING)[:, g, c % RING, :]
                                rrd = [f"va{c % RING}"]
                            mm(psx,
                               EB[:, si * 512 + h * P: si * 512 + (h + 1) * P],
                               rhs, start=(si == 0), stop=(si == nsl - 1),
                               reads=["eb0", "eb1", "eb2", "eb3"] + rrd,
                               writes=[pname])
                        V(lambda e, psx=psx: e.reciprocal(out=RCP[:, 0:1],
                                                          in_=psx[:, 131:132]),
                          reads=[pname], writes=["rcp"])
                        V(lambda e, psx=psx, hh=hh: e.tensor_scalar(
                            out=ATF[:, hh * P:(hh + 1) * P], in0=psx[:, 0:P],
                            scalar1=RCP[:, 0:1], scalar2=None, op0=A.mult),
                          reads=[pname, "rcp"], writes=["atf"])

                for g4 in range(4):
                    for b4 in range(4):
                        k = g4 * 4 + b4
                        transp(psB[:, b4 * P:(b4 + 1) * P], ATF[:, k * P:(k + 1) * P],
                               IDf[:], start=(b4 == 0), stop=(b4 == 3),
                               reads=["atf", "IDf"], writes=["psB"])
                    V(lambda e, g4=g4: e.tensor_copy(ATT[:, g4 * 512:(g4 + 1) * 512], psB[:]),
                      reads=["psB"], writes=["att"])
                    V(lambda e, g4=g4: e.tensor_tensor(
                        out=ATT2[:, g4 * 512:(g4 + 1) * 512], in0=psB[:],
                        in1=ATT[:, g4 * 512:(g4 + 1) * 512], op=A.subtract),
                      reads=["psB", "att"], writes=["att2"])
                dma("G", "gA",
                    sAtT[0, :, jj, :, :].rearrange("k p q -> p k q"),
                    ATT[:].rearrange("p (k q) -> p k q", k=KC),
                    ["att"], ["sAtT"])
                dma("G", "gA",
                    sAtT[1, :, jj, :, :].rearrange("k p q -> p k q"),
                    ATT2[:].rearrange("p (k q) -> p k q", k=KC),
                    ["att2"], ["sAtT"])

            dma("Y", "dB", Brow[0:1, 0:D], sBrows[2:3, 0:D], ["browh", "sBrows"], ["browh"])
            dma("Y", "dB", Brow[0:1, DQKV:DQKV + D], sBrows[3:4, 0:D],
                ["browl", "sBrows"], ["browl"])
            for dc4 in range(4):
                co = dc4 * 512
                for k in range(KC):
                    asl = k % 2
                    ah_, al_ = f"at{asl}h", f"at{asl}l"
                    dma("S", f"dA{asl}", AT[asl][:, 0:512].rearrange("p (t q) -> p t q", t=4),
                        sAtT[0, k, :, :, :].rearrange("t p q -> p t q"),
                        [ah_, "sAtT"], [ah_])
                    dma("Y", f"dAl{asl}", AT[asl][:, 512:1024].rearrange("p (t q) -> p t q", t=4),
                        sAtT[1, k, :, :, :].rearrange("t p q -> p t q"),
                        [al_, "sAtT"], [al_])
                    wsl = k % 2
                    wh_, wl_ = f"wo{wsl}h", f"wo{wsl}l"
                    dma("S", f"dO{wsl}", WO[wsl][:, 0:512],
                        sWoH[k * P:(k + 1) * P, co:co + 512], [wh_, "sWo"], [wh_])
                    dma("Y", f"dOl{wsl}", WO[wsl][:, 512:1024],
                        sWoL[k * P:(k + 1) * P, co:co + 512], [wl_, "sWo"], [wl_])
                    for q4 in range(4):
                        ah = AT[asl][:, q4 * P:(q4 + 1) * P]
                        al = AT[asl][:, 512 + q4 * P:512 + (q4 + 1) * P]
                        bn = f"ps6_{q4}"
                        out = ps6[:, q4 * 512:(q4 + 1) * 512]
                        mm(out, ah, WO[wsl][:, 0:512], (k == 0), False, [ah_, wh_], [bn])
                        mm(out, ah, WO[wsl][:, 512:1024], False, False, [ah_, wl_], [bn])
                        mm(out, al, WO[wsl][:, 0:512], False, False, [al_, wh_], [bn],
                           tick=(q4 == 3))
                for q4 in range(4):
                    bn = f"ps6_{q4}"
                    out = ps6[:, q4 * 512:(q4 + 1) * 512]
                    mm(out, OnesR[:], Brow[0:1, co:co + 512], False, False,
                       ["browh", "OnesR"], [bn])
                    mm(out, OnesR[:], Brow[0:1, DQKV + co:DQKV + co + 512], False, True,
                       ["browl", "OnesR"], [bn])
                    psl = EB[:, q4 * 512:(q4 + 1) * 512]
                    pn2 = f"eb{q4}"
                    tt2 = seq * TPS + sb4[q4]
                    xs = q4 % 2
                    xn2 = f"xt{xs}"
                    dma("Y", f"dX{xs}", XT[xs][:, 0:512],
                        dx[tt2 * P:(tt2 + 1) * P, co:co + 512], [xn2], [xn2])
                    V(lambda e, out=out, psl=psl, xs=xs: e.tensor_tensor(
                        out=psl, in0=out, in1=XT[xs][:, 0:512], op=A.add),
                      reads=[bn, xn2], writes=[pn2])
                    dma("G", "gP", sPP[tt2 * P:(tt2 + 1) * P, co:co + 512],
                        psl, [pn2], ["sPP"])

            dma("Y", "dB", Brow[0:1, 0:D], sBrows[4:5, 0:D], ["browh", "sBrows"], ["browh"])
            dma("Y", "dB", Brow[0:1, DQKV:DQKV + D], sBrows[5:6, 0:D],
                ["browl", "sBrows"], ["browl"])
            for dc4 in range(4):
                co = dc4 * 512
                for k in range(KC):
                    wsl = k % 2
                    wh_, wl_ = f"we{wsl}h", f"we{wsl}l"
                    dma("S", f"dE{wsl}", WE[wsl][:, 0:512],
                        sWeH[k * P:(k + 1) * P, co:co + 512], [wh_, "sWe"], [wh_])
                    dma("Y", f"dEl{wsl}", WE[wsl][:, 512:1024],
                        sWeL[k * P:(k + 1) * P, co:co + 512], [wl_, "sWe"], [wl_])
                    for jj in range(4):
                        xh = XnTH[:, jj * D + k * P:jj * D + (k + 1) * P]
                        xl = XnTL[:, jj * D + k * P:jj * D + (k + 1) * P]
                        bn = f"ps6_{jj}"
                        out = ps6[:, jj * 512:(jj + 1) * 512]
                        mm(out, xh, WE[wsl][:, 0:512], (k == 0), False, ["xnth", wh_], [bn])
                        mm(out, xh, WE[wsl][:, 512:1024], False, False, ["xnth", wl_], [bn])
                        mm(out, xl, WE[wsl][:, 0:512], False, False, ["xntl", wh_], [bn],
                           tick=(jj == 3))
                for jj in range(4):
                    bn = f"ps6_{jj}"
                    out = ps6[:, jj * 512:(jj + 1) * 512]
                    mm(out, OnesR[:], Brow[0:1, co:co + 512], False, False,
                       ["browh", "OnesR"], [bn])
                    mm(out, OnesR[:], Brow[0:1, DQKV + co:DQKV + co + 512], False, True,
                       ["browl", "OnesR"], [bn])
                    psl = EB[:, jj * 512:(jj + 1) * 512]
                    pn2 = f"eb{jj}"
                    V(lambda e, out=out, psl=psl: e.tensor_copy(psl, out),
                      reads=[bn], writes=[pn2])
                    tt2 = seq * TPS + sb4[jj]
                    dma("G", "gF", sFfn[tt2 * P:(tt2 + 1) * P, co:co + 512],
                        psl, [pn2], ["sFfn"])

    if KP < 4:
        with nc.Block() as block:
            sch.realize(block)
        es.close()
        return nc
    sch.emit("G", lambda e: e.collective_compute(
        "AllGather", A.bypass, replica_groups=[list(range(NCORES))],
        ins=[hSin.ap().opt()], outs=[hSout.ap().opt()]),
        reads=["dSin"], writes=["ccout"], sem="cc", amt=1, tick=False)
    dma("Y", "dR", SBLK[:], sSout.rearrange("r e2 f -> (r e2) f"), ["ccout"], ["sblk"])
    V(lambda e: e.memset(LO[:], 0.0), reads=[], writes=["lo"])
    V(lambda e: e.memset(HI[:], 1.0), reads=[], writes=["hi"])
    for _ in range(NBIS):
        V(lambda e: e.tensor_tensor(out=MID[:], in0=LO[:], in1=HI[:], op=A.add),
          reads=["lo", "hi"], writes=["mid"])
        V(lambda e: e.tensor_scalar(out=MID[:], in0=MID[:], scalar1=0.5,
                                    scalar2=None, op0=A.mult),
          reads=["mid"], writes=["mid"])
        for sl in range(4):
            V(lambda e, sl=sl: e.tensor_scalar(
                out=SCR[:, 0:1024], in0=SBLK[:, sl * 1024:(sl + 1) * 1024],
                scalar1=MID[:], scalar2=None, op0=A.is_gt, op1=A.add,
                accum_out=STAT[:, 24 + sl:25 + sl]),
              reads=["sblk", "mid"], writes=["scr", "stc"])
        V(lambda e: e.tensor_reduce(out=STAT[:, 28:29], in_=STAT[:, 24:28],
                                    axis=AXX, op=A.add),
          reads=["stc"], writes=["cnt"])
        mm(psA[:, 0:1], Bmat[:], STAT[:, 28:29], True, True, ["cnt", "Bmat"], ["psA"])
        V(lambda e: e.tensor_scalar(out=STAT[:, 29:30], in0=psA[:, 0:1],
                                    scalar1=float(TOPK), scalar2=None, op0=A.is_ge),
          reads=["psA"], writes=["ge"])
        V(lambda e: e.tensor_tensor(out=STAT[:, 30:31], in0=MID[:], in1=LO[:],
                                    op=A.subtract),
          reads=["mid", "lo"], writes=["t30"])
        V(lambda e: e.tensor_tensor(out=STAT[:, 30:31], in0=STAT[:, 30:31],
                                    in1=STAT[:, 29:30], op=A.mult),
          reads=["t30", "ge"], writes=["t30"])
        V(lambda e: e.tensor_tensor(out=LO[:], in0=LO[:], in1=STAT[:, 30:31], op=A.add),
          reads=["lo", "t30"], writes=["lo"])
        V(lambda e: e.tensor_tensor(out=STAT[:, 31:32], in0=MID[:], in1=HI[:],
                                    op=A.subtract),
          reads=["mid", "hi"], writes=["t31"])
        V(lambda e: e.tensor_tensor(out=STAT[:, 32:33], in0=STAT[:, 31:32],
                                    in1=STAT[:, 29:30], op=A.mult),
          reads=["t31", "ge"], writes=["t32"])
        V(lambda e: e.tensor_tensor(out=STAT[:, 31:32], in0=STAT[:, 31:32],
                                    in1=STAT[:, 32:33], op=A.subtract),
          reads=["t31", "t32"], writes=["t31"])
        V(lambda e: e.tensor_tensor(out=HI[:], in0=HI[:], in1=STAT[:, 31:32], op=A.add),
          reads=["hi", "t31"], writes=["hi"])
    V(lambda e: e.scalar_tensor_tensor(out=SBLK[:], in0=SBLK[:], scalar=LO[:],
                                       in1=SBLK[:], op0=A.is_gt, op1=A.mult),
      reads=["sblk", "lo"], writes=["sblk"])
    for sl in range(8):
        mm(psA[0:8, 0:512], Cmat[:, 0:8], SBLK[:, sl * 512:(sl + 1) * 512],
           True, True, ["sblk", "Cmat"], ["psA"])
        V(lambda e, sl=sl: e.tensor_copy(SCR[0:8, sl * 512:(sl + 1) * 512], psA[0:8, 0:512]),
          reads=["psA"], writes=["scr"])
    for s128 in range(NT):
        mm(psA[:, 0:1], SCR[0:8, s128 * P:(s128 + 1) * P], Oneh[:], True, True,
           ["scr", "Oneh"], ["psA"])
        V(lambda e, s128=s128: e.tensor_copy(WCOL[:, s128:s128 + 1], psA[:, 0:1]),
          reads=["psA"], writes=["wcol"])

    if KP < 5:
        with nc.Block() as block:
            sch.realize(block)
        es.close()
        return nc
    FFHs = [SCR[:, 0:1024], SCR[:, 1024:2048]]
    PPHs = [SCR[:, 2048:3072], SCR[:, 3072:4096]]
    first_touch = {"ffh0": True, "ffh1": True, "pph0": True, "pph1": True}
    for tt in range(NT):
        for h2 in range(2):
            co = h2 * 1024
            sl = (tt * 2 + h2) % 2
            xn_ = f"xt{sl}"
            fn_ = f"ffh{sl}"
            pn_ = f"pph{sl}"
            fw = [fn_] + (["scr"] if first_touch[fn_] else [])
            pw = [pn_] + (["scr"] if first_touch[pn_] else [])
            first_touch[fn_] = first_touch[pn_] = False
            dma("Y", f"dA{sl}", FFHs[sl], sFfn[tt * P:(tt + 1) * P, co:co + 1024],
                [fn_, "sFfn"], fw)
            dma("Y", f"dO{sl}", PPHs[sl], sPP[tt * P:(tt + 1) * P, co:co + 1024],
                [pn_, "sPP"], pw)
            V(lambda e, sl=sl, tt=tt: e.scalar_tensor_tensor(
                out=PPHs[sl], in0=FFHs[sl], scalar=WCOL[:, tt:tt + 1],
                in1=PPHs[sl], op0=A.mult, op1=A.add),
              reads=[fn_, "wcol", pn_], writes=[pn_])
            dma("G", "gO", dOut[tt * P:(tt + 1) * P, co:co + 1024], PPHs[sl],
                [pn_], ["outw"])

    sch.wait_all("G", [("gO", sch.cnt["gO"])])

    with nc.Block() as block:
        sch.realize(block)
    es.close()
    return nc


def _consts():
    import ml_dtypes
    bf = lambda a: np.asarray(a, dtype=ml_dtypes.bfloat16)
    i = np.arange(P)
    u = i[:, None]
    q = i[None, :]
    IDf = np.eye(P, dtype=np.float32)
    MLO = np.where(u >= q, 0.0, NEG).astype(np.float32)
    MLOS = np.where((u >= q) | (u < SINK), 0.0, NEG).astype(np.float32)
    MHI = np.where(u <= q, 0.0, NEG).astype(np.float32)
    MSINK = np.where(np.repeat(u < SINK, P, axis=1), 0.0, NEG).astype(np.float32)
    t = np.arange(T, dtype=np.float64)
    inv = 1.0 / (10000.0 ** (np.arange(DH // 2, dtype=np.float64) * 2.0 / DH))
    ang = t[:, None] * inv[None, :]
    cos = np.cos(ang)
    sin = np.sin(ang)
    ROPC = np.concatenate([cos, cos], axis=1).astype(np.float32)
    ROPS = np.concatenate([-sin, sin], axis=1).astype(np.float32)
    Bm = np.zeros((P, P), dtype=np.float32)
    for p in range(P):
        for qq in range(P):
            if p % 16 == qq % 16:
                Bm[p, qq] = 1.0
    Cm = np.zeros((P, 8), dtype=np.float32)
    for p in range(P):
        Cm[p, p // 16] = 1.0
    return dict(IDf=IDf, IDb=bf(IDf), MLO=bf(MLO), MLOS=bf(MLOS), MHI=bf(MHI),
                MSINK=bf(MSINK), ROPC=ROPC, ROPS=ROPS, Bmat=Bm, Cmat=Cm,
                OnesRow=bf(np.ones((1, P), dtype=np.float32)),
                OnesRowF=np.ones((1, P), dtype=np.float32))


def kernel(L0_attn_x, L0_attn_Wq, L0_attn_bq, L0_attn_Wk, L0_attn_bk,
           L0_attn_Wv, L0_attn_bv, L0_attn_Wo, L0_attn_bo,
           L0_ffn_router_w, L0_ffn_router_b, L0_ffn_expert_w, L0_ffn_expert_b):
    nc = build_program()
    consts = _consts()
    x = np.asarray(L0_attn_x, dtype=np.float32)
    in_maps = []
    for c in range(NCORES):
        oneh = np.zeros((8, 1), dtype=np.float32)
        oneh[c, 0] = 1.0
        m = {
            "x": np.ascontiguousarray(x[c * BPC:(c + 1) * BPC].reshape(BPC * T, D)),
            "Wq": np.asarray(L0_attn_Wq, dtype=np.float32),
            "bq": np.asarray(L0_attn_bq, dtype=np.float32),
            "Wk": np.asarray(L0_attn_Wk, dtype=np.float32),
            "bk": np.asarray(L0_attn_bk, dtype=np.float32),
            "Wv": np.asarray(L0_attn_Wv, dtype=np.float32),
            "bv": np.asarray(L0_attn_bv, dtype=np.float32),
            "Wo": np.asarray(L0_attn_Wo, dtype=np.float32),
            "bo": np.asarray(L0_attn_bo, dtype=np.float32),
            "Wr": np.asarray(L0_ffn_router_w, dtype=np.float32),
            "br": np.asarray(L0_ffn_router_b, dtype=np.float32),
            "We": np.asarray(L0_ffn_expert_w, dtype=np.float32),
            "be": np.asarray(L0_ffn_expert_b, dtype=np.float32),
            "Oneh": oneh,
        }
        m.update(consts)
        in_maps.append(m)
    res = run_bass_kernel_spmd(nc, in_maps, core_ids=list(range(NCORES)))
    outs = [np.asarray(r["out"]).reshape(BPC, T, D) for r in res.results]
    return np.concatenate(outs, axis=0)


if __name__ == "__main__":
    build_program()
    print("built ok")


# version 14
# speedup vs baseline: 1.0398x; 1.0093x over previous
import os
import numpy as np
from contextlib import ExitStack

import concourse.bass as bass
import concourse.mybir as mb
import bass_rust
from concourse.bass_utils import run_bass_kernel_spmd
from concourse.alu_op_type import AluOpType as A

F32 = mb.dt.float32
BF16 = mb.dt.bfloat16
AXX = mb.AxisListType.X
AF = mb.ActivationFunctionType

P = 128
D = 2048
T = 2048
NCORES = 8
BPC = 2
NT = BPC * T // P
TPS = T // P
KC = D // P
NH, NG, DH = 16, 4, 128
WIN, SINK = 512, 4
NE = 16
BS = 16 * T
TOPK = BS // NE
SCALE = DH ** -0.5
EPS = 1e-5
DQKV = D + 1024
NEG = -1.0e30
RING = 5
NBIS = 34

ENGS = ("T", "V", "S", "G", "Y")


class Sched:

    def __init__(self, nc, es):
        self.nc = nc
        self.es = es
        self.ops = {e: [] for e in ENGS}
        self.sems = {}
        self.cnt = {}
        for e in ENGS:
            self.mksem("tk" + e)
        self.bufs = {}
        self.sat = {e: {} for e in ENGS}
        self.pending = {e: [] for e in ENGS}

    def mksem(self, name):
        self.sems[name] = self.es.enter_context(self.nc.semaphore(name))
        self.cnt[name] = 0
        return name

    def emit(self, eng, fn, reads=(), writes=(), sem=None, amt=1, tick=True,
             extra_waits=()):
        need = {}
        for s, c in extra_waits:
            if need.get(s, 0) < c:
                need[s] = c

        def add(mark):
            if mark is not None and need.get(mark[0], 0) < mark[1]:
                need[mark[0]] = mark[1]

        for b in reads:
            st = self.bufs.get(b)
            if st:
                add(st.get("w"))
        for b in writes:
            st = self.bufs.get(b)
            if st:
                add(st.get("w"))
                for m in st.get("r", ()):
                    add(m)
        sat = self.sat[eng]
        waits = [(s, c) for s, c in need.items() if sat.get(s, 0) < c]
        for s, c in waits:
            sat[s] = c
        if sem is None and tick:
            sem = "tk" + eng
        if sem is not None:
            self.cnt[sem] += amt
            mark = (sem, self.cnt[sem])
        else:
            mark = None
        self.ops[eng].append((waits, fn, sem, amt))
        for b in writes:
            self.bufs[b] = {"w": mark, "r": []}
        if mark is None:
            self.pending[eng].extend(reads)
            self.pending[eng].extend(writes)
        else:
            for b in list(reads) + self.pending[eng]:
                st = self.bufs.setdefault(b, {"w": None, "r": []})
                st["r"].append(mark)
                if st.get("w") is None and b in self.pending[eng]:
                    pass
            for b in self.pending[eng]:
                st = self.bufs.setdefault(b, {"w": None, "r": []})
            self.pending[eng] = []
        return mark

    def wait_all(self, eng, marks):
        sat = self.sat[eng]
        ws = []
        for s, c in marks:
            if sat.get(s, 0) < c:
                ws.append((s, c))
                sat[s] = c
        if ws:
            self.ops[eng].append((ws, None, None, 0))

    def realize(self, block):
        def runner(ename):
            def run(eng):
                for waits, fn, sem, amt in self.ops[ename]:
                    for s, c in waits:
                        eng.wait_ge(self.sems[s], c)
                    if fn is not None:
                        inst = fn(eng)
                        if sem is not None:
                            inst.then_inc(self.sems[sem], amt)
            return run

        block.tensor(runner("T"))
        block.vector(runner("V"))
        block.scalar(runner("S"))
        block.gpsimd(runner("G"))
        block.sync(runner("Y"))


def build_program():
    nc = bass.Bass("TRN2", target_bir_lowering=False)
    es = ExitStack()

    def din(name, shape, dt):
        return nc.dram_tensor(name, shape, dt, kind="ExternalInput").ap()

    dx = din("x", [BPC * T, D], F32)
    dWq = din("Wq", [D, D], F32)
    dbq = din("bq", [D], F32)
    dWk = din("Wk", [D, 512], F32)
    dbk = din("bk", [512], F32)
    dWv = din("Wv", [D, 512], F32)
    dbv = din("bv", [512], F32)
    dWo = din("Wo", [D, D], F32)
    dbo = din("bo", [D], F32)
    dWr = din("Wr", [D, NE], F32)
    dbr = din("br", [NE], F32)
    dWe = din("We", [D, D], F32)
    dbe = din("be", [D], F32)
    dIDf = din("IDf", [P, P], F32)
    dIDb = din("IDb", [P, P], BF16)
    dMLO = din("MLO", [P, P], BF16)
    dMLOS = din("MLOS", [P, P], BF16)
    dMHI = din("MHI", [P, P], BF16)
    dMSINK = din("MSINK", [P, P], BF16)
    dROPC = din("ROPC", [T, DH], F32)
    dROPS = din("ROPS", [T, DH], F32)
    dBmat = din("Bmat", [P, P], F32)
    dCmat = din("Cmat", [P, 8], F32)
    dOneh = din("Oneh", [8, 1], F32)
    dOnes = din("OnesRow", [1, P], BF16)
    dOnesF = din("OnesRowF", [1, P], F32)
    dOut = nc.dram_tensor("out", [BPC * T, D], F32, kind="ExternalOutput").ap()

    if int(os.environ.get("KNOSCR", "0")):
        class _FakeAp:
            def __getitem__(self, *a):
                return None
            def rearrange(self, *a, **k):
                return None
            def ap(self):
                return None
            def opt(self):
                return None
        _f = _FakeAp()
        sWqkvH = sWqkvL = sWoH = sWoL = sWeH = sWeL = _f
        sBrows = sAtT = sFfn = sPP = sSin = sSout = _f
        hSin = hSout = _f
    else:
        sWqkvH = nc.dram_tensor("sWqkvH", [D, DQKV], BF16).ap()
        sWqkvL = nc.dram_tensor("sWqkvL", [D, DQKV], BF16).ap()
        sWoH = nc.dram_tensor("sWoH", [D, D], BF16).ap()
        sWoL = nc.dram_tensor("sWoL", [D, D], BF16).ap()
        sWeH = nc.dram_tensor("sWeH", [D, D], BF16).ap()
        sWeL = nc.dram_tensor("sWeL", [D, D], BF16).ap()
        sBrows = nc.dram_tensor("sBrows", [6, DQKV], BF16).ap()
        sAtT = nc.dram_tensor("sAtT", [2, KC, 4, P, P], BF16).ap()
        KDBG = int(os.environ.get("KDBG", "0"))
        kind = dict(kind="ExternalOutput") if KDBG else {}
        sFfn = nc.dram_tensor("sFfn", [NT * P, D], F32, **kind).ap()
        sPP = nc.dram_tensor("sPP", [NT * P, D], F32, **kind).ap()
        hSin = nc.dram_tensor("sSin", [NE, BPC * T], F32, **kind)
        hSout = nc.dram_tensor("sSout", [NCORES, NE, BPC * T], F32)
        sSin = hSin.ap()
        sSout = hSout.ap()

    if int(os.environ.get("KBARE", "0")):
        with nc.sbuf_tensor("bx", [64, 64], F32) as bx, \
             nc.semaphore("bd") as bd, nc.semaphore("bg") as bg, nc.Block() as block:
            @block.sync
            def _(sync):
                sync.dma_start(out=bx[:], in_=dx[0:64, 0:64]).then_inc(bd, 16)
            @block.gpsimd
            def _(g):
                g.wait_ge(bd, 16)
                g.dma_start(out=dOut[0:64, 0:64], in_=bx[:]).then_inc(bg, 16)
                g.wait_ge(bg, 16)
        return nc
    KSB = int(os.environ.get("KSB", "0"))
    def sb(name, shape, dt):
        if KSB:
            shape = [1, 4]
        return es.enter_context(nc.sbuf_tensor("b_" + name, shape, dt))

    IDf = sb("IDf", [P, P], F32)
    IDb = sb("IDb", [P, P], BF16)
    MLO = sb("MLO", [P, P], BF16)
    MLOS = sb("MLOS", [P, P], BF16)
    MHI = sb("MHI", [P, P], BF16)
    MSINK = sb("MSINK", [P, P], BF16)
    Bmat = sb("Bmat", [P, P], F32)
    Cmat = sb("Cmat", [P, 8], F32)
    Oneh = sb("Oneh", [8, 1], F32)
    OnesR = sb("OnesR", [1, P], BF16)
    OnesF = sb("OnesF", [1, P], F32)
    RopC = sb("RopC", [P, 4 * DH], F32)
    RopS = sb("RopS", [P, 4 * DH], F32)
    WrH = sb("WrH", [P, KC * NE], BF16)
    WrL = sb("WrL", [P, KC * NE], BF16)
    BrowR = sb("BrowR", [1, 2 * NE], BF16)
    Brow = sb("Brow", [1, 2 * DQKV], BF16)
    WB = [sb(f"WB{i}", [P, 2 * DQKV], BF16) for i in range(2)]
    WO = [sb(f"WOs{i}", [P, 1024], BF16) for i in range(2)]
    WE = [sb(f"WEs{i}", [P, 1024], BF16) for i in range(2)]
    AT = [sb(f"ATs{i}", [P, 1024], BF16) for i in range(2)]
    XT = [sb(f"XTs{i}", [P, D], F32) for i in range(2)]
    SCR = sb("SCR", [P, 2 * D], F32)
    XnTH = sb("XnTH", [P, 4 * D], BF16)
    XnTL = sb("XnTL", [P, 4 * D], BF16)
    QB = sb("QB", [P, 512], F32)
    QROT = sb("QROT", [P, 512], F32)
    QT = sb("QT", [P, NH * P], F32)
    KT = sb("KT", [P, NG * RING * P], F32)
    KTS = sb("KTS", [P, NG * P], F32)
    VA = sb("VA", [P, NG * RING * 132], F32)
    VAS = sb("VAS", [P, NG * 132], F32)
    EB = sb("EB", [P, 4 * 6 * P], F32)
    ATF = sb("ATF", [P, D], F32)
    ATT = sb("ATT", [P, D], BF16)
    ATT2 = sb("ATT2", [P, D], BF16)
    STAT = sb("STAT", [P, 40], F32)
    QN2 = sb("QN2", [P, NH], F32)
    KN2 = sb("KN2", [P, NG], F32)
    CB = sb("CB", [P, NH], F32)
    RCP = sb("RCP", [P, 2], F32)
    STILE = sb("STILE", [P, NE], F32)
    SBLK = sb("SBLK", [P, 4096], F32)
    LO = sb("LO", [P, 1], F32)
    HI = sb("HI", [P, 1], F32)
    MID = sb("MID", [P, 1], F32)
    WCOL = sb("WCOL", [P, NT], F32)

    ps6 = es.enter_context(nc.psum_tensor("ps6", [P, 3072], F32))
    psA = es.enter_context(nc.psum_tensor("psA", [P, 512], F32))
    psB = es.enter_context(nc.psum_tensor("psB", [P, 512], F32))

    sch = Sched(nc, es)
    for s in (["dC", "dW0", "dW1", "dWb0", "dWb1", "gWl0", "gWl1", "gWlb0", "gWlb1",
               "dX0", "dX1", "dR", "dB"]
              + [f"d{p}{i}" for p in ("A", "Al", "O", "Ol", "E", "El") for i in range(4)]
              + ["gS", "gA", "gF", "gP", "gO", "gW", "cc"]):
        sch.mksem(s)

    def dma(eng, sem, out, in_, reads, writes):
        pw = [(sem, sch.cnt[sem])] if sch.cnt[sem] > 0 else []
        sch.emit(eng, lambda e, o=out, i=in_: e.dma_start(out=o, in_=i),
                 reads=reads, writes=writes, sem=sem, amt=16, tick=False,
                 extra_waits=pw)

    def mm(out, lhsT, rhs, start, stop, reads, writes, tick=None):
        sch.emit("T", lambda e, o=out, l=lhsT, r=rhs, st=start, sp=stop:
                 e.matmul(o, l, r, start=st, stop=sp),
                 reads=reads, writes=writes, tick=(stop if tick is None else tick))

    def transp(out, in_, ident, start, stop, reads, writes):
        sch.emit("T", lambda e, o=out, i=in_, d=ident, st=start, sp=stop:
                 e.matmul(o, i, d, start=st, stop=sp, is_transpose=True),
                 reads=reads, writes=writes, tick=stop)

    def V(fn, reads=(), writes=()):
        sch.emit("V", fn, reads=reads, writes=writes)

    def S_(fn, reads=(), writes=()):
        sch.emit("S", fn, reads=reads, writes=writes)

    def apf(t, off, dims):
        base = t[:]
        return bass.AP(base.tensor, off, [list(base.ap[0])] + [list(d) for d in dims])

    KNC = int(os.environ.get("KNC", "99"))
    for buf, src, nm in [(IDf, dIDf, "IDf"), (IDb, dIDb, "IDb"), (MLO, dMLO, "MLO"),
                         (MLOS, dMLOS, "MLOS"), (MHI, dMHI, "MHI"), (MSINK, dMSINK, "MSINK"),
                         (Bmat, dBmat, "Bmat"), (Cmat, dCmat, "Cmat"), (Oneh, dOneh, "Oneh"),
                         (OnesR, dOnes, "OnesR"), (OnesF, dOnesF, "OnesF")][:KNC]:
        dma("Y", "dC", buf[:], src, [], [nm])

    KST = int(os.environ.get("KST", "9"))
    def split_weight(src_ap, dst_h, dst_l, ncols, dname):
        if KST < 1:
            return
        for k in range(KC):
            xb = XT[k % 2]
            xn_ = f"xt{k % 2}"
            dma("Y", f"dX{k % 2}", xb[:, 0:ncols], src_ap[k * P:(k + 1) * P, :], [xn_], [xn_])
            V(lambda e, xb=xb, n=ncols: e.tensor_copy(ATT[:, 0:n], xb[:, 0:n]),
              reads=[xn_], writes=["att"])
            V(lambda e, xb=xb, n=ncols: e.tensor_tensor(
                out=ATT2[:, 0:n], in0=xb[:, 0:n], in1=ATT[:, 0:n], op=A.subtract),
              reads=[xn_, "att"], writes=["att2"])
            dma("G", "gW", dst_h[k * P:(k + 1) * P, :], ATT[:, 0:ncols], ["att"], [dname])
            dma("G", "gW", dst_l[k * P:(k + 1) * P, :], ATT2[:, 0:ncols], ["att2"], [dname])

    split_weight(dWq, sWqkvH[:, 0:D], sWqkvL[:, 0:D], D, "sWqkv")
    split_weight(dWk, sWqkvH[:, D:D + 512], sWqkvL[:, D:D + 512], 512, "sWqkv")
    split_weight(dWv, sWqkvH[:, D + 512:DQKV], sWqkvL[:, D + 512:DQKV], 512, "sWqkv")
    split_weight(dWo, sWoH, sWoL, D, "sWo")
    split_weight(dWe, sWeH, sWeL, D, "sWe")

    for k in (range(KC) if KST >= 2 else []):
        dma("Y", "dR", SCR[:, 0:NE], dWr[k * P:(k + 1) * P, :], ["scr"], ["scr"])
        V(lambda e, k=k: e.tensor_copy(WrH[:, k * NE:(k + 1) * NE], SCR[:, 0:NE]),
          reads=["scr"], writes=["WrH"])
        V(lambda e, k=k: e.tensor_tensor(
            out=WrL[:, k * NE:(k + 1) * NE], in0=SCR[:, 0:NE],
            in1=WrH[:, k * NE:(k + 1) * NE], op=A.subtract),
          reads=["scr", "WrH"], writes=["WrL"])

    def split_bias(parts, row_h, row_l):
        if KST < 3:
            return
        for ap1, ncols, off in parts:
            dma("Y", "dB", SCR[0:1, off:off + ncols],
                ap1.rearrange("(a n) -> a n", a=1), ["scr"], ["scr"])
        V(lambda e: e.tensor_copy(Brow[0:1, 0:DQKV], SCR[0:1, 0:DQKV]),
          reads=["scr"], writes=["browh"])
        V(lambda e: e.tensor_tensor(
            out=Brow[0:1, DQKV:2 * DQKV], in0=SCR[0:1, 0:DQKV],
            in1=Brow[0:1, 0:DQKV], op=A.subtract),
          reads=["scr", "browh"], writes=["browl"])
        dma("G", "gW", row_h, Brow[0:1, 0:DQKV], ["browh"], ["sBrows"])
        dma("G", "gW", row_l, Brow[0:1, DQKV:2 * DQKV], ["browl"], ["sBrows"])

    split_bias([(dbq, D, 0), (dbk, 512, D), (dbv, 512, D + 512)], sBrows[0:1, :], sBrows[1:2, :])
    split_bias([(dbo, D, 0)], sBrows[2:3, :], sBrows[3:4, :])
    split_bias([(dbe, D, 0)], sBrows[4:5, :], sBrows[5:6, :])
    if KST >= 3:
        dma("Y", "dB", SCR[0:1, 0:NE], dbr.rearrange("(a n) -> a n", a=1), ["scr"], ["scr"])
        V(lambda e: e.tensor_copy(BrowR[0:1, 0:NE], SCR[0:1, 0:NE]),
          reads=["scr"], writes=["browr"])
        V(lambda e: e.tensor_tensor(
            out=BrowR[0:1, NE:2 * NE], in0=SCR[0:1, 0:NE],
            in1=BrowR[0:1, 0:NE], op=A.subtract),
          reads=["scr", "browr"], writes=["browr"])

    if KST < 4:
        with nc.Block() as block:
            sch.realize(block)
        es.close()
        return nc
    V(lambda e: e.memset(VA[:], 0.0), reads=[], writes=["va0", "va1", "va2", "va3", "va4", "vaone"])
    V(lambda e: e.memset(
        VA[:].rearrange("p (gs c) -> p gs c", c=132)[:, :, 131:132], 1.0),
      reads=["vaone"], writes=["vaone"])
    V(lambda e: e.memset(VAS[:], 0.0), reads=[], writes=["VAS"])
    V(lambda e: e.memset(
        VAS[:].rearrange("p (g c) -> p g c", c=132)[:, :, 131:132], 1.0),
      reads=["VAS"], writes=["VAS"])
    V(lambda e: e.memset(KTS[:], 0.0), reads=[], writes=["KTS"])

    KP = int(os.environ.get("KP", "9"))
    NSEQ = 0 if KP < 2 else (1 if KP == 2 else BPC)
    for seq in range(NSEQ):
        tok0 = seq * T
        for sbi in range(TPS // 4):
            sb4 = [sbi * 4 + j for j in range(4)]
            dma("Y", "dR", RopC[:].rearrange("p (j c) -> p j c", j=4),
                dROPC.rearrange("(j p) c -> p j c", p=P)[:, sbi * 4:(sbi + 1) * 4, :],
                ["ropC"], ["ropC"])
            dma("Y", "dR", RopS[:].rearrange("p (j c) -> p j c", j=4),
                dROPS.rearrange("(j p) c -> p j c", p=P)[:, sbi * 4:(sbi + 1) * 4, :],
                ["ropS"], ["ropS"])

            dma("Y", "dB", Brow[0:1, 0:DQKV], sBrows[0:1, :], ["browh", "sBrows"], ["browh"])
            dma("Y", "dB", Brow[0:1, DQKV:2 * DQKV], sBrows[1:2, :],
                ["browl", "sBrows"], ["browl"])
            for jj, tp in enumerate(sb4):
                tt = seq * TPS + tp
                xslot = tt % 2
                xn_ = f"xt{xslot}"
                dma("Y", f"dX{xslot}", XT[xslot][:],
                    dx[tok0 + tp * P: tok0 + (tp + 1) * P, :], [xn_], [xn_])
                st = STAT
                V(lambda e, xb=XT[xslot]: e.reduce_sum(out=st[:, 0:1], in_=xb[:], axis=AXX),
                  reads=[xn_], writes=["st"])
                S_(lambda e, xb=XT[xslot]: e.activation(
                    out=SCR[:, 0:D], in_=xb[:], func=AF.Square, scale=1.0,
                    accum_out=st[:, 1:2]),
                   reads=[xn_], writes=["scr", "st"])
                V(lambda e: e.tensor_scalar(out=st[:, 2:3], in0=st[:, 0:1],
                                            scalar1=1.0 / D, scalar2=None, op0=A.mult),
                  reads=["st"], writes=["st"])
                V(lambda e: e.tensor_tensor(out=st[:, 3:4], in0=st[:, 2:3],
                                            in1=st[:, 2:3], op=A.mult),
                  reads=["st"], writes=["st"])
                V(lambda e: e.tensor_scalar(out=st[:, 4:5], in0=st[:, 1:2],
                                            scalar1=1.0 / D, scalar2=None, op0=A.mult),
                  reads=["st"], writes=["st"])
                V(lambda e: e.tensor_tensor(out=st[:, 5:6], in0=st[:, 4:5],
                                            in1=st[:, 3:4], op=A.subtract),
                  reads=["st"], writes=["st"])
                V(lambda e: e.tensor_scalar(out=st[:, 6:7], in0=st[:, 5:6],
                                            scalar1=EPS, scalar2=None, op0=A.add),
                  reads=["st"], writes=["st"])
                V(lambda e: e.memset(st[:, 7:8], 1.0), reads=[], writes=["sty"])
                for _ in range(6):
                    V(lambda e: e.tensor_tensor(out=st[:, 8:9], in0=st[:, 7:8],
                                                in1=st[:, 7:8], op=A.mult),
                      reads=["sty"], writes=["st8"])
                    V(lambda e: e.tensor_tensor(out=st[:, 8:9], in0=st[:, 8:9],
                                                in1=st[:, 6:7], op=A.mult),
                      reads=["st8", "st"], writes=["st8"])
                    V(lambda e: e.tensor_scalar(out=st[:, 8:9], in0=st[:, 8:9],
                                                scalar1=-0.5, scalar2=1.5,
                                                op0=A.mult, op1=A.add),
                      reads=["st8"], writes=["st8"])
                    V(lambda e: e.tensor_tensor(out=st[:, 7:8], in0=st[:, 7:8],
                                                in1=st[:, 8:9], op=A.mult),
                      reads=["sty", "st8"], writes=["sty"])
                V(lambda e: e.tensor_tensor(out=st[:, 9:10], in0=st[:, 2:3],
                                            in1=st[:, 7:8], op=A.mult),
                  reads=["st", "sty"], writes=["st9"])
                V(lambda e: e.tensor_scalar(out=st[:, 9:10], in0=st[:, 9:10],
                                            scalar1=-1.0, scalar2=None, op0=A.mult),
                  reads=["st9"], writes=["st9"])
                V(lambda e, xb=XT[xslot]: e.tensor_scalar(
                    out=SCR[:, 0:D], in0=xb[:], scalar1=st[:, 7:8], scalar2=st[:, 9:10],
                    op0=A.mult, op1=A.add),
                  reads=[xn_, "sty", "st9"], writes=["scr"])
                for g4 in range(4):
                    for b4 in range(4):
                        k = g4 * 4 + b4
                        transp(psB[:, b4 * P:(b4 + 1) * P], SCR[:, k * P:(k + 1) * P],
                               IDf[:], start=(b4 == 0), stop=(b4 == 3),
                               reads=["scr", "IDf"], writes=["psB"])
                    V(lambda e, g4=g4, jj=jj: e.tensor_copy(
                        XnTH[:, jj * D + g4 * 512:jj * D + (g4 + 1) * 512], psB[:]),
                      reads=["psB"], writes=["xnth"])
                    V(lambda e, g4=g4, jj=jj: e.tensor_tensor(
                        out=XnTL[:, jj * D + g4 * 512:jj * D + (g4 + 1) * 512],
                        in0=psB[:],
                        in1=XnTH[:, jj * D + g4 * 512:jj * D + (g4 + 1) * 512],
                        op=A.subtract),
                      reads=["psB", "xnth"], writes=["xntl"])

                for k in range(KC):
                    ws = k % 2
                    whA, whB = f"wb{ws}hA", f"wb{ws}hB"
                    wlA, wlB = f"wb{ws}lA", f"wb{ws}lB"
                    dma("S", f"dW{ws}", WB[ws][:, 0:1536],
                        sWqkvH[k * P:(k + 1) * P, 0:1536], [whA, "sWqkv"], [whA])
                    dma("S", f"dWb{ws}", WB[ws][:, 1536:DQKV],
                        sWqkvH[k * P:(k + 1) * P, 1536:DQKV], [whB, "sWqkv"], [whB])
                    dma("G", f"gWl{ws}", WB[ws][:, DQKV:DQKV + 1536],
                        sWqkvL[k * P:(k + 1) * P, 0:1536], [wlA, "sWqkv"], [wlA])
                    dma("G", f"gWlb{ws}", WB[ws][:, DQKV + 1536:2 * DQKV],
                        sWqkvL[k * P:(k + 1) * P, 1536:DQKV], [wlB, "sWqkv"], [wlB])
                    xh = XnTH[:, jj * D + k * P:jj * D + (k + 1) * P]
                    xl = XnTL[:, jj * D + k * P:jj * D + (k + 1) * P]
                    st0 = (k == 0)
                    for dc in range(6):
                        wh_ = whA if dc < 3 else whB
                        wl_ = wlA if dc < 3 else wlB
                        wh = WB[ws][:, dc * 512:(dc + 1) * 512]
                        wl = WB[ws][:, DQKV + dc * 512:DQKV + (dc + 1) * 512]
                        pn = f"ps6_{dc}"
                        mm(ps6[:, dc * 512:(dc + 1) * 512], xh, wh, st0, False,
                           ["xnth", wh_], [pn])
                        mm(ps6[:, dc * 512:(dc + 1) * 512], xh, wl, False, False,
                           ["xnth", wl_], [pn])
                        mm(ps6[:, dc * 512:(dc + 1) * 512], xl, wh, False, False,
                           ["xntl", wh_], [pn], tick=(dc == 2 or dc == 5))
                    mm(psA[:, 0:NE], xh, WrH[:, k * NE:(k + 1) * NE], st0, False,
                       ["xnth", "WrH"], ["psA"])
                    mm(psA[:, 0:NE], xh, WrL[:, k * NE:(k + 1) * NE], False, False,
                       ["xnth", "WrL"], ["psA"])
                    mm(psA[:, 0:NE], xl, WrH[:, k * NE:(k + 1) * NE], False, False,
                       ["xntl", "WrH"], ["psA"], tick=True)
                for dc in range(6):
                    pn = f"ps6_{dc}"
                    mm(ps6[:, dc * 512:(dc + 1) * 512], OnesR[:],
                       Brow[0:1, dc * 512:(dc + 1) * 512], False, False,
                       ["browh", "OnesR"], [pn])
                    mm(ps6[:, dc * 512:(dc + 1) * 512], OnesR[:],
                       Brow[0:1, DQKV + dc * 512:DQKV + (dc + 1) * 512], False, True,
                       ["browl", "OnesR"], [pn])
                mm(psA[:, 0:NE], OnesR[:], BrowR[0:1, 0:NE], False, False,
                   ["browr", "OnesR"], ["psA"])
                mm(psA[:, 0:NE], OnesR[:], BrowR[0:1, NE:2 * NE], False, True,
                   ["browr", "OnesR"], ["psA"])

                V(lambda e: e.reduce_max(out=st[:, 10:11], in_=psA[:, 0:NE],
                                         axis=AXX, negate=True),
                  reads=["psA"], writes=["st10"])
                S_(lambda e: e.activation(out=STILE[:], in_=psA[:, 0:NE], func=AF.Exp,
                                          bias=st[:, 10:11], scale=1.0,
                                          accum_out=st[:, 11:12]),
                   reads=["psA", "st10"], writes=["stile", "st11"])
                V(lambda e: e.reciprocal(out=st[:, 12:13], in_=st[:, 11:12]),
                  reads=["st11"], writes=["st12"])
                V(lambda e: e.tensor_scalar(out=STILE[:], in0=STILE[:],
                                            scalar1=st[:, 12:13], scalar2=None, op0=A.mult),
                  reads=["stile", "st12"], writes=["stile"])
                transp(psA[0:NE, 0:P], STILE[:], IDf[:], True, True,
                       ["stile", "IDf"], ["psA"])
                V(lambda e: e.tensor_copy(SCR[0:NE, 0:P], psA[0:NE, 0:P]),
                  reads=["psA"], writes=["scr"])
                dma("G", "gS", sSin[:, tt * P:(tt + 1) * P], SCR[0:NE, 0:P],
                    ["scr"], ["dSin"])

                slot = tp % RING
                co_c = jj * DH
                for ch in range(4):
                    pq = ps6[:, ch * 512:(ch + 1) * 512]
                    pn = f"ps6_{ch}"
                    cb_ap = apf(RopC, co_c, [[0, 4], [1, DH]])
                    sb_lo = apf(RopS, co_c, [[0, 4], [1, 64]])
                    sb_hi = apf(RopS, co_c + 64, [[0, 4], [1, 64]])
                    V(lambda e, pq=pq, cb_ap=cb_ap: e.tensor_tensor(
                        out=QROT[:].rearrange("p (h c) -> p h c", h=4),
                        in0=pq.rearrange("p (h c) -> p h c", h=4),
                        in1=cb_ap, op=A.mult),
                      reads=[pn, "ropC"], writes=["qrot"])
                    V(lambda e, pq=pq, sb_lo=sb_lo: e.tensor_tensor(
                        out=QB[:].rearrange("p (h c) -> p h c", h=4)[:, :, 0:64],
                        in0=pq.rearrange("p (h c) -> p h c", h=4)[:, :, 64:128],
                        in1=sb_lo, op=A.mult),
                      reads=[pn, "ropS"], writes=["qb"])
                    V(lambda e, pq=pq, sb_hi=sb_hi: e.tensor_tensor(
                        out=QB[:].rearrange("p (h c) -> p h c", h=4)[:, :, 64:128],
                        in0=pq.rearrange("p (h c) -> p h c", h=4)[:, :, 0:64],
                        in1=sb_hi, op=A.mult),
                      reads=[pn, "ropS"], writes=["qb"])
                    V(lambda e: e.tensor_tensor(out=QROT[:], in0=QROT[:], in1=QB[:], op=A.add),
                      reads=["qrot", "qb"], writes=["qrot"])
                    V(lambda e: e.tensor_tensor(out=QB[:], in0=QROT[:], in1=QROT[:], op=A.mult),
                      reads=["qrot"], writes=["qb"])
                    V(lambda e, ch=ch: e.tensor_reduce(
                        out=QN2[:, ch * 4:(ch + 1) * 4],
                        in_=QB[:].rearrange("p (h c) -> p h c", h=4), axis=AXX, op=A.add),
                      reads=["qb"], writes=["qn2"])
                    for b4 in range(4):
                        transp(psA[:, b4 * P:(b4 + 1) * P], QROT[:, b4 * P:(b4 + 1) * P],
                               IDf[:], start=(b4 == 0), stop=(b4 == 3),
                               reads=["qrot", "IDf"], writes=["psA"])
                    V(lambda e, ch=ch: e.tensor_copy(QT[:, ch * 512:(ch + 1) * 512], psA[:]),
                      reads=["psA"], writes=["qt"])
                pk = ps6[:, 2048:2560]
                cb_ap = apf(RopC, co_c, [[0, 4], [1, DH]])
                sb_lo = apf(RopS, co_c, [[0, 4], [1, 64]])
                sb_hi = apf(RopS, co_c + 64, [[0, 4], [1, 64]])
                V(lambda e, cb_ap=cb_ap: e.tensor_tensor(
                    out=QROT[:].rearrange("p (h c) -> p h c", h=4),
                    in0=pk.rearrange("p (h c) -> p h c", h=4),
                    in1=cb_ap, op=A.mult),
                  reads=["ps6_4", "ropC"], writes=["qrot"])
                V(lambda e, sb_lo=sb_lo: e.tensor_tensor(
                    out=QB[:].rearrange("p (h c) -> p h c", h=4)[:, :, 0:64],
                    in0=pk.rearrange("p (h c) -> p h c", h=4)[:, :, 64:128],
                    in1=sb_lo, op=A.mult),
                  reads=["ps6_4", "ropS"], writes=["qb"])
                V(lambda e, sb_hi=sb_hi: e.tensor_tensor(
                    out=QB[:].rearrange("p (h c) -> p h c", h=4)[:, :, 64:128],
                    in0=pk.rearrange("p (h c) -> p h c", h=4)[:, :, 0:64],
                    in1=sb_hi, op=A.mult),
                  reads=["ps6_4", "ropS"], writes=["qb"])
                V(lambda e: e.tensor_tensor(out=QROT[:], in0=QROT[:], in1=QB[:], op=A.add),
                  reads=["qrot", "qb"], writes=["qrot"])
                V(lambda e: e.tensor_tensor(out=QB[:], in0=QROT[:], in1=QROT[:], op=A.mult),
                  reads=["qrot"], writes=["qb"])
                V(lambda e: e.tensor_reduce(
                    out=st[:, 16:20], in_=QB[:].rearrange("p (g c) -> p g c", g=4),
                    axis=AXX, op=A.add),
                  reads=["qb"], writes=["st16"])
                if tp == 0:
                    V(lambda e: e.tensor_copy(KN2[:], st[:, 16:20]),
                      reads=["st16"], writes=["kn2"])
                else:
                    V(lambda e: e.tensor_tensor(out=KN2[:], in0=KN2[:],
                                                in1=st[:, 16:20], op=A.max),
                      reads=["kn2", "st16"], writes=["kn2"])
                for g in range(4):
                    transp(psA[:, g * P:(g + 1) * P], QROT[:, g * P:(g + 1) * P],
                           IDf[:], start=(g == 0), stop=(g == 3),
                           reads=["qrot", "IDf"], writes=["psA"])
                V(lambda e, slot=slot: e.tensor_copy(
                    KT[:].rearrange("p (g s c) -> p g s c", g=NG, s=RING)[:, :, slot, :],
                    psA[:].rearrange("p (g c) -> p g c", g=4)),
                  reads=["psA"], writes=[f"kt{slot}"])
                if tp == 0:
                    V(lambda e: e.tensor_copy(
                        KTS[:].rearrange("p (g c) -> p g c", g=NG)[:, :, 0:4],
                        psA[:].rearrange("p (g c) -> p g c", g=4)[:, :, 0:4]),
                      reads=["psA", "KTS"], writes=["KTS"])
                V(lambda e, slot=slot: e.tensor_copy(
                    VA[:].rearrange("p (g s c) -> p g s c", g=NG, s=RING)[:, :, slot, 0:128],
                    ps6[:, 2560:3072].rearrange("p (g c) -> p g c", g=4)),
                  reads=["ps6_5", "vaone"], writes=[f"va{slot}"])
                if tp == 0:
                    V(lambda e: e.tensor_copy(
                        VAS[0:4, :].rearrange("p (g c) -> p g c", g=NG)[:, :, 0:128],
                        ps6[0:4, 2560:3072].rearrange("p (g c) -> p g c", g=4)),
                      reads=["ps6_5", "VAS"], writes=["VAS"])

                if tp >= 5:
                    slots = [("sink", None, MSINK, "MSINK")]
                    for c in range(tp - 4, tp + 1):
                        msk, mn = (MLO, "MLO") if c == tp - 4 else \
                                  ((MHI, "MHI") if c == tp else (None, None))
                        slots.append(("ring", c, msk, mn))
                else:
                    slots = []
                    for c in range(0, tp + 1):
                        if c == tp:
                            msk, mn = MHI, "MHI"
                        elif tp == 4 and c == 0:
                            msk, mn = MLOS, "MLOS"
                        else:
                            msk, mn = None, None
                        slots.append(("ring", c, msk, mn))
                nsl = len(slots)

                transp(psA[0:NH, 0:P], QN2[:], IDf[:], True, True, ["qn2", "IDf"], ["psA"])
                V(lambda e: e.reduce_max(out=st[0:NH, 20:21], in_=psA[0:NH, 0:P], axis=AXX),
                  reads=["psA"], writes=["st20"])
                transp(psA[0:NG, 0:P], KN2[:], IDf[:], True, True, ["kn2", "IDf"], ["psA"])
                V(lambda e: e.reduce_max(out=st[0:NG, 21:22], in_=psA[0:NG, 0:P], axis=AXX),
                  reads=["psA"], writes=["st21"])
                transp(psA[0:1, 0:NH], st[0:NH, 20:21], IDf[0:NH, 0:NH], True, True,
                       ["st20", "IDf"], ["psA"])
                V(lambda e: e.tensor_copy(SCR[0:1, 0:NH], psA[0:1, 0:NH]),
                  reads=["psA"], writes=["scr"])
                transp(psA[0:1, 0:NG], st[0:NG, 21:22], IDf[0:NG, 0:NG], True, True,
                       ["st21", "IDf"], ["psA"])
                V(lambda e: e.tensor_copy(SCR[0:1, 16:16 + NG], psA[0:1, 0:NG]),
                  reads=["psA", "scr"], writes=["scr"])
                V(lambda e: e.tensor_reduce(
                    out=SCR[0:1, 24:28], in_=SCR[0:1, 0:16].rearrange("p (g h) -> p g h", g=4),
                    axis=AXX, op=A.max),
                  reads=["scr"], writes=["scr"])
                V(lambda e: e.tensor_tensor(
                    out=SCR[0:1, 32:36], in0=SCR[0:1, 24:28],
                    in1=SCR[0:1, 16:20], op=A.add),
                  reads=["scr"], writes=["scr"])
                V(lambda e: e.tensor_scalar(out=SCR[0:1, 32:36], in0=SCR[0:1, 32:36],
                                            scalar1=-0.5 * SCALE, scalar2=None, op0=A.mult),
                  reads=["scr"], writes=["scr"])
                mm(psA[:, 0:NG], OnesF[:], SCR[0:1, 32:36], True, True,
                   ["scr", "OnesF"], ["psA"])
                V(lambda e: e.tensor_copy(CB[:, 0:NG], psA[:, 0:NG]), reads=["psA"], writes=["cb"])

                for g in range(NG):
                    order = ([4, 5, 0, 1, 2, 3] if nsl == 6 else list(range(nsl)))
                    for si in order:
                        kind, c, msk, mn = slots[si]
                        if kind == "sink":
                            lhs = KTS[:].rearrange("p (g2 c2) -> p g2 c2", g2=NG)[:, g, :]
                            lrd = ["KTS"]
                        else:
                            lhs = KT[:].rearrange("p (g2 s c2) -> p g2 s c2",
                                                  g2=NG, s=RING)[:, g, c % RING, :]
                            lrd = [f"kt{c % RING}"]
                        pn = f"ps6_{si}"
                        for h in range(4):
                            hh = g * 4 + h
                            mm(ps6[:, si * 512 + h * P: si * 512 + (h + 1) * P],
                               lhs, QT[:, hh * P:(hh + 1) * P],
                               start=(h == 0), stop=(msk is None and h == 3),
                               reads=lrd + ["qt"], writes=[pn])
                        if msk is not None:
                            for h in range(4):
                                mm(ps6[:, si * 512 + h * P: si * 512 + (h + 1) * P],
                                   IDb[:], msk[:], start=False, stop=(h == 3),
                                   reads=["IDb", mn], writes=[pn])
                    S_(lambda e, g=g, nsl=nsl: e.activation(
                        out=EB[:, 0:nsl * 512],
                        in_=ps6[:, 0:nsl * 512],
                        func=AF.Exp, bias=CB[:, g:g + 1], scale=SCALE),
                       reads=[f"ps6_{si}" for si in range(nsl)] + ["cb"],
                       writes=["eb0", "eb1", "eb2", "eb3"])
                    for h in range(4):
                        hh = g * 4 + h
                        psx = (psA if h % 2 == 0 else psB)[:, 0:132]
                        pname = "psA" if h % 2 == 0 else "psB"
                        for si, (kind, c, msk, mn) in enumerate(slots):
                            if kind == "sink":
                                rhs = VAS[:].rearrange("p (g2 c2) -> p g2 c2",
                                                       g2=NG)[:, g, :]
                                rrd = ["VAS"]
                            else:
                                rhs = VA[:].rearrange("p (g2 s c2) -> p g2 s c2",
                                                      g2=NG, s=RING)[:, g, c % RING, :]
                                rrd = [f"va{c % RING}"]
                            mm(psx,
                               EB[:, si * 512 + h * P: si * 512 + (h + 1) * P],
                               rhs, start=(si == 0), stop=(si == nsl - 1),
                               reads=["eb0", "eb1", "eb2", "eb3"] + rrd,
                               writes=[pname])
                        V(lambda e, psx=psx: e.reciprocal(out=RCP[:, 0:1],
                                                          in_=psx[:, 131:132]),
                          reads=[pname], writes=["rcp"])
                        V(lambda e, psx=psx, hh=hh: e.tensor_scalar(
                            out=ATF[:, hh * P:(hh + 1) * P], in0=psx[:, 0:P],
                            scalar1=RCP[:, 0:1], scalar2=None, op0=A.mult),
                          reads=[pname, "rcp"], writes=["atf"])

                for g4 in range(4):
                    for b4 in range(4):
                        k = g4 * 4 + b4
                        transp(psB[:, b4 * P:(b4 + 1) * P], ATF[:, k * P:(k + 1) * P],
                               IDf[:], start=(b4 == 0), stop=(b4 == 3),
                               reads=["atf", "IDf"], writes=["psB"])
                    V(lambda e, g4=g4: e.tensor_copy(ATT[:, g4 * 512:(g4 + 1) * 512], psB[:]),
                      reads=["psB"], writes=["att"])
                    V(lambda e, g4=g4: e.tensor_tensor(
                        out=ATT2[:, g4 * 512:(g4 + 1) * 512], in0=psB[:],
                        in1=ATT[:, g4 * 512:(g4 + 1) * 512], op=A.subtract),
                      reads=["psB", "att"], writes=["att2"])
                dma("G", "gA",
                    sAtT[0, :, jj, :, :].rearrange("k p q -> p k q"),
                    ATT[:].rearrange("p (k q) -> p k q", k=KC),
                    ["att"], ["sAtT"])
                dma("G", "gA",
                    sAtT[1, :, jj, :, :].rearrange("k p q -> p k q"),
                    ATT2[:].rearrange("p (k q) -> p k q", k=KC),
                    ["att2"], ["sAtT"])

            dma("Y", "dB", Brow[0:1, 0:D], sBrows[2:3, 0:D], ["browh", "sBrows"], ["browh"])
            dma("Y", "dB", Brow[0:1, DQKV:DQKV + D], sBrows[3:4, 0:D],
                ["browl", "sBrows"], ["browl"])
            for dc4 in range(4):
                co = dc4 * 512
                for k in range(KC):
                    asl = k % 2
                    ah_, al_ = f"at{asl}h", f"at{asl}l"
                    dma("S", f"dA{asl}", AT[asl][:, 0:512].rearrange("p (t q) -> p t q", t=4),
                        sAtT[0, k, :, :, :].rearrange("t p q -> p t q"),
                        [ah_, "sAtT"], [ah_])
                    dma("Y", f"dAl{asl}", AT[asl][:, 512:1024].rearrange("p (t q) -> p t q", t=4),
                        sAtT[1, k, :, :, :].rearrange("t p q -> p t q"),
                        [al_, "sAtT"], [al_])
                    wsl = k % 2
                    wh_, wl_ = f"wo{wsl}h", f"wo{wsl}l"
                    dma("S", f"dO{wsl}", WO[wsl][:, 0:512],
                        sWoH[k * P:(k + 1) * P, co:co + 512], [wh_, "sWo"], [wh_])
                    dma("Y", f"dOl{wsl}", WO[wsl][:, 512:1024],
                        sWoL[k * P:(k + 1) * P, co:co + 512], [wl_, "sWo"], [wl_])
                    for q4 in range(4):
                        ah = AT[asl][:, q4 * P:(q4 + 1) * P]
                        al = AT[asl][:, 512 + q4 * P:512 + (q4 + 1) * P]
                        bn = f"ps6_{q4}"
                        out = ps6[:, q4 * 512:(q4 + 1) * 512]
                        mm(out, ah, WO[wsl][:, 0:512], (k == 0), False, [ah_, wh_], [bn])
                        mm(out, ah, WO[wsl][:, 512:1024], False, False, [ah_, wl_], [bn])
                        mm(out, al, WO[wsl][:, 0:512], False, False, [al_, wh_], [bn],
                           tick=(q4 == 3))
                for q4 in range(4):
                    bn = f"ps6_{q4}"
                    out = ps6[:, q4 * 512:(q4 + 1) * 512]
                    mm(out, OnesR[:], Brow[0:1, co:co + 512], False, False,
                       ["browh", "OnesR"], [bn])
                    mm(out, OnesR[:], Brow[0:1, DQKV + co:DQKV + co + 512], False, True,
                       ["browl", "OnesR"], [bn])
                    psl = EB[:, q4 * 512:(q4 + 1) * 512]
                    pn2 = f"eb{q4}"
                    tt2 = seq * TPS + sb4[q4]
                    xs = q4 % 2
                    xn2 = f"xt{xs}"
                    dma("Y", f"dX{xs}", XT[xs][:, 0:512],
                        dx[tt2 * P:(tt2 + 1) * P, co:co + 512], [xn2], [xn2])
                    V(lambda e, out=out, psl=psl, xs=xs: e.tensor_tensor(
                        out=psl, in0=out, in1=XT[xs][:, 0:512], op=A.add),
                      reads=[bn, xn2], writes=[pn2])
                    dma("G", "gP", sPP[tt2 * P:(tt2 + 1) * P, co:co + 512],
                        psl, [pn2], ["sPP"])

            dma("Y", "dB", Brow[0:1, 0:D], sBrows[4:5, 0:D], ["browh", "sBrows"], ["browh"])
            dma("Y", "dB", Brow[0:1, DQKV:DQKV + D], sBrows[5:6, 0:D],
                ["browl", "sBrows"], ["browl"])
            for dc4 in range(4):
                co = dc4 * 512
                for k in range(KC):
                    wsl = k % 2
                    wh_, wl_ = f"we{wsl}h", f"we{wsl}l"
                    dma("S", f"dE{wsl}", WE[wsl][:, 0:512],
                        sWeH[k * P:(k + 1) * P, co:co + 512], [wh_, "sWe"], [wh_])
                    dma("Y", f"dEl{wsl}", WE[wsl][:, 512:1024],
                        sWeL[k * P:(k + 1) * P, co:co + 512], [wl_, "sWe"], [wl_])
                    for jj in range(4):
                        xh = XnTH[:, jj * D + k * P:jj * D + (k + 1) * P]
                        xl = XnTL[:, jj * D + k * P:jj * D + (k + 1) * P]
                        bn = f"ps6_{jj}"
                        out = ps6[:, jj * 512:(jj + 1) * 512]
                        mm(out, xh, WE[wsl][:, 0:512], (k == 0), False, ["xnth", wh_], [bn])
                        mm(out, xh, WE[wsl][:, 512:1024], False, False, ["xnth", wl_], [bn])
                        mm(out, xl, WE[wsl][:, 0:512], False, False, ["xntl", wh_], [bn],
                           tick=(jj == 3))
                for jj in range(4):
                    bn = f"ps6_{jj}"
                    out = ps6[:, jj * 512:(jj + 1) * 512]
                    mm(out, OnesR[:], Brow[0:1, co:co + 512], False, False,
                       ["browh", "OnesR"], [bn])
                    mm(out, OnesR[:], Brow[0:1, DQKV + co:DQKV + co + 512], False, True,
                       ["browl", "OnesR"], [bn])
                    psl = EB[:, jj * 512:(jj + 1) * 512]
                    pn2 = f"eb{jj}"
                    V(lambda e, out=out, psl=psl: e.tensor_copy(psl, out),
                      reads=[bn], writes=[pn2])
                    tt2 = seq * TPS + sb4[jj]
                    dma("G", "gF", sFfn[tt2 * P:(tt2 + 1) * P, co:co + 512],
                        psl, [pn2], ["sFfn"])

    if KP < 4:
        with nc.Block() as block:
            sch.realize(block)
        es.close()
        return nc
    sch.emit("G", lambda e: e.collective_compute(
        "AllGather", A.bypass, replica_groups=[list(range(NCORES))],
        ins=[hSin.ap().opt()], outs=[hSout.ap().opt()]),
        reads=["dSin"], writes=["ccout"], sem="cc", amt=1, tick=False)
    dma("Y", "dR", SBLK[:], sSout.rearrange("r e2 f -> (r e2) f"), ["ccout"], ["sblk"])
    V(lambda e: e.memset(LO[:], 0.0), reads=[], writes=["lo"])
    V(lambda e: e.memset(HI[:], 1.0), reads=[], writes=["hi"])
    for _ in range(NBIS):
        V(lambda e: e.tensor_tensor(out=MID[:], in0=LO[:], in1=HI[:], op=A.add),
          reads=["lo", "hi"], writes=["mid"])
        V(lambda e: e.tensor_scalar(out=MID[:], in0=MID[:], scalar1=0.5,
                                    scalar2=None, op0=A.mult),
          reads=["mid"], writes=["mid"])
        for sl in range(4):
            V(lambda e, sl=sl: e.tensor_scalar(
                out=SCR[:, 0:1024], in0=SBLK[:, sl * 1024:(sl + 1) * 1024],
                scalar1=MID[:], scalar2=None, op0=A.is_gt, op1=A.add,
                accum_out=STAT[:, 24 + sl:25 + sl]),
              reads=["sblk", "mid"], writes=["scr", "stc"])
        V(lambda e: e.tensor_reduce(out=STAT[:, 28:29], in_=STAT[:, 24:28],
                                    axis=AXX, op=A.add),
          reads=["stc"], writes=["cnt"])
        mm(psA[:, 0:1], Bmat[:], STAT[:, 28:29], True, True, ["cnt", "Bmat"], ["psA"])
        V(lambda e: e.tensor_scalar(out=STAT[:, 29:30], in0=psA[:, 0:1],
                                    scalar1=float(TOPK), scalar2=None, op0=A.is_ge),
          reads=["psA"], writes=["ge"])
        V(lambda e: e.tensor_tensor(out=STAT[:, 30:31], in0=MID[:], in1=LO[:],
                                    op=A.subtract),
          reads=["mid", "lo"], writes=["t30"])
        V(lambda e: e.tensor_tensor(out=STAT[:, 30:31], in0=STAT[:, 30:31],
                                    in1=STAT[:, 29:30], op=A.mult),
          reads=["t30", "ge"], writes=["t30"])
        V(lambda e: e.tensor_tensor(out=LO[:], in0=LO[:], in1=STAT[:, 30:31], op=A.add),
          reads=["lo", "t30"], writes=["lo"])
        V(lambda e: e.tensor_tensor(out=STAT[:, 31:32], in0=MID[:], in1=HI[:],
                                    op=A.subtract),
          reads=["mid", "hi"], writes=["t31"])
        V(lambda e: e.tensor_tensor(out=STAT[:, 32:33], in0=STAT[:, 31:32],
                                    in1=STAT[:, 29:30], op=A.mult),
          reads=["t31", "ge"], writes=["t32"])
        V(lambda e: e.tensor_tensor(out=STAT[:, 31:32], in0=STAT[:, 31:32],
                                    in1=STAT[:, 32:33], op=A.subtract),
          reads=["t31", "t32"], writes=["t31"])
        V(lambda e: e.tensor_tensor(out=HI[:], in0=HI[:], in1=STAT[:, 31:32], op=A.add),
          reads=["hi", "t31"], writes=["hi"])
    V(lambda e: e.scalar_tensor_tensor(out=SBLK[:], in0=SBLK[:], scalar=LO[:],
                                       in1=SBLK[:], op0=A.is_gt, op1=A.mult),
      reads=["sblk", "lo"], writes=["sblk"])
    for sl in range(8):
        mm(psA[0:8, 0:512], Cmat[:, 0:8], SBLK[:, sl * 512:(sl + 1) * 512],
           True, True, ["sblk", "Cmat"], ["psA"])
        V(lambda e, sl=sl: e.tensor_copy(SCR[0:8, sl * 512:(sl + 1) * 512], psA[0:8, 0:512]),
          reads=["psA"], writes=["scr"])
    for s128 in range(NT):
        mm(psA[:, 0:1], SCR[0:8, s128 * P:(s128 + 1) * P], Oneh[:], True, True,
           ["scr", "Oneh"], ["psA"])
        V(lambda e, s128=s128: e.tensor_copy(WCOL[:, s128:s128 + 1], psA[:, 0:1]),
          reads=["psA"], writes=["wcol"])

    if KP < 5:
        with nc.Block() as block:
            sch.realize(block)
        es.close()
        return nc
    FFHs = [SCR[:, 0:1024], SCR[:, 1024:2048]]
    PPHs = [SCR[:, 2048:3072], SCR[:, 3072:4096]]
    first_touch = {"ffh0": True, "ffh1": True, "pph0": True, "pph1": True}
    for tt in range(NT):
        for h2 in range(2):
            co = h2 * 1024
            sl = (tt * 2 + h2) % 2
            xn_ = f"xt{sl}"
            fn_ = f"ffh{sl}"
            pn_ = f"pph{sl}"
            fw = [fn_] + (["scr"] if first_touch[fn_] else [])
            pw = [pn_] + (["scr"] if first_touch[pn_] else [])
            first_touch[fn_] = first_touch[pn_] = False
            dma("Y", f"dA{sl}", FFHs[sl], sFfn[tt * P:(tt + 1) * P, co:co + 1024],
                [fn_, "sFfn"], fw)
            dma("Y", f"dO{sl}", PPHs[sl], sPP[tt * P:(tt + 1) * P, co:co + 1024],
                [pn_, "sPP"], pw)
            V(lambda e, sl=sl, tt=tt: e.scalar_tensor_tensor(
                out=PPHs[sl], in0=FFHs[sl], scalar=WCOL[:, tt:tt + 1],
                in1=PPHs[sl], op0=A.mult, op1=A.add),
              reads=[fn_, "wcol", pn_], writes=[pn_])
            dma("G", "gO", dOut[tt * P:(tt + 1) * P, co:co + 1024], PPHs[sl],
                [pn_], ["outw"])

    sch.wait_all("G", [("gO", sch.cnt["gO"])])

    with nc.Block() as block:
        sch.realize(block)
    es.close()
    return nc


def _consts():
    import ml_dtypes
    bf = lambda a: np.asarray(a, dtype=ml_dtypes.bfloat16)
    i = np.arange(P)
    u = i[:, None]
    q = i[None, :]
    IDf = np.eye(P, dtype=np.float32)
    MLO = np.where(u >= q, 0.0, NEG).astype(np.float32)
    MLOS = np.where((u >= q) | (u < SINK), 0.0, NEG).astype(np.float32)
    MHI = np.where(u <= q, 0.0, NEG).astype(np.float32)
    MSINK = np.where(np.repeat(u < SINK, P, axis=1), 0.0, NEG).astype(np.float32)
    t = np.arange(T, dtype=np.float64)
    inv = 1.0 / (10000.0 ** (np.arange(DH // 2, dtype=np.float64) * 2.0 / DH))
    ang = t[:, None] * inv[None, :]
    cos = np.cos(ang)
    sin = np.sin(ang)
    ROPC = np.concatenate([cos, cos], axis=1).astype(np.float32)
    ROPS = np.concatenate([-sin, sin], axis=1).astype(np.float32)
    Bm = np.zeros((P, P), dtype=np.float32)
    for p in range(P):
        for qq in range(P):
            if p % 16 == qq % 16:
                Bm[p, qq] = 1.0
    Cm = np.zeros((P, 8), dtype=np.float32)
    for p in range(P):
        Cm[p, p // 16] = 1.0
    return dict(IDf=IDf, IDb=bf(IDf), MLO=bf(MLO), MLOS=bf(MLOS), MHI=bf(MHI),
                MSINK=bf(MSINK), ROPC=ROPC, ROPS=ROPS, Bmat=Bm, Cmat=Cm,
                OnesRow=bf(np.ones((1, P), dtype=np.float32)),
                OnesRowF=np.ones((1, P), dtype=np.float32))


def kernel(L0_attn_x, L0_attn_Wq, L0_attn_bq, L0_attn_Wk, L0_attn_bk,
           L0_attn_Wv, L0_attn_bv, L0_attn_Wo, L0_attn_bo,
           L0_ffn_router_w, L0_ffn_router_b, L0_ffn_expert_w, L0_ffn_expert_b):
    nc = build_program()
    consts = _consts()
    x = np.asarray(L0_attn_x, dtype=np.float32)
    in_maps = []
    for c in range(NCORES):
        oneh = np.zeros((8, 1), dtype=np.float32)
        oneh[c, 0] = 1.0
        m = {
            "x": np.ascontiguousarray(x[c * BPC:(c + 1) * BPC].reshape(BPC * T, D)),
            "Wq": np.asarray(L0_attn_Wq, dtype=np.float32),
            "bq": np.asarray(L0_attn_bq, dtype=np.float32),
            "Wk": np.asarray(L0_attn_Wk, dtype=np.float32),
            "bk": np.asarray(L0_attn_bk, dtype=np.float32),
            "Wv": np.asarray(L0_attn_Wv, dtype=np.float32),
            "bv": np.asarray(L0_attn_bv, dtype=np.float32),
            "Wo": np.asarray(L0_attn_Wo, dtype=np.float32),
            "bo": np.asarray(L0_attn_bo, dtype=np.float32),
            "Wr": np.asarray(L0_ffn_router_w, dtype=np.float32),
            "br": np.asarray(L0_ffn_router_b, dtype=np.float32),
            "We": np.asarray(L0_ffn_expert_w, dtype=np.float32),
            "be": np.asarray(L0_ffn_expert_b, dtype=np.float32),
            "Oneh": oneh,
        }
        m.update(consts)
        in_maps.append(m)
    res = run_bass_kernel_spmd(nc, in_maps, core_ids=list(range(NCORES)))
    outs = [np.asarray(r["out"]).reshape(BPC, T, D) for r in res.results]
    return np.concatenate(outs, axis=0)


if __name__ == "__main__":
    build_program()
    print("built ok")
